# version 07
# baseline (speedup 1.0000x reference)
import numpy as np
import ml_dtypes
from contextlib import ExitStack
import concourse.bass as bass
import concourse.mybir as mybir
import concourse.bass_utils as _bass_utils
from concourse.bass_utils import run_bass_kernel_spmd

if not getattr(_bass_utils, "_ldw_opt_patched", False):
    _orig_run_command = _bass_utils.run_command

    def _run_command_ldw(cmd, *a, **kw):
        if isinstance(cmd, list):
            cmd = ["--enable-ldw-opt=true" if c == "--enable-ldw-opt=false"
                   else c for c in cmd]
        return _orig_run_command(cmd, *a, **kw)

    _bass_utils.run_command = _run_command_ldw
    _bass_utils._ldw_opt_patched = True

F32 = mybir.dt.float32
BF16 = mybir.dt.bfloat16
AF = mybir.ActivationFunctionType
ALU = mybir.AluOpType
AXX = mybir.AxisListType.X

EXP_SCALE = float(np.float32(1.0 / np.sqrt(128.0)))
EXP_SHIFT = -10.0


class CFG:
    def __init__(self, R=8, T=1024, D=2048, F=1024, E=8):
        self.R, self.T, self.D, self.F, self.E = R, T, D, F, E
        self.M = D // 128
        self.H = D // 128
        self.CF = F // 128
        self.TS = T // 128
        self.NE = E + 1
        self.S = R * T
        assert T % 128 == 0 and D % 128 == 0 and F % 128 == 0

    def nblk(self):
        out, c = [], 0
        while c < self.T:
            n = min(512, self.T - c)
            out.append((c, n))
            c += n
        return out


class Builder:
    ENGINES = ("sync", "gpsimd", "tensor", "vector", "scalar")

    def __init__(self, cfg: CFG):
        self.cfg = cfg
        self.nc = bass.Bass("TRN2", target_bir_lowering=False)
        self.steps = []
        self.sems = {}
        self.planned = {}
        for s in ("s_sync", "s_gp", "s_gpe", "s_pe", "s_act", "s_dve", "s_cc",
                  "s_pre", "s_wr0", "s_wr1", "s_wr2", "s_wd0", "s_wd1", "s_wd2",
                  "s_kr0", "s_kr1", "s_vr0", "s_vr1", "s_kv0", "s_kv1", "s_kv2",
                  "s_kv3", "s_bcA", "s_bcB",
                  "s_rec0", "s_rec1", "s_wbc0", "s_wbc1", "s_wdram", "s_out", "s_cc2",
                  "s_pr0", "s_pr1", "s_pr2", "s_pr3",
                  "s_pre2"):
            self.sems[s] = self.nc.alloc_semaphore(s)
            self.planned[s] = 0
        self.rdy = {}
        self.frees = {}
        self.pe_pending = set()
        self.last_tensor_step = None

    def emit(self, eng, fn, waits=(), inc=None):
        assert eng in self.ENGINES
        w = [x for x in waits if x is not None]
        mi = None
        if inc is not None:
            self.planned[inc[0]] += inc[1]
            mi = (inc[0], self.planned[inc[0]])
        self.steps.append((eng, fn, tuple(w), inc))
        if eng == "tensor":
            self.last_tensor_step = len(self.steps) - 1
            if mi is not None and self.pe_pending:
                for key in self.pe_pending:
                    self.consumed(key, mi)
                self.pe_pending.clear()
        return mi

    def dma(self, eng, out_ap, in_ap, waits=(), sem=None):
        assert sem is not None
        return self.emit(eng, lambda e, o=out_ap, i=in_ap: e.dma_start(out=o, in_=i),
                         waits=waits, inc=(sem, 16))

    def mm(self, out_ap, lhsT, rhs, start, stop, waits=(), inc=False):
        return self.emit(
            "tensor",
            lambda t, o=out_ap, l=lhsT, r=rhs, a=start, b=stop:
                t.matmul(o, l, r, start=a, stop=b),
            waits=waits, inc=("s_pe", 1) if inc else None)

    def act(self, fn, waits=()):
        return self.emit("scalar", fn, waits=waits, inc=("s_act", 1))

    def dve(self, fn, waits=()):
        return self.emit("vector", fn, waits=waits, inc=("s_dve", 1))

    def gpe(self, fn, waits=()):
        return self.emit("gpsimd", fn, waits=waits, inc=("s_gpe", 1))

    def produced(self, key, m):
        self.rdy[key] = m
        self.frees[key] = {}

    def consumed(self, key, m):
        if m is None:
            self.pe_pending.add(key)
            return
        d = self.frees.setdefault(key, {})
        s, v = m
        if d.get(s, 0) < v:
            d[s] = v

    def r(self, key):
        m = self.rdy.get(key)
        return [m] if m is not None else []

    def _pe_flush(self):
        if not self.pe_pending:
            return
        i = self.last_tensor_step
        eng, fn, wts, inc = self.steps[i]
        assert eng == "tensor"
        if inc is None:
            self.planned["s_pe"] += 1
            inc = ("s_pe", 1)
            self.steps[i] = (eng, fn, wts, inc)
        mi = ("s_pe", self.planned["s_pe"])
        for key in self.pe_pending:
            self.consumed(key, mi)
        self.pe_pending.clear()

    def fence(self, from_keys, to_keys):
        acc = {}
        for k in from_keys:
            for (sn, v) in self.w(k):
                if acc.get(sn, 0) < v:
                    acc[sn] = v
        for k in to_keys:
            d = self.frees.setdefault(k, {})
            for sn, v in acc.items():
                if d.get(sn, 0) < v:
                    d[sn] = v

    def w(self, key):
        if key in self.pe_pending:
            self._pe_flush()
        out = list(self.frees.get(key, {}).items())
        m = self.rdy.get(key)
        if m is not None:
            out.append(m)
        return out

    def realize(self, block):
        sems, steps = self.sems, self.steps

        def mk(eng_name):
            def run(e):
                hwm = {}
                for eng, fn, waits, inc in steps:
                    if eng != eng_name:
                        continue
                    for (sn, val) in waits:
                        if hwm.get(sn, 0) < val:
                            e.wait_ge(sems[sn], val)
                            hwm[sn] = val
                    ins = fn(e)
                    if inc is not None:
                        ins.then_inc(sems[inc[0]], inc[1])
            return run

        block.sync(mk('sync'))
        block.gpsimd(mk('gpsimd'))
        block.tensor(mk('tensor'))
        block.vector(mk('vector'))
        block.scalar(mk('scalar'))


def build(cfg: CFG):
    b = Builder(cfg)
    nc = b.nc
    T, D, M, H, CF, TS, NE, E, R, F = (cfg.T, cfg.D, cfg.M, cfg.H, cfg.CF,
                                       cfg.TS, cfg.NE, cfg.E, cfg.R, cfg.F)
    NBLK = cfg.nblk()

    XT = nc.dram_tensor("xt", [D, T], F32, kind="ExternalInput")
    WQ = nc.dram_tensor("wq", [M, 128, D], F32, kind="ExternalInput")
    WK = nc.dram_tensor("wk", [M, 128, D], F32, kind="ExternalInput")
    WV = nc.dram_tensor("wv", [M, 128, D], F32, kind="ExternalInput")
    WO = nc.dram_tensor("wo", [M, 128, D], F32, kind="ExternalInput")
    BCOLS = nc.dram_tensor("bcols", [128, 5 * M], F32, kind="ExternalInput")
    RW = nc.dram_tensor("rw", [128, M * 8], F32, kind="ExternalInput")
    WGU = nc.dram_tensor("wgu", [NE, CF, 128, 2 * D], F32, kind="ExternalInput")
    WD = nc.dram_tensor("wd", [NE, M, 128, F], F32, kind="ExternalInput")
    OUT = nc.dram_tensor("outT", [D, T], F32, kind="ExternalOutput")

    k_in = nc.dram_tensor("k_in", [D * T // 1024, 1024], BF16, kind="Internal")
    v_in = nc.dram_tensor("v_in", [D * T // 1024, 1024], BF16, kind="Internal")
    ag_space = "Shared" if R > 4 else "Local"
    ag_k = nc.dram_tensor("ag_k", [R * D * T // 1024, 1024], BF16,
                          kind="Internal", addr_space=ag_space)
    ag_v = nc.dram_tensor("ag_v", [R * D * T // 1024, 1024], BF16,
                          kind="Internal", addr_space=ag_space)
    bc_dram = nc.dram_tensor("bc_dram", [4, T], F32, kind="Internal")
    w_dram = nc.dram_tensor("w_dram", [8, T], BF16, kind="Internal")

    cvals = [0.0, 1.0, -1.0, 0.5, 1e-5, 1e-6, EXP_SCALE, EXP_SHIFT]
    CPOOL_D = nc.inline_tensor(np.array([cvals], np.float32), name="cpool_d")
    ONESC_D = nc.inline_tensor(np.ones((128, 1), ml_dtypes.bfloat16), name="onesc_d")
    ONESF_D = nc.inline_tensor(np.ones((128, 1), np.float32), name="onesf_d")
    IDF_D = nc.inline_tensor(np.eye(128, dtype=np.float32), name="idf_d")
    IDB_D = nc.inline_tensor(np.eye(128, dtype=ml_dtypes.bfloat16), name="idb_d")

    with ExitStack() as ctx:
        sbt = lambda n, s, d: ctx.enter_context(nc.sbuf_tensor(n, s, d))
        pst = lambda n, s, d: ctx.enter_context(nc.psum_tensor(n, s, d))

        xT = sbt("xT", [128, M * T], F32)
        hT = sbt("hT", [128, M * T], BF16)
        qT = sbt("qT", [128, H * T], BF16)
        lnscr = sbt("lnscr", [128, T], F32)
        sq = sbt("sq", [128, 2 * T], BF16)
        cpool = sbt("cpool", [128, len(cvals)], F32)
        onescol = sbt("onescol", [128, 1], BF16)
        onescolf = sbt("onescolf", [128, 1], F32)
        identf = sbt("identf", [128, 128], F32)
        identb = sbt("identb", [128, 128], BF16)
        bcols = sbt("bcols_sb", [128, 5 * M], F32)
        rw_sb = sbt("rw_sb", [128, M * 8], BF16)
        stats = sbt("stats", [1, 4 * T], F32)
        e_sb = sbt("e_sb", [128, 8], F32)
        msk = sbt("msk", [128, 8], F32)
        m1e = sbt("m1e", [128, 4], F32)
        w_sb = sbt("w_sb", [128, 8], F32)
        wT_bf = sbt("wT_bf", [8, T], BF16)
        WSLOT = 2 * D
        wring = sbt("wring", [128, 2 * WSLOT], BF16)
        AR = max(8 * T,
                 2 * CF * T + 6 * T + 6 * F,
                 36 * T)
        arena = sbt("arena", [128, AR], mybir.dt.uint8)
        ar_f32 = arena.ap().bitcast(F32)
        ar_bf = arena.ap().bitcast(BF16)
        nb2 = AR // 2
        nf4 = AR // 4
        A2bc = ar_f32[:, 0:T]
        B2bc = ar_f32[:, T:2 * T]
        hu_v = ar_bf[:, 0:CF * T]
        wbc_v = ar_bf[:, CF * T:CF * T + 3 * T]
        wdr_v = ar_bf[:, CF * T + 3 * T:CF * T + 3 * T + 3 * F]
        kring_v = ar_bf[:, 0:2 * T]
        vring_v = ar_bf[:, 2 * T:4 * T]
        expb_v = ar_bf[:, 4 * T:8 * T]
        evr_v = ar_bf[:, 8 * T:12 * T]
        recbc_v = ar_f32[:, 6 * T:8 * T]
        acc_v = ar_f32[:, 8 * T:9 * T]

        PA = pst("PA", [128, T], F32)
        PB = pst("PB", [128, T], F32)
        PC = pst("PC", [128, T], F32)
        PD = pst("PD", [128, T], F32)
        PS = {"PA": PA, "PB": PB, "PC": PC, "PD": PD}

        for i, v in enumerate(cvals):
            nc.const_aps.register(F32, v, cpool[:, i:i + 1])

        xT_c = lambda c: xT[:, c * T:(c + 1) * T]
        hT_c = lambda c: hT[:, c * T:(c + 1) * T]
        qT_c = lambda c: qT[:, c * T:(c + 1) * T]
        sq_s = lambda s: sq[:, s * T:(s + 1) * T]
        wring_s = lambda s, n=None: wring[:, s * WSLOT:s * WSLOT + (n or WSLOT)]
        wdr_s = lambda s: wdr_v[:, s * F:(s + 1) * F]
        hu_c = lambda c: hu_v[:, c * T:(c + 1) * T]
        wbc_s = lambda s: wbc_v[:, s * T:(s + 1) * T]
        kring_s = lambda s: kring_v[:, s * T:(s + 1) * T]
        vring_s = lambda s: vring_v[:, s * T:(s + 1) * T]
        expb_s = lambda s: expb_v[:, s * T:(s + 1) * T]
        evr_s = lambda s: evr_v[:, s * T:(s + 1) * T]
        recbc_s = lambda s: recbc_v[:, s * T:(s + 1) * T]
        rrow_s = lambda s: stats[0:1, s * T:(s + 1) * T]
        lnscr_bf = lnscr.ap().bitcast(BF16)[:, 0:T]
        LNK = ["A2bc", "B2bc"]
        FFNK = ([f"hu{c}" for c in range(CF)] + ["wbc0", "wbc1", "wbc2"]
                + [f"wdr{i}" for i in range(3)])
        ATTK = (["kring0", "kring1", "vring0", "vring1"]
                + [f"exp{i}" for i in range(4)] + [f"evr{i}" for i in range(4)]
                + ["recbc0", "recbc1", "accL", "accR"])

        bq_c = lambda c: bcols[:, 0 * M + c:0 * M + c + 1]
        bk_c = lambda c: bcols[:, 1 * M + c:1 * M + c + 1]
        bv_c = lambda c: bcols[:, 2 * M + c:2 * M + c + 1]
        bo_c = lambda c: bcols[:, 3 * M + c:3 * M + c + 1]
        rms_c = lambda c: bcols[:, 4 * M + c:4 * M + c + 1]

        kf = k_in.ap().rearrange("a b -> (a b)")
        vf = v_in.ap().rearrange("a b -> (a b)")
        agkf = ag_k.ap().rearrange("a b -> (a b)")
        agvf = ag_v.ap().rearrange("a b -> (a b)")

        def k_stage_ap(h):
            return kf[h * 128 * T:(h * 128 + 128) * T].rearrange(
                "(p k) -> p k", k=T)

        def v_stage_ap(h):
            vv = vf.rearrange("(k d) -> k d", d=D)
            return vv.rearrange("(s p) d -> p s d", p=128)[:, :, h * 128:(h + 1) * 128]

        def k_read_ap(r, h):
            base = r * D * T
            return agkf[base + h * 128 * T:base + (h * 128 + 128) * T].rearrange(
                "(p k) -> p k", k=T)

        def v_read_ap(r, h):
            base = r * D * T
            vv = agvf[base:base + D * T].rearrange("(k d) -> k d", d=D)
            return vv.rearrange("(c p) d -> p c d", p=128)[:, :, h * 128:(h + 1) * 128]

        m_cp = b.dma("sync", cpool[:, :],
                     CPOOL_D[:, :].to_broadcast([128, len(cvals)]), sem="s_pre")
        m_oc = b.dma("sync", onescol[:, :], ONESC_D[:, :], sem="s_pre")
        b.dma("sync", onescolf[:, :], ONESF_D[:, :], sem="s_pre")
        m_idf = b.dma("sync", identf[:, :], IDF_D[:, :], sem="s_pre")
        m_idb = b.dma("sync", identb[:, :], IDB_D[:, :], sem="s_pre")
        m_bcl = b.dma("sync", bcols[:, :], BCOLS[:, :], sem="s_pre")
        m_xt = b.dma("sync", xT[:, :], XT.ap().rearrange("(m p) t -> p m t", p=128), sem="s_pre")
        m_rw = b.dma("gpsimd", rw_sb[:, :], RW[:, :], sem="s_pre2")
        m_all = ("s_pre", b.planned["s_pre"])
        m_cp = m_oc = m_idf = m_idb = m_bcl = m_xt = m_all
        for c in range(M):
            b.produced(f"xT{c}", m_xt)
        b.produced("cpool", m_cp)

        def strow(i):
            return stats[0:1, i * T:(i + 1) * T]

        def layernorm(with_rms):
            cast_m = {}
            sum_m = suma_m = None
            for c in range(M):
                cm = b.dve(lambda v, c=c: v.tensor_copy(hT_c(c), xT_c(c)),
                           waits=b.r(f"xT{c}") + b.w(f"hT{c}"))
                b.consumed(f"xT{c}", cm)
                cast_m[c] = cm
                sl = c % 2
                sm = b.act(lambda s, c=c, sl=sl: s.activation(
                    sq_s(sl), xT_c(c), AF.Square),
                    waits=b.r(f"xT{c}") + b.w(f"sq{sl}") + b.r("cpool"))
                b.consumed(f"xT{c}", sm)
                b.produced(f"sq{sl}", sm)
                for bi, (c0, nb) in enumerate(NBLK):
                    last = (c == M - 1) and (bi == len(NBLK) - 1)
                    wa = [cm, m_oc] + (b.w("PA") if c == 0 and bi == 0 else [])
                    ma = b.mm(PA[0:1, c0:c0 + nb], onescol[:, :],
                              hT_c(c)[:, c0:c0 + nb],
                              start=(c == 0), stop=(c == M - 1), waits=wa,
                              inc=last)
                    b.consumed(f"hT{c}", ma)
                    wb = [sm, m_oc] + (b.w("PB") if c == 0 and bi == 0 else [])
                    mb = b.mm(PB[0:1, c0:c0 + nb], onescol[:, :],
                              sq_s(sl)[:, c0:c0 + nb],
                              start=(c == 0), stop=(c == M - 1), waits=wb,
                              inc=last)
                    b.consumed(f"sq{sl}", mb)
                    if last:
                        sum_m, suma_m = mb, ma
            b.produced("PA", suma_m)
            b.produced("PB", sum_m)
            r0, r1, r2, r3 = strow(0), strow(1), strow(2), strow(3)
            m_mu = b.dve(lambda v: v.tensor_scalar_mul(r0, PA[0:1, :], 1.0 / D),
                         waits=[suma_m, sum_m] + b.w("st0"))
            b.produced("st0", m_mu)
            b.consumed("PA", m_mu)
            m_e2 = b.dve(lambda v: v.tensor_scalar_mul(r1, PB[0:1, :], 1.0 / D),
                         waits=[sum_m] + b.w("st1"))
            b.produced("st1", m_e2)
            b.consumed("PB", m_e2)
            m_m2 = b.dve(lambda v: v.tensor_mul(r2, r0, r0),
                         waits=[m_mu] + b.w("st2"))
            b.produced("st2", m_m2)
            b.consumed("st0", m_m2)
            m_v = b.dve(lambda v: v.tensor_sub(r1, r1, r2),
                        waits=[m_e2, m_m2] + b.w("st1"))
            b.produced("st1", m_v)
            b.consumed("st2", m_v)
            m_sd = b.act(lambda sc: sc.activation(r2, r1, AF.Sqrt, bias=1e-5,
                                                  scale=1.0),
                         waits=[m_v] + b.r("cpool") + b.w("st2"))
            b.produced("st2", m_sd)
            b.consumed("st1", m_sd)
            m_rst = b.dve(lambda v: v.reciprocal(r2, r2),
                          waits=[m_sd] + b.w("st2"))
            b.produced("st2", m_rst)
            if with_rms:
                m_q = b.dve(lambda v: v.tensor_mul(r3, r2, r2),
                            waits=[m_rst] + b.w("st3"))
                b.produced("st3", m_q)
                b.consumed("st2", m_q)
                m_vr = b.dve(lambda v: v.tensor_mul(r3, r1, r3),
                             waits=[m_q, m_v] + b.w("st3"))
                b.produced("st3", m_vr)
                b.consumed("st1", m_vr)
                m_sr = b.act(lambda sc: sc.activation(r3, r3, AF.Sqrt, bias=1e-6,
                                                      scale=1.0),
                             waits=[m_vr] + b.r("cpool") + b.w("st3"))
                b.produced("st3", m_sr)
                m_rmr = b.dve(lambda v: v.reciprocal(r3, r3),
                              waits=[m_sr] + b.w("st3"))
                b.produced("st3", m_rmr)
                m_a2f = b.dve(lambda v: v.tensor_mul(r3, r2, r3),
                              waits=[m_rmr, m_rst] + b.w("st3"))
                b.produced("st3", m_a2f)
                b.consumed("st2", m_a2f)
                a2row, k_a2row = r3, "st3"
            else:
                a2row, k_a2row, m_a2f = r2, "st2", m_rst
            m_b2 = b.dve(lambda v: v.scalar_tensor_tensor(
                r0, r0, -1.0, a2row, ALU.mult, ALU.mult),
                waits=[m_mu, m_a2f] + b.w("st0"))
            b.produced("st0", m_b2)
            b.consumed(k_a2row, m_b2)
            b2, k_b2 = r0, "st0"
            m_oA = b.dma("sync", bc_dram[0:1, :], a2row,
                         waits=[m_a2f, m_b2] + b.w("bcd0"), sem="s_bcA")
            b.produced("bcd0", m_oA)
            b.consumed(k_a2row, m_oA)
            if not with_rms:
                b.consumed("st2", m_oA)
            m_oB = b.dma("sync", bc_dram[1:2, :], b2, waits=[m_b2] + b.w("bcd1"), sem="s_bcB")
            b.produced("bcd1", m_oB)
            b.consumed(k_b2, m_oB)
            m_iA = b.dma("sync", A2bc[:, :], bc_dram[0:1, :].to_broadcast([128, T]),
                         waits=[m_oA] + b.w("A2bc"), sem="s_bcA")
            b.produced("A2bc", m_iA)
            b.consumed("bcd0", m_iA)
            m_iB = b.dma("sync", B2bc[:, :], bc_dram[1:2, :].to_broadcast([128, T]),
                         waits=[m_oB] + b.w("B2bc"), sem="s_bcB")
            b.produced("B2bc", m_iB)
            b.consumed("bcd1", m_iB)
            for c in range(M):
                wv_ = (b.r(f"xT{c}") + [m_iA, sum_m, suma_m, cast_m[c]]
                       + b.w("lnscr"))
                if with_rms:
                    m_t1 = b.dve(lambda v, c=c: v.scalar_tensor_tensor(
                        lnscr[:, :], xT_c(c), rms_c(c), A2bc[:, :],
                        ALU.mult, ALU.mult), waits=wv_ + [m_bcl])
                    m_h = b.dve(lambda v, c=c: v.scalar_tensor_tensor(
                        hT_c(c), B2bc[:, :], rms_c(c), lnscr[:, :],
                        ALU.mult, ALU.add),
                        waits=[m_t1, m_iB] + b.w(f"hT{c}"))
                else:
                    m_t1 = b.dve(lambda v, c=c: v.tensor_mul(
                        lnscr[:, :], xT_c(c), A2bc[:, :]), waits=wv_)
                    m_h = b.dve(lambda v, c=c: v.tensor_add(
                        hT_c(c), lnscr[:, :], B2bc[:, :]),
                        waits=[m_t1, m_iB] + b.w(f"hT{c}"))
                b.consumed(f"xT{c}", m_t1)
                b.produced("lnscr", m_t1)
                b.consumed("lnscr", m_h)
                b.produced(f"hT{c}", m_h)
                b.consumed("A2bc", m_t1)
                b.consumed("B2bc", m_h)

        def router():
            tr_ms = []
            mm1 = m1e[:, 0:1]
            mm2 = m1e[:, 1:2]
            mneg = m1e[:, 2:3]
            mrec = m1e[:, 3:4]
            for s in range(TS):
                lg = None
                for m in range(M):
                    wv_ = b.r(f"hT{m}") + [m_rw]
                    if m == 0:
                        wv_ += b.w("PC")
                    lg = b.mm(PC[:, 0:8], hT_c(m)[:, s * 128:(s + 1) * 128],
                              rw_sb[:, m * 8:(m + 1) * 8],
                              start=(m == 0), stop=(m == M - 1), waits=wv_,
                              inc=(m == M - 1))
                    b.consumed(f"hT{m}", lg)
                b.produced("PC", lg)
                m_max = b.dve(lambda v: v.reduce_max(mm1, PC[:, 0:8], axis=AXX),
                              waits=[lg] + b.w("m1e"))
                m_mn = b.dve(lambda v: v.tensor_scalar_mul(mneg, mm1, -1.0),
                             waits=[m_max])
                m_e = b.act(lambda sc: sc.activation(e_sb[:, :], PC[:, 0:8],
                                                     AF.Exp, bias=mneg, scale=1.0),
                            waits=[m_mn, lg] + b.w("e_sb") + b.r("cpool"))
                b.consumed("PC", m_e)
                b.produced("e_sb", m_e)
                m_em = b.dve(lambda v: v.reduce_max(mm1, e_sb[:, :], axis=AXX),
                             waits=[m_e, m_mn])
                b.consumed("e_sb", m_em)
                m_k1 = b.dve(lambda v: v.tensor_scalar(
                    msk[:, :], e_sb[:, :], mm1, None, ALU.is_ge),
                    waits=[m_em] + b.w("msk"))
                b.consumed("e_sb", m_k1)
                m_k1m = b.dve(lambda v: v.tensor_scalar_mul(msk[:, :], msk[:, :],
                                                            mm1), waits=[m_k1])
                m_e2 = b.dve(lambda v: v.tensor_sub(msk[:, :], e_sb[:, :],
                                                    msk[:, :]), waits=[m_k1m])
                b.consumed("e_sb", m_e2)
                m_m2 = b.dve(lambda v: v.reduce_max(mm2, msk[:, :], axis=AXX),
                             waits=[m_e2])
                m_kp = b.dve(lambda v: v.tensor_scalar(
                    msk[:, :], e_sb[:, :], mm2, None, ALU.is_ge), waits=[m_m2])
                b.consumed("e_sb", m_kp)
                m_wr = b.dve(lambda v: v.tensor_mul(msk[:, :], e_sb[:, :],
                                                    msk[:, :]), waits=[m_kp])
                b.consumed("e_sb", m_wr)
                m_dn = b.dve(lambda v: v.tensor_add(mneg, mm1, mm2),
                             waits=[m_m2, m_em])
                m_rc = b.dve(lambda v: v.reciprocal(mrec, mneg), waits=[m_dn])
                m_rh = b.dve(lambda v: v.tensor_scalar_mul(mrec, mrec, 0.5),
                             waits=[m_rc])
                m_w = b.dve(lambda v: v.tensor_scalar_mul(w_sb[:, :], msk[:, :],
                                                          mrec),
                            waits=[m_rh, m_wr] + b.w("w_sb"))
                b.produced("w_sb", m_w)
                b.produced("m1e", m_w)
                b.produced("msk", m_w)
                wv_ = [m_w, m_idf]
                if s == 0:
                    wv_ += b.w("PD")
                first_in_bank = (s * 128) % 512 == 0
                last_in_bank = ((s + 1) * 128) % 512 == 0 or s == TS - 1
                tr = b.emit("tensor",
                            lambda t, s=s, a=first_in_bank, z=last_in_bank:
                                t.matmul(PD[0:8, s * 128:(s + 1) * 128],
                                         w_sb[:, :], identf[:, :],
                                         start=a, stop=z, is_transpose=True),
                            waits=wv_, inc=("s_pe", 1))
                b.consumed("w_sb", tr)
                tr_ms.append(tr)
            b.produced("PD", tr_ms[-1])
            m_ev = b.act(lambda sc: sc.activation(wT_bf[:, :], PD[0:8, 0:T],
                                                  AF.Identity, bias=0.0, scale=1.0),
                         waits=tr_ms + b.w("wT_bf") + b.r("cpool"))
            b.consumed("PD", m_ev)
            b.produced("wT_bf", m_ev)
            m_st = b.dma("sync", w_dram[0:8, :], wT_bf[:, :],
                         waits=[m_ev] + b.w("w_dram"), sem="s_wdram")
            b.produced("w_dram", m_st)
            b.consumed("wT_bf", m_st)
            return m_st

        wslab_i = [0]
        pslab_i = [0]
        wd_i = [0]

        def ffn(tag, final_store=False):
            m_route = router()
            m_w8 = b.emit("gpsimd", lambda g: g.memset(wbc_s(2), 0.5),
                          waits=b.w("wbc2"), inc=("s_gpe", 1))
            b.produced("wbc2", m_w8)
            for e in range(NE):
                if e < E:
                    ws = e % 2
                    wkey = f"wbc{ws}"
                    m_bc = b.dma("scalar", wbc_s(ws),
                                 w_dram[e:e + 1, :].to_broadcast([128, T]),
                                 waits=[m_route] + b.w(wkey), sem=f"s_wbc{ws}")
                    b.produced(wkey, m_bc)
                    wbc_ap = wbc_s(ws)
                else:
                    wkey = "wbc2"
                    wbc_ap = wbc_s(2)
                for c in range(CF):
                    sl = wslab_i[0] % 2
                    wslab_i[0] += 1
                    m_sl = b.dma("gpsimd", wring_s(sl),
                                 WGU[e, c, :, :], waits=b.w(f"wring{sl}"),
                                 sem=f"s_wr{sl}")
                    b.produced(f"wring{sl}", m_sl)
                    lg = lu = None
                    for m in range(M):
                        for bi, (c0, nb) in enumerate(NBLK):
                            wv_ = [m_sl] + b.r(f"hT{m}")
                            if m == 0 and bi == 0:
                                wv_ += b.w("PA")
                            lg = b.mm(PA[:, c0:c0 + nb],
                                      wring_s(sl)[:, m * 128:(m + 1) * 128],
                                      hT_c(m)[:, c0:c0 + nb],
                                      start=(m == 0), stop=(m == M - 1),
                                      waits=wv_,
                                      inc=(m == M - 1 and bi == len(NBLK) - 1))
                            b.consumed(f"hT{m}", lg)
                            b.consumed(f"wring{sl}", lg)
                    for m in range(M):
                        for bi, (c0, nb) in enumerate(NBLK):
                            wv_ = [m_sl] + b.r(f"hT{m}")
                            if m == 0 and bi == 0:
                                wv_ += b.w("PB")
                            lu = b.mm(PB[:, c0:c0 + nb],
                                      wring_s(sl)[:, D + m * 128:D + (m + 1) * 128],
                                      hT_c(m)[:, c0:c0 + nb],
                                      start=(m == 0), stop=(m == M - 1),
                                      waits=wv_,
                                      inc=(m == M - 1 and bi == len(NBLK) - 1))
                            b.consumed(f"hT{m}", lu)
                            b.consumed(f"wring{sl}", lu)
                    b.produced("PA", lg)
                    b.produced("PB", lu)
                    ssl = c % 2
                    m_si = b.act(lambda s, sl2=ssl: s.activation(
                        sq_s(sl2), PA[:, :], AF.Silu),
                        waits=[lg] + b.w(f"sq{ssl}") + b.r("cpool"))
                    b.consumed("PA", m_si)
                    b.produced(f"sq{ssl}", m_si)
                    m_t = b.dve(lambda v, sl2=ssl, c=c: v.tensor_mul(
                        hu_c(c), sq_s(sl2), PB[:, :]),
                        waits=[m_si, lu] + b.w(f"hu{c}"))
                    b.consumed("PB", m_t)
                    b.consumed(f"sq{ssl}", m_t)
                    b.produced(f"hu{c}", m_t)
                    m_hu = b.dve(lambda v, c=c, wb=wbc_ap: v.tensor_mul(
                        hu_c(c), hu_c(c), wb),
                        waits=[m_t] + b.r(wkey) + b.w(f"hu{c}"))
                    b.consumed(wkey, m_hu)
                    b.produced(f"hu{c}", m_hu)
                for j in range(M):
                    sl = wd_i[0] % 3
                    wd_i[0] += 1
                    m_sl = b.dma("gpsimd", wdr_s(sl), WD[e, j, :, :],
                                 waits=b.w(f"wdr{sl}"), sem=f"s_wd{sl}")
                    b.produced(f"wdr{sl}", m_sl)
                    pk = "PC" if j % 2 == 0 else "PD"
                    P = PS[pk]
                    dn = None
                    for c in range(CF):
                        for bi, (c0, nb) in enumerate(NBLK):
                            wv_ = [m_sl] + b.r(f"hu{c}")
                            if c == 0 and bi == 0:
                                wv_ += b.w(pk)
                            dn = b.mm(P[:, c0:c0 + nb],
                                      wdr_s(sl)[:, c * 128:(c + 1) * 128],
                                      hu_c(c)[:, c0:c0 + nb],
                                      start=(c == 0), stop=(c == CF - 1),
                                      waits=wv_,
                                      inc=(c == CF - 1 and bi == len(NBLK) - 1))
                            b.consumed(f"hu{c}", dn)
                            b.consumed(f"wdr{sl}", dn)
                    b.produced(pk, dn)
                    m_x = b.dve(lambda v, j=j, P=P: v.tensor_add(
                        xT_c(j), xT_c(j), P[:, :]),
                        waits=[dn] + b.w(f"xT{j}"))
                    b.consumed(pk, m_x)
                    b.produced(f"xT{j}", m_x)
                    if final_store and e == NE - 1:
                        mo = b.dma("sync",
                                   OUT.ap().rearrange("(m p) t -> p m t", p=128)
                                   [:, j:j + 1, :],
                                   xT_c(j), waits=[m_x], sem="s_out")
                        b.consumed(f"xT{j}", mo)
                        out_stores.append(mo)

        out_stores = []

        def pr_s(sl):
            return wring[:, sl * D:(sl + 1) * D]

        def proj_feature_major(Wdram, bias_c, out_key_fn, out_ap_fn):
            for j in range(M):
                sl = pslab_i[0] % 4
                pslab_i[0] += 1
                m_sl = b.dma("gpsimd", pr_s(sl), Wdram[j, :, :],
                             waits=b.w(f"pr{sl}"), sem=f"s_pr{sl}")
                b.produced(f"pr{sl}", m_sl)
                pk = "PA" if j % 2 == 0 else "PB"
                P = PS[pk]
                pm = None
                for m in range(M):
                    for bi, (c0, nb) in enumerate(NBLK):
                        wv_ = [m_sl] + b.r(f"hT{m}")
                        if m == 0 and bi == 0:
                            wv_ += b.w(pk)
                        pm = b.mm(P[:, c0:c0 + nb],
                                  pr_s(sl)[:, m * 128:(m + 1) * 128],
                                  hT_c(m)[:, c0:c0 + nb],
                                  start=(m == 0), stop=(m == M - 1), waits=wv_,
                                  inc=(m == M - 1 and bi == len(NBLK) - 1))
                        b.consumed(f"hT{m}", pm)
                        b.consumed(f"pr{sl}", pm)
                b.produced(pk, pm)
                okey = out_key_fn(j)
                m_ev = b.act(lambda s, j=j, P=P, oap=out_ap_fn(j), bap=bias_c(j):
                             s.activation(oap, P[:, :], AF.Identity, bias=bap,
                                          scale=1.0),
                             waits=[pm, m_bcl] + b.w(okey) + b.r("cpool"))
                b.consumed(pk, m_ev)
                b.produced(okey, m_ev)
                yield j, okey, m_ev

        def qkv_and_ag():
            k_writes, v_writes = [], []
            for j, okey, m_ev in proj_feature_major(
                    WK, bk_c, lambda j: f"evr{j % 2}", lambda j: evr_s(j % 2)):
                m_o = b.dma("sync", k_stage_ap(j), evr_s(j % 2), waits=[m_ev],
                             sem=f"s_kv{j % 2}")
                b.consumed(okey, m_o)
                k_writes.append(m_o)
            m_ccK = b.emit("gpsimd",
                           lambda g: g.collective_compute(
                               "AllGather", ALU.bypass,
                               replica_groups=[list(range(R))],
                               ins=[k_in.ap().opt()],
                               outs=[ag_k.ap().opt()]),
                           waits=k_writes, inc=("s_cc", 1))
            PDb = PD.ap().bitcast(BF16)
            for j, okey, m_ev in proj_feature_major(
                    WV, bv_c, lambda j: f"evr{j % 2}", lambda j: evr_s(j % 2)):
                trs = []
                for s in range(TS):
                    wv_ = [m_ev, m_idb]
                    if s == 0:
                        wv_ += b.w("PD")
                    tr = b.emit("tensor",
                                lambda t, s=s, j=j: t.matmul(
                                    PDb[:, s * 128:(s + 1) * 128],
                                    evr_s(j % 2)[:, s * 128:(s + 1) * 128],
                                    identb[:, :],
                                    start=(s == 0), stop=(s == TS - 1),
                                    is_transpose=True),
                                waits=wv_, inc=("s_pe", 1))
                    b.consumed(okey, tr)
                    trs.append(tr)
                b.produced("PD", trs[-1])
                vk = 2 + (j % 2)
                m_vv = b.act(lambda s, vk=vk: s.activation(
                    evr_s(vk), PDb[:, 0:T], AF.Identity, bias=0.0, scale=1.0),
                    waits=trs + b.w(f"evr{vk}") + b.r("cpool"))
                b.consumed("PD", m_vv)
                b.produced(f"evr{vk}", m_vv)
                m_o = b.dma("sync", v_stage_ap(j),
                            evr_s(vk).rearrange("p (s d) -> p s d", d=128),
                            waits=[m_vv], sem=f"s_kv{vk}")
                b.consumed(f"evr{vk}", m_o)
                v_writes.append(m_o)
            m_ccV = b.emit("gpsimd",
                           lambda g: g.collective_compute(
                               "AllGather", ALU.bypass,
                               replica_groups=[list(range(R))],
                               ins=[v_in.ap().opt()],
                               outs=[ag_v.ap().opt()]),
                           waits=v_writes, inc=("s_cc2", 1))
            for _ in proj_feature_major(WQ, bq_c, lambda j: f"qT{j}",
                                        lambda j: qT_c(j)):
                pass
            return m_ccK, m_ccV

        def attention(m_ccK, m_ccV):
            NKC = R * TS

            def kv_dma(h, r):
                ks = r % 2
                m_k = b.dma("sync", kring_s(ks), k_read_ap(r, h),
                            waits=[m_ccK] + b.w(f"kring{ks}"), sem=f"s_kr{ks}")
                b.produced(f"kring{ks}", m_k)
                m_v = b.dma("sync",
                            vring_s(ks).rearrange("p (c d) -> p c d", d=128),
                            v_read_ap(r, h), waits=[m_ccV] + b.w(f"vring{ks}"),
                            sem=f"s_vr{ks}")
                b.produced(f"vring{ks}", m_v)
                return ks, m_k, m_v

            exp_i = [0]
            pend_tail = [None]

            def emit_div(h):
                rs = h % 2
                m_ri = b.rdy[f"recbc{rs}"]
                m_at = b.rdy[f"qT{h}"]
                m_dv = b.dve(lambda v, h=h, rs=rs: v.tensor_mul(
                    qT_c(h), qT_c(h), recbc_s(rs)),
                    waits=[m_ri, m_at])
                b.consumed(f"recbc{rs}", m_dv)
                b.produced(f"qT{h}", m_dv)

            def emit_tail():
                if pend_tail[0] is None:
                    return
                h, avs = pend_tail[0]
                pend_tail[0] = None
                m_aL = b.rdy["accL"]
                sms = None
                for bi, (c0, nb) in enumerate(NBLK):
                    wv_ = [m_aL, m_oc] + (b.w("PD") if bi == 0 else [])
                    sms = b.mm(PD[0:1, c0:c0 + nb], onescolf[:, :],
                               acc_v[:, c0:c0 + nb],
                               start=True, stop=True, waits=wv_,
                               inc=(bi == len(NBLK) - 1))
                    b.consumed("accL", sms)
                b.produced("PD", sms)
                m_at = b.act(lambda sc, h=h: sc.activation(
                    qT_c(h), PC[:, :], AF.Identity, bias=0.0, scale=1.0),
                    waits=[avs] + b.w(f"qT{h}") + b.r("cpool"))
                b.consumed("PC", m_at)
                b.produced(f"qT{h}", m_at)
                rs = h % 2
                m_rc = b.dve(lambda v, rs=rs: v.reciprocal(rrow_s(rs),
                                                           PD[0:1, :]),
                             waits=[sms] + b.w(f"st{rs}"))
                b.consumed("PD", m_rc)
                b.produced(f"st{rs}", m_rc)
                m_ro = b.dma("gpsimd", bc_dram[2:3, :], rrow_s(rs),
                             waits=[m_rc] + b.w("bcd2"), sem=f"s_rec{rs}")
                b.produced("bcd2", m_ro)
                b.consumed(f"st{rs}", m_ro)
                m_ri = b.dma("gpsimd", recbc_s(rs),
                             bc_dram[2:3, :].to_broadcast([128, T]),
                             waits=[m_ro] + b.w(f"recbc{rs}"), sem=f"s_rec{rs}")
                b.produced(f"recbc{rs}", m_ri)
                b.consumed("bcd2", m_ri)

            for h in range(H):
                pend = {}
                pend[0] = kv_dma(h, 0)
                if R > 1:
                    pend[1] = kv_dma(h, 1)
                avs = None
                prev_av = None
                for r in range(R):
                    ks, m_k, m_v = pend.pop(r)
                    for c in range(TS):
                        kc = r * TS + c
                        spk = "PA" if kc % 2 == 0 else "PB"
                        SP = PS[spk]
                        sc_m = None
                        for bi, (c0, nb) in enumerate(NBLK):
                            wv_ = [m_k] + b.r(f"qT{h}") + (b.w(spk) if bi == 0
                                                           else [])
                            sc_m = b.mm(SP[:, c0:c0 + nb],
                                        kring_s(ks)[:, c * 128:(c + 1) * 128],
                                        qT_c(h)[:, c0:c0 + nb],
                                        start=True, stop=True, waits=wv_,
                                        inc=(bi == len(NBLK) - 1))
                            b.consumed(f"qT{h}", sc_m)
                            b.consumed(f"kring{ks}", sc_m)
                        b.produced(spk, sc_m)
                        es = exp_i[0] % 4
                        exp_i[0] += 1
                        m_x = b.act(lambda sc, es=es, SP=SP: sc.activation(
                            expb_s(es), SP[:, :], AF.Exp, bias=EXP_SHIFT,
                            scale=EXP_SCALE),
                            waits=[sc_m] + b.w(f"exp{es}") + b.r("cpool"))
                        b.consumed(spk, m_x)
                        b.produced(f"exp{es}", m_x)
                        if kc == 0:
                            emit_tail()
                        if kc == 0:
                            m_aL = b.dve(lambda v, es=es: v.tensor_copy(
                                acc_v, expb_s(es)),
                                waits=[m_x] + b.w("accL"))
                        else:
                            m_aL = b.dve(lambda v, es=es: v.tensor_add(
                                acc_v, acc_v, expb_s(es)),
                                waits=[m_x] + b.w("accL"))
                        b.produced("accL", m_aL)
                        b.consumed(f"exp{es}", m_aL)
                        if prev_av is not None:
                            pSP, pes, pks, pkc, pm_v = prev_av
                            pc = pkc % TS
                            m_px = b.rdy[f"exp{pes}"]
                            for bi, (c0, nb) in enumerate(NBLK):
                                wv_ = [m_px, pm_v] + (b.w("PC")
                                                      if pkc == 0 and bi == 0
                                                      else [])
                                avs = b.mm(PC[:, c0:c0 + nb],
                                           vring_s(pks)[:, pc * 128:(pc + 1) * 128],
                                           expb_s(pes)[:, c0:c0 + nb],
                                           start=(pkc == 0), stop=False,
                                           waits=wv_,
                                           inc=(bi == len(NBLK) - 1))
                                b.consumed(f"exp{pes}", avs)
                                b.consumed(f"vring{pks}", avs)
                        prev_av = (SP, es, ks, kc, m_v)
                        if h > 0 and r == 0 and c == min(4, TS - 1):
                            emit_div(h - 1)
                    if r + 2 < R:
                        pend[r + 2] = kv_dma(h, r + 2)
                pSP, pes, pks, pkc, pm_v = prev_av
                pc = pkc % TS
                m_px = b.rdy[f"exp{pes}"]
                for bi, (c0, nb) in enumerate(NBLK):
                    wv_ = [m_px, pm_v] + (b.w("PC") if pkc == 0 and bi == 0
                                          else [])
                    avs = b.mm(PC[:, c0:c0 + nb],
                               vring_s(pks)[:, pc * 128:(pc + 1) * 128],
                               expb_s(pes)[:, c0:c0 + nb],
                               start=(pkc == 0), stop=True, waits=wv_,
                               inc=(bi == len(NBLK) - 1))
                    b.consumed(f"exp{pes}", avs)
                    b.consumed(f"vring{pks}", avs)
                b.produced("PC", avs)
                pend_tail[0] = (h, avs)
            emit_tail()
            emit_div(H - 1)
        def outproj():
            for j in range(M):
                sl = pslab_i[0] % 4
                pslab_i[0] += 1
                m_sl = b.dma("gpsimd", pr_s(sl), WO[j, :, :],
                             waits=b.w(f"pr{sl}"), sem=f"s_pr{sl}")
                b.produced(f"pr{sl}", m_sl)
                pk = "PA" if j % 2 == 0 else "PB"
                P = PS[pk]
                pm = None
                for m in range(M):
                    for bi, (c0, nb) in enumerate(NBLK):
                        wv_ = [m_sl] + b.r(f"qT{m}")
                        if m == 0 and bi == 0:
                            wv_ += b.w(pk)
                        pm = b.mm(P[:, c0:c0 + nb],
                                  pr_s(sl)[:, m * 128:(m + 1) * 128],
                                  qT_c(m)[:, c0:c0 + nb],
                                  start=(m == 0), stop=(m == M - 1), waits=wv_,
                                  inc=(m == M - 1 and bi == len(NBLK) - 1))
                        b.consumed(f"qT{m}", pm)
                        b.consumed(f"pr{sl}", pm)
                b.produced(pk, pm)
                m_x = b.dve(lambda v, j=j, P=P: v.scalar_tensor_tensor(
                    xT_c(j), P[:, :], bo_c(j), xT_c(j), ALU.add, ALU.add),
                    waits=[pm, m_bcl] + b.w(f"xT{j}"))
                b.consumed(pk, m_x)
                b.produced(f"xT{j}", m_x)

        WRK = ["wring0", "wring1"]
        PRK = ["pr0", "pr1", "pr2", "pr3"]
        layernorm(with_rms=True)
        b.fence(LNK, FFNK)
        ffn("f1")
        b.fence(FFNK, LNK)
        b.fence(WRK, PRK)
        layernorm(with_rms=False)
        b.fence(LNK, ATTK)
        m_ccK, m_ccV = qkv_and_ag()
        attention(m_ccK, m_ccV)
        outproj()
        b.fence(ATTK, LNK)
        b.fence(PRK, WRK)
        layernorm(with_rms=True)
        b.fence(LNK, FFNK)
        ffn("f2", final_store=True)
        b.emit("sync", lambda e: e.nop(), waits=[out_stores[-1]])

        block = ctx.enter_context(nc.Block())
        b.realize(block)
    return nc



def _host_layout(inputs, cfg: CFG):
    D, M, F, E, NE, CF = cfg.D, cfg.M, cfg.F, cfg.E, cfg.NE, cfg.CF
    x = np.asarray(inputs["L0_attn_hidden_states"], np.float32)[0]
    wq = np.asarray(inputs["L0_attn_wq"], np.float32)
    wk = np.asarray(inputs["L0_attn_wk"], np.float32)
    wv = np.asarray(inputs["L0_attn_wv"], np.float32)
    wo = np.asarray(inputs["L0_attn_wo"], np.float32)

    def wslabs(w):
        wt = np.ascontiguousarray(w.T)
        a = wt.reshape(M, 128, M, 128)
        return np.ascontiguousarray(a.transpose(2, 1, 0, 3)).reshape(M, 128, D)

    def bcol(v):
        return np.ascontiguousarray(np.asarray(v, np.float32).reshape(M, 128).T)

    bq = bcol(inputs["L0_attn_bq"]); bk = bcol(inputs["L0_attn_bk"])
    bv = bcol(inputs["L0_attn_bv"]); bo = bcol(inputs["L0_attn_bo"])
    rms = bcol(inputs["L0_ffn_rms_weight"])
    bcols = np.ascontiguousarray(np.concatenate([bq, bk, bv, bo, rms], axis=1))

    rw = np.asarray(inputs["L0_ffn_router_w"], np.float32)
    rw_s = np.ascontiguousarray(rw.reshape(M, 128, 8).transpose(1, 0, 2)
                                .reshape(128, M * 8))

    wg = np.asarray(inputs["L0_ffn_w_gate"], np.float32)
    wu = np.asarray(inputs["L0_ffn_w_up"], np.float32)
    wd = np.asarray(inputs["L0_ffn_w_down"], np.float32)
    swg = np.asarray(inputs["L0_ffn_sw_gate"], np.float32)
    swu = np.asarray(inputs["L0_ffn_sw_up"], np.float32)
    swd = np.asarray(inputs["L0_ffn_sw_down"], np.float32)

    def gu_slabs(g, u):
        def t(w):
            return w.reshape(M, 128, CF, 128).transpose(2, 1, 0, 3)
        out = np.stack([t(g), t(u)], axis=2)
        return out.reshape(CF, 128, 2 * D)

    wgu = np.empty((NE, CF, 128, 2 * D), np.float32)
    for e in range(E):
        wgu[e] = gu_slabs(wg[e], wu[e])
    wgu[E] = gu_slabs(swg, swu)
    wgu = np.ascontiguousarray(wgu)

    def d_slabs(w):
        return (w.reshape(CF, 128, M, 128).transpose(2, 1, 0, 3)
                .reshape(M, 128, F))

    wds = np.empty((NE, M, 128, F), np.float32)
    for e in range(E):
        wds[e] = d_slabs(wd[e])
    wds[E] = d_slabs(swd)
    wds = np.ascontiguousarray(wds)

    shared = dict(wq=wslabs(wq), wk=wslabs(wk), wv=wslabs(wv), wo=wslabs(wo),
                  bcols=bcols, rw=rw_s, wgu=wgu, wd=wds)
    in_maps = []
    for r in range(cfg.R):
        m = dict(shared)
        m["xt"] = np.ascontiguousarray(x[r * cfg.T:(r + 1) * cfg.T, :].T)
        in_maps.append(m)
    return in_maps


def kernel(**inputs):
    cfg = CFG()
    nc = build(cfg)
    in_maps = _host_layout(inputs, cfg)
    res = run_bass_kernel_spmd(nc, in_maps, core_ids=list(range(cfg.R)))
    out = np.empty((1, cfg.S, cfg.D), np.float32)
    for r in range(cfg.R):
        out[0, r * cfg.T:(r + 1) * cfg.T, :] = res.results[r]["outT"].T
    return out


# version 08
# speedup vs baseline: 1.0008x; 1.0008x over previous
import numpy as np
import ml_dtypes
from contextlib import ExitStack
import concourse.bass as bass
import concourse.mybir as mybir
import concourse.bass_utils as _bass_utils
from concourse.bass_utils import run_bass_kernel_spmd

if not getattr(_bass_utils, "_ldw_opt_patched", False):
    _orig_run_command = _bass_utils.run_command

    def _run_command_ldw(cmd, *a, **kw):
        if isinstance(cmd, list):
            cmd = ["--enable-ldw-opt=true" if c == "--enable-ldw-opt=false"
                   else c for c in cmd]
        return _orig_run_command(cmd, *a, **kw)

    _bass_utils.run_command = _run_command_ldw
    _bass_utils._ldw_opt_patched = True

F32 = mybir.dt.float32
BF16 = mybir.dt.bfloat16
AF = mybir.ActivationFunctionType
ALU = mybir.AluOpType
AXX = mybir.AxisListType.X

EXP_SCALE = float(np.float32(1.0 / np.sqrt(128.0)))
EXP_SHIFT = -10.0


class CFG:
    def __init__(self, R=8, T=1024, D=2048, F=1024, E=8):
        self.R, self.T, self.D, self.F, self.E = R, T, D, F, E
        self.M = D // 128
        self.H = D // 128
        self.CF = F // 128
        self.TS = T // 128
        self.NE = E + 1
        self.S = R * T
        assert T % 128 == 0 and D % 128 == 0 and F % 128 == 0

    def nblk(self):
        out, c = [], 0
        while c < self.T:
            n = min(512, self.T - c)
            out.append((c, n))
            c += n
        return out


class Builder:
    ENGINES = ("sync", "gpsimd", "tensor", "vector", "scalar")

    def __init__(self, cfg: CFG):
        self.cfg = cfg
        self.nc = bass.Bass("TRN2", target_bir_lowering=False)
        self.steps = []
        self.sems = {}
        self.planned = {}
        for s in ("s_sync", "s_gp", "s_gpe", "s_pe", "s_act", "s_dve", "s_cc",
                  "s_pre", "s_wr0", "s_wr1", "s_wr2", "s_wd0", "s_wd1", "s_wd2",
                  "s_kr0", "s_kr1", "s_vr0", "s_vr1", "s_kv0", "s_kv1", "s_kv2",
                  "s_kv3", "s_bcA", "s_bcB",
                  "s_rec0", "s_rec1", "s_wbc0", "s_wbc1", "s_wdram", "s_out", "s_cc2",
                  "s_pr0", "s_pr1", "s_pr2", "s_pr3",
                  "s_pre2"):
            self.sems[s] = self.nc.alloc_semaphore(s)
            self.planned[s] = 0
        self.rdy = {}
        self.frees = {}
        self.pe_pending = set()
        self.last_tensor_step = None

    def emit(self, eng, fn, waits=(), inc=None):
        assert eng in self.ENGINES
        w = [x for x in waits if x is not None]
        mi = None
        if inc is not None:
            self.planned[inc[0]] += inc[1]
            mi = (inc[0], self.planned[inc[0]])
        self.steps.append((eng, fn, tuple(w), inc))
        if eng == "tensor":
            self.last_tensor_step = len(self.steps) - 1
            if mi is not None and self.pe_pending:
                for key in self.pe_pending:
                    self.consumed(key, mi)
                self.pe_pending.clear()
        return mi

    def dma(self, eng, out_ap, in_ap, waits=(), sem=None):
        assert sem is not None
        return self.emit(eng, lambda e, o=out_ap, i=in_ap: e.dma_start(out=o, in_=i),
                         waits=waits, inc=(sem, 16))

    def mm(self, out_ap, lhsT, rhs, start, stop, waits=(), inc=False):
        return self.emit(
            "tensor",
            lambda t, o=out_ap, l=lhsT, r=rhs, a=start, b=stop:
                t.matmul(o, l, r, start=a, stop=b),
            waits=waits, inc=("s_pe", 1) if inc else None)

    def act(self, fn, waits=()):
        return self.emit("scalar", fn, waits=waits, inc=("s_act", 1))

    def dve(self, fn, waits=()):
        return self.emit("vector", fn, waits=waits, inc=("s_dve", 1))

    def gpe(self, fn, waits=()):
        return self.emit("gpsimd", fn, waits=waits, inc=("s_gpe", 1))

    def produced(self, key, m):
        self.rdy[key] = m
        self.frees[key] = {}

    def consumed(self, key, m):
        if m is None:
            self.pe_pending.add(key)
            return
        d = self.frees.setdefault(key, {})
        s, v = m
        if d.get(s, 0) < v:
            d[s] = v

    def r(self, key):
        m = self.rdy.get(key)
        return [m] if m is not None else []

    def _pe_flush(self):
        if not self.pe_pending:
            return
        i = self.last_tensor_step
        eng, fn, wts, inc = self.steps[i]
        assert eng == "tensor"
        if inc is None:
            self.planned["s_pe"] += 1
            inc = ("s_pe", 1)
            self.steps[i] = (eng, fn, wts, inc)
        mi = ("s_pe", self.planned["s_pe"])
        for key in self.pe_pending:
            self.consumed(key, mi)
        self.pe_pending.clear()

    def fence(self, from_keys, to_keys):
        acc = {}
        for k in from_keys:
            for (sn, v) in self.w(k):
                if acc.get(sn, 0) < v:
                    acc[sn] = v
        for k in to_keys:
            d = self.frees.setdefault(k, {})
            for sn, v in acc.items():
                if d.get(sn, 0) < v:
                    d[sn] = v

    def w(self, key):
        if key in self.pe_pending:
            self._pe_flush()
        out = list(self.frees.get(key, {}).items())
        m = self.rdy.get(key)
        if m is not None:
            out.append(m)
        return out

    def realize(self, block):
        sems, steps = self.sems, self.steps

        def mk(eng_name):
            def run(e):
                hwm = {}
                for eng, fn, waits, inc in steps:
                    if eng != eng_name:
                        continue
                    for (sn, val) in waits:
                        if hwm.get(sn, 0) < val:
                            e.wait_ge(sems[sn], val)
                            hwm[sn] = val
                    ins = fn(e)
                    if inc is not None:
                        ins.then_inc(sems[inc[0]], inc[1])
            return run

        block.sync(mk('sync'))
        block.gpsimd(mk('gpsimd'))
        block.tensor(mk('tensor'))
        block.vector(mk('vector'))
        block.scalar(mk('scalar'))


def build(cfg: CFG):
    b = Builder(cfg)
    nc = b.nc
    T, D, M, H, CF, TS, NE, E, R, F = (cfg.T, cfg.D, cfg.M, cfg.H, cfg.CF,
                                       cfg.TS, cfg.NE, cfg.E, cfg.R, cfg.F)
    NBLK = cfg.nblk()

    XT = nc.dram_tensor("xt", [D, T], F32, kind="ExternalInput")
    WQ = nc.dram_tensor("wq", [M, 128, D], F32, kind="ExternalInput")
    WK = nc.dram_tensor("wk", [M, 128, D], F32, kind="ExternalInput")
    WV = nc.dram_tensor("wv", [M, 128, D], F32, kind="ExternalInput")
    WO = nc.dram_tensor("wo", [M, 128, D], F32, kind="ExternalInput")
    BCOLS = nc.dram_tensor("bcols", [128, 5 * M], F32, kind="ExternalInput")
    RW = nc.dram_tensor("rw", [128, M * 8], F32, kind="ExternalInput")
    WGU = nc.dram_tensor("wgu", [NE, CF, 128, 2 * D], F32, kind="ExternalInput")
    WD = nc.dram_tensor("wd", [NE, M, 128, F], F32, kind="ExternalInput")
    OUT = nc.dram_tensor("outT", [D, T], F32, kind="ExternalOutput")

    k_in = nc.dram_tensor("k_in", [D * T // 1024, 1024], BF16, kind="Internal")
    v_in = nc.dram_tensor("v_in", [D * T // 1024, 1024], BF16, kind="Internal")
    ag_space = "Shared" if R > 4 else "Local"
    ag_k = nc.dram_tensor("ag_k", [R * D * T // 1024, 1024], BF16,
                          kind="Internal", addr_space=ag_space)
    ag_v = nc.dram_tensor("ag_v", [R * D * T // 1024, 1024], BF16,
                          kind="Internal", addr_space=ag_space)
    bc_dram = nc.dram_tensor("bc_dram", [4, T], F32, kind="Internal")
    w_dram = nc.dram_tensor("w_dram", [8, T], BF16, kind="Internal")

    cvals = [0.0, 1.0, -1.0, 0.5, 1e-5, 1e-6, EXP_SCALE, EXP_SHIFT]
    CPOOL_D = nc.inline_tensor(np.array([cvals], np.float32), name="cpool_d")
    ONESC_D = nc.inline_tensor(np.ones((128, 1), ml_dtypes.bfloat16), name="onesc_d")
    ONESF_D = nc.inline_tensor(np.ones((128, 1), np.float32), name="onesf_d")
    IDF_D = nc.inline_tensor(np.eye(128, dtype=np.float32), name="idf_d")
    IDB_D = nc.inline_tensor(np.eye(128, dtype=ml_dtypes.bfloat16), name="idb_d")

    with ExitStack() as ctx:
        sbt = lambda n, s, d: ctx.enter_context(nc.sbuf_tensor(n, s, d))
        pst = lambda n, s, d: ctx.enter_context(nc.psum_tensor(n, s, d))

        xT = sbt("xT", [128, M * T], F32)
        hT = sbt("hT", [128, M * T], BF16)
        qT = sbt("qT", [128, H * T], BF16)
        lnscr = sbt("lnscr", [128, T], F32)
        sq = sbt("sq", [128, 2 * T], BF16)
        cpool = sbt("cpool", [128, len(cvals)], F32)
        onescol = sbt("onescol", [128, 1], BF16)
        onescolf = sbt("onescolf", [128, 1], F32)
        identf = sbt("identf", [128, 128], F32)
        identb = sbt("identb", [128, 128], BF16)
        bcols = sbt("bcols_sb", [128, 5 * M], F32)
        rw_sb = sbt("rw_sb", [128, M * 8], BF16)
        stats = sbt("stats", [1, 4 * T], F32)
        e_sb = sbt("e_sb", [128, 8], F32)
        msk = sbt("msk", [128, 8], F32)
        m1e = sbt("m1e", [128, 4], F32)
        w_sb = sbt("w_sb", [128, 8], F32)
        wT_bf = sbt("wT_bf", [8, T], BF16)
        WSLOT = 2 * D
        wring = sbt("wring", [128, 2 * WSLOT], BF16)
        AR = max(8 * T,
                 2 * CF * T + 6 * T + 6 * F,
                 36 * T)
        arena = sbt("arena", [128, AR], mybir.dt.uint8)
        ar_f32 = arena.ap().bitcast(F32)
        ar_bf = arena.ap().bitcast(BF16)
        nb2 = AR // 2
        nf4 = AR // 4
        A2bc = ar_f32[:, 0:T]
        B2bc = ar_f32[:, T:2 * T]
        hu_v = ar_bf[:, 0:CF * T]
        wbc_v = ar_bf[:, CF * T:CF * T + 3 * T]
        wdr_v = ar_bf[:, CF * T + 3 * T:CF * T + 3 * T + 3 * F]
        kring_v = ar_bf[:, 0:2 * T]
        vring_v = ar_bf[:, 2 * T:4 * T]
        expb_v = ar_bf[:, 4 * T:8 * T]
        evr_v = ar_bf[:, 8 * T:12 * T]
        recbc_v = ar_f32[:, 6 * T:8 * T]
        acc_v = ar_f32[:, 8 * T:9 * T]

        PA = pst("PA", [128, T], F32)
        PB = pst("PB", [128, T], F32)
        PC = pst("PC", [128, T], F32)
        PD = pst("PD", [128, T], F32)
        PS = {"PA": PA, "PB": PB, "PC": PC, "PD": PD}

        for i, v in enumerate(cvals):
            nc.const_aps.register(F32, v, cpool[:, i:i + 1])

        xT_c = lambda c: xT[:, c * T:(c + 1) * T]
        hT_c = lambda c: hT[:, c * T:(c + 1) * T]
        qT_c = lambda c: qT[:, c * T:(c + 1) * T]
        sq_s = lambda s: sq[:, s * T:(s + 1) * T]
        wring_s = lambda s, n=None: wring[:, s * WSLOT:s * WSLOT + (n or WSLOT)]
        wdr_s = lambda s: wdr_v[:, s * F:(s + 1) * F]
        hu_c = lambda c: hu_v[:, c * T:(c + 1) * T]
        wbc_s = lambda s: wbc_v[:, s * T:(s + 1) * T]
        kring_s = lambda s: kring_v[:, s * T:(s + 1) * T]
        vring_s = lambda s: vring_v[:, s * T:(s + 1) * T]
        expb_s = lambda s: expb_v[:, s * T:(s + 1) * T]
        evr_s = lambda s: evr_v[:, s * T:(s + 1) * T]
        recbc_s = lambda s: recbc_v[:, s * T:(s + 1) * T]
        rrow_s = lambda s: stats[0:1, s * T:(s + 1) * T]
        lnscr_bf = lnscr.ap().bitcast(BF16)[:, 0:T]
        LNK = ["A2bc", "B2bc"]
        FFNK = ([f"hu{c}" for c in range(CF)] + ["wbc0", "wbc1", "wbc2"]
                + [f"wdr{i}" for i in range(3)])
        ATTK = (["kring0", "kring1", "vring0", "vring1"]
                + [f"exp{i}" for i in range(4)] + [f"evr{i}" for i in range(4)]
                + ["recbc0", "recbc1", "accL", "accR"])

        bq_c = lambda c: bcols[:, 0 * M + c:0 * M + c + 1]
        bk_c = lambda c: bcols[:, 1 * M + c:1 * M + c + 1]
        bv_c = lambda c: bcols[:, 2 * M + c:2 * M + c + 1]
        bo_c = lambda c: bcols[:, 3 * M + c:3 * M + c + 1]
        rms_c = lambda c: bcols[:, 4 * M + c:4 * M + c + 1]

        kf = k_in.ap().rearrange("a b -> (a b)")
        vf = v_in.ap().rearrange("a b -> (a b)")
        agkf = ag_k.ap().rearrange("a b -> (a b)")
        agvf = ag_v.ap().rearrange("a b -> (a b)")

        def k_stage_ap(h):
            return kf[h * 128 * T:(h * 128 + 128) * T].rearrange(
                "(p k) -> p k", k=T)

        def v_stage_ap(h):
            vv = vf.rearrange("(k d) -> k d", d=D)
            return vv.rearrange("(s p) d -> p s d", p=128)[:, :, h * 128:(h + 1) * 128]

        def k_read_ap(r, h):
            base = r * D * T
            return agkf[base + h * 128 * T:base + (h * 128 + 128) * T].rearrange(
                "(p k) -> p k", k=T)

        def v_read_ap(r, h):
            base = r * D * T
            vv = agvf[base:base + D * T].rearrange("(k d) -> k d", d=D)
            return vv.rearrange("(c p) d -> p c d", p=128)[:, :, h * 128:(h + 1) * 128]

        m_cp = b.dma("sync", cpool[:, :],
                     CPOOL_D[:, :].to_broadcast([128, len(cvals)]), sem="s_pre")
        m_oc = b.dma("sync", onescol[:, :], ONESC_D[:, :], sem="s_pre")
        b.dma("sync", onescolf[:, :], ONESF_D[:, :], sem="s_pre")
        m_idf = b.dma("sync", identf[:, :], IDF_D[:, :], sem="s_pre")
        m_idb = b.dma("sync", identb[:, :], IDB_D[:, :], sem="s_pre")
        m_bcl = b.dma("sync", bcols[:, :], BCOLS[:, :], sem="s_pre")
        m_xt = b.dma("sync", xT[:, :], XT.ap().rearrange("(m p) t -> p m t", p=128), sem="s_pre")
        m_rw = b.dma("gpsimd", rw_sb[:, :], RW[:, :], sem="s_pre2")
        m_all = ("s_pre", b.planned["s_pre"])
        m_cp = m_oc = m_idf = m_idb = m_bcl = m_xt = m_all
        for c in range(M):
            b.produced(f"xT{c}", m_xt)
        b.produced("cpool", m_cp)

        def strow(i):
            return stats[0:1, i * T:(i + 1) * T]

        def layernorm(with_rms):
            cast_m = {}
            sum_m = suma_m = None
            for c in range(M):
                cm = b.dve(lambda v, c=c: v.tensor_copy(hT_c(c), xT_c(c)),
                           waits=b.r(f"xT{c}") + b.w(f"hT{c}"))
                b.consumed(f"xT{c}", cm)
                cast_m[c] = cm
                sl = c % 2
                sm = b.act(lambda s, c=c, sl=sl: s.activation(
                    sq_s(sl), xT_c(c), AF.Square),
                    waits=b.r(f"xT{c}") + b.w(f"sq{sl}") + b.r("cpool"))
                b.consumed(f"xT{c}", sm)
                b.produced(f"sq{sl}", sm)
                for bi, (c0, nb) in enumerate(NBLK):
                    last = (c == M - 1) and (bi == len(NBLK) - 1)
                    wa = [cm, m_oc] + (b.w("PA") if c == 0 and bi == 0 else [])
                    ma = b.mm(PA[0:1, c0:c0 + nb], onescol[:, :],
                              hT_c(c)[:, c0:c0 + nb],
                              start=(c == 0), stop=(c == M - 1), waits=wa,
                              inc=last)
                    b.consumed(f"hT{c}", ma)
                    wb = [sm, m_oc] + (b.w("PB") if c == 0 and bi == 0 else [])
                    mb = b.mm(PB[0:1, c0:c0 + nb], onescol[:, :],
                              sq_s(sl)[:, c0:c0 + nb],
                              start=(c == 0), stop=(c == M - 1), waits=wb,
                              inc=last)
                    b.consumed(f"sq{sl}", mb)
                    if last:
                        sum_m, suma_m = mb, ma
            b.produced("PA", suma_m)
            b.produced("PB", sum_m)
            r0, r1, r2, r3 = strow(0), strow(1), strow(2), strow(3)
            m_mu = b.dve(lambda v: v.tensor_scalar_mul(r0, PA[0:1, :], 1.0 / D),
                         waits=[suma_m, sum_m] + b.w("st0"))
            b.produced("st0", m_mu)
            b.consumed("PA", m_mu)
            m_e2 = b.dve(lambda v: v.tensor_scalar_mul(r1, PB[0:1, :], 1.0 / D),
                         waits=[sum_m] + b.w("st1"))
            b.produced("st1", m_e2)
            b.consumed("PB", m_e2)
            m_m2 = b.dve(lambda v: v.tensor_mul(r2, r0, r0),
                         waits=[m_mu] + b.w("st2"))
            b.produced("st2", m_m2)
            b.consumed("st0", m_m2)
            m_v = b.dve(lambda v: v.tensor_sub(r1, r1, r2),
                        waits=[m_e2, m_m2] + b.w("st1"))
            b.produced("st1", m_v)
            b.consumed("st2", m_v)
            m_sd = b.act(lambda sc: sc.activation(r2, r1, AF.Sqrt, bias=1e-5,
                                                  scale=1.0),
                         waits=[m_v] + b.r("cpool") + b.w("st2"))
            b.produced("st2", m_sd)
            b.consumed("st1", m_sd)
            m_rst = b.dve(lambda v: v.reciprocal(r2, r2),
                          waits=[m_sd] + b.w("st2"))
            b.produced("st2", m_rst)
            if with_rms:
                m_q = b.dve(lambda v: v.tensor_mul(r3, r2, r2),
                            waits=[m_rst] + b.w("st3"))
                b.produced("st3", m_q)
                b.consumed("st2", m_q)
                m_vr = b.dve(lambda v: v.tensor_mul(r3, r1, r3),
                             waits=[m_q, m_v] + b.w("st3"))
                b.produced("st3", m_vr)
                b.consumed("st1", m_vr)
                m_sr = b.act(lambda sc: sc.activation(r3, r3, AF.Sqrt, bias=1e-6,
                                                      scale=1.0),
                             waits=[m_vr] + b.r("cpool") + b.w("st3"))
                b.produced("st3", m_sr)
                m_rmr = b.dve(lambda v: v.reciprocal(r3, r3),
                              waits=[m_sr] + b.w("st3"))
                b.produced("st3", m_rmr)
                m_a2f = b.dve(lambda v: v.tensor_mul(r3, r2, r3),
                              waits=[m_rmr, m_rst] + b.w("st3"))
                b.produced("st3", m_a2f)
                b.consumed("st2", m_a2f)
                a2row, k_a2row = r3, "st3"
            else:
                a2row, k_a2row, m_a2f = r2, "st2", m_rst
            m_b2 = b.dve(lambda v: v.scalar_tensor_tensor(
                r0, r0, -1.0, a2row, ALU.mult, ALU.mult),
                waits=[m_mu, m_a2f] + b.w("st0"))
            b.produced("st0", m_b2)
            b.consumed(k_a2row, m_b2)
            b2, k_b2 = r0, "st0"
            m_oA = b.dma("sync", bc_dram[0:1, :], a2row,
                         waits=[m_a2f, m_b2] + b.w("bcd0"), sem="s_bcA")
            b.produced("bcd0", m_oA)
            b.consumed(k_a2row, m_oA)
            if not with_rms:
                b.consumed("st2", m_oA)
            m_oB = b.dma("sync", bc_dram[1:2, :], b2, waits=[m_b2] + b.w("bcd1"), sem="s_bcB")
            b.produced("bcd1", m_oB)
            b.consumed(k_b2, m_oB)
            m_iA = b.dma("sync", A2bc[:, :], bc_dram[0:1, :].to_broadcast([128, T]),
                         waits=[m_oA] + b.w("A2bc"), sem="s_bcA")
            b.produced("A2bc", m_iA)
            b.consumed("bcd0", m_iA)
            m_iB = b.dma("sync", B2bc[:, :], bc_dram[1:2, :].to_broadcast([128, T]),
                         waits=[m_oB] + b.w("B2bc"), sem="s_bcB")
            b.produced("B2bc", m_iB)
            b.consumed("bcd1", m_iB)
            for c in range(M):
                wv_ = (b.r(f"xT{c}") + [m_iA, sum_m, suma_m, cast_m[c]]
                       + b.w("lnscr"))
                if with_rms:
                    m_t1 = b.dve(lambda v, c=c: v.scalar_tensor_tensor(
                        lnscr[:, :], xT_c(c), rms_c(c), A2bc[:, :],
                        ALU.mult, ALU.mult), waits=wv_ + [m_bcl])
                    m_h = b.dve(lambda v, c=c: v.scalar_tensor_tensor(
                        hT_c(c), B2bc[:, :], rms_c(c), lnscr[:, :],
                        ALU.mult, ALU.add),
                        waits=[m_t1, m_iB] + b.w(f"hT{c}"))
                else:
                    m_t1 = b.dve(lambda v, c=c: v.tensor_mul(
                        lnscr[:, :], xT_c(c), A2bc[:, :]), waits=wv_)
                    m_h = b.dve(lambda v, c=c: v.tensor_add(
                        hT_c(c), lnscr[:, :], B2bc[:, :]),
                        waits=[m_t1, m_iB] + b.w(f"hT{c}"))
                b.consumed(f"xT{c}", m_t1)
                b.produced("lnscr", m_t1)
                b.consumed("lnscr", m_h)
                b.produced(f"hT{c}", m_h)
                b.consumed("A2bc", m_t1)
                b.consumed("B2bc", m_h)

        def router():
            tr_ms = []
            mm1 = m1e[:, 0:1]
            mm2 = m1e[:, 1:2]
            mneg = m1e[:, 2:3]
            mrec = m1e[:, 3:4]
            for s in range(TS):
                lg = None
                for m in range(M):
                    wv_ = b.r(f"hT{m}") + [m_rw]
                    if m == 0:
                        wv_ += b.w("PC")
                    lg = b.mm(PC[:, 0:8], hT_c(m)[:, s * 128:(s + 1) * 128],
                              rw_sb[:, m * 8:(m + 1) * 8],
                              start=(m == 0), stop=(m == M - 1), waits=wv_,
                              inc=(m == M - 1))
                    b.consumed(f"hT{m}", lg)
                b.produced("PC", lg)
                m_max = b.dve(lambda v: v.reduce_max(mm1, PC[:, 0:8], axis=AXX),
                              waits=[lg] + b.w("m1e"))
                m_mn = b.dve(lambda v: v.tensor_scalar_mul(mneg, mm1, -1.0),
                             waits=[m_max])
                m_e = b.act(lambda sc: sc.activation(e_sb[:, :], PC[:, 0:8],
                                                     AF.Exp, bias=mneg, scale=1.0),
                            waits=[m_mn, lg] + b.w("e_sb") + b.r("cpool"))
                b.consumed("PC", m_e)
                b.produced("e_sb", m_e)
                m_em = b.dve(lambda v: v.reduce_max(mm1, e_sb[:, :], axis=AXX),
                             waits=[m_e, m_mn])
                b.consumed("e_sb", m_em)
                m_k1 = b.dve(lambda v: v.tensor_scalar(
                    msk[:, :], e_sb[:, :], mm1, None, ALU.is_ge),
                    waits=[m_em] + b.w("msk"))
                b.consumed("e_sb", m_k1)
                m_k1m = b.dve(lambda v: v.tensor_scalar_mul(msk[:, :], msk[:, :],
                                                            mm1), waits=[m_k1])
                m_e2 = b.dve(lambda v: v.tensor_sub(msk[:, :], e_sb[:, :],
                                                    msk[:, :]), waits=[m_k1m])
                b.consumed("e_sb", m_e2)
                m_m2 = b.dve(lambda v: v.reduce_max(mm2, msk[:, :], axis=AXX),
                             waits=[m_e2])
                m_kp = b.dve(lambda v: v.tensor_scalar(
                    msk[:, :], e_sb[:, :], mm2, None, ALU.is_ge), waits=[m_m2])
                b.consumed("e_sb", m_kp)
                m_wr = b.dve(lambda v: v.tensor_mul(msk[:, :], e_sb[:, :],
                                                    msk[:, :]), waits=[m_kp])
                b.consumed("e_sb", m_wr)
                m_dn = b.dve(lambda v: v.tensor_add(mneg, mm1, mm2),
                             waits=[m_m2, m_em])
                m_rc = b.dve(lambda v: v.reciprocal(mrec, mneg), waits=[m_dn])
                m_rh = b.dve(lambda v: v.tensor_scalar_mul(mrec, mrec, 0.5),
                             waits=[m_rc])
                m_w = b.dve(lambda v: v.tensor_scalar_mul(w_sb[:, :], msk[:, :],
                                                          mrec),
                            waits=[m_rh, m_wr] + b.w("w_sb"))
                b.produced("w_sb", m_w)
                b.produced("m1e", m_w)
                b.produced("msk", m_w)
                wv_ = [m_w, m_idf]
                if s == 0:
                    wv_ += b.w("PD")
                first_in_bank = (s * 128) % 512 == 0
                last_in_bank = ((s + 1) * 128) % 512 == 0 or s == TS - 1
                tr = b.emit("tensor",
                            lambda t, s=s, a=first_in_bank, z=last_in_bank:
                                t.matmul(PD[0:8, s * 128:(s + 1) * 128],
                                         w_sb[:, :], identf[:, :],
                                         start=a, stop=z, is_transpose=True),
                            waits=wv_, inc=("s_pe", 1))
                b.consumed("w_sb", tr)
                tr_ms.append(tr)
            b.produced("PD", tr_ms[-1])
            m_ev = b.act(lambda sc: sc.activation(wT_bf[:, :], PD[0:8, 0:T],
                                                  AF.Identity, bias=0.0, scale=1.0),
                         waits=tr_ms + b.w("wT_bf") + b.r("cpool"))
            b.consumed("PD", m_ev)
            b.produced("wT_bf", m_ev)
            m_st = b.dma("sync", w_dram[0:8, :], wT_bf[:, :],
                         waits=[m_ev] + b.w("w_dram"), sem="s_wdram")
            b.produced("w_dram", m_st)
            b.consumed("wT_bf", m_st)
            return m_st

        wslab_i = [0]
        pslab_i = [0]
        wd_i = [0]

        def ffn(tag, final_store=False, hoisted=None):
            m_route = router()
            m_w8 = b.emit("gpsimd", lambda g: g.memset(wbc_s(2), 0.5),
                          waits=b.w("wbc2"), inc=("s_gpe", 1))
            b.produced("wbc2", m_w8)
            for e in range(NE):
                if e < E:
                    ws = e % 2
                    wkey = f"wbc{ws}"
                    m_bc = b.dma("scalar", wbc_s(ws),
                                 w_dram[e:e + 1, :].to_broadcast([128, T]),
                                 waits=[m_route] + b.w(wkey), sem=f"s_wbc{ws}")
                    b.produced(wkey, m_bc)
                    wbc_ap = wbc_s(ws)
                else:
                    wkey = "wbc2"
                    wbc_ap = wbc_s(2)
                for c in range(CF):
                    sl = wslab_i[0] % 2
                    wslab_i[0] += 1
                    if hoisted is not None and e == 0 and c < len(hoisted):
                        m_sl = hoisted[c]
                    else:
                        m_sl = b.dma("gpsimd", wring_s(sl),
                                     WGU[e, c, :, :], waits=b.w(f"wring{sl}"),
                                     sem=f"s_wr{sl}")
                    b.produced(f"wring{sl}", m_sl)
                    lg = lu = None
                    for m in range(M):
                        for bi, (c0, nb) in enumerate(NBLK):
                            wv_ = [m_sl] + b.r(f"hT{m}")
                            if m == 0 and bi == 0:
                                wv_ += b.w("PA")
                            lg = b.mm(PA[:, c0:c0 + nb],
                                      wring_s(sl)[:, m * 128:(m + 1) * 128],
                                      hT_c(m)[:, c0:c0 + nb],
                                      start=(m == 0), stop=(m == M - 1),
                                      waits=wv_,
                                      inc=(m == M - 1 and bi == len(NBLK) - 1))
                            b.consumed(f"hT{m}", lg)
                            b.consumed(f"wring{sl}", lg)
                    for m in range(M):
                        for bi, (c0, nb) in enumerate(NBLK):
                            wv_ = [m_sl] + b.r(f"hT{m}")
                            if m == 0 and bi == 0:
                                wv_ += b.w("PB")
                            lu = b.mm(PB[:, c0:c0 + nb],
                                      wring_s(sl)[:, D + m * 128:D + (m + 1) * 128],
                                      hT_c(m)[:, c0:c0 + nb],
                                      start=(m == 0), stop=(m == M - 1),
                                      waits=wv_,
                                      inc=(m == M - 1 and bi == len(NBLK) - 1))
                            b.consumed(f"hT{m}", lu)
                            b.consumed(f"wring{sl}", lu)
                    b.produced("PA", lg)
                    b.produced("PB", lu)
                    ssl = c % 2
                    m_si = b.act(lambda s, sl2=ssl: s.activation(
                        sq_s(sl2), PA[:, :], AF.Silu),
                        waits=[lg] + b.w(f"sq{ssl}") + b.r("cpool"))
                    b.consumed("PA", m_si)
                    b.produced(f"sq{ssl}", m_si)
                    m_t = b.dve(lambda v, sl2=ssl, c=c: v.tensor_mul(
                        hu_c(c), sq_s(sl2), PB[:, :]),
                        waits=[m_si, lu] + b.w(f"hu{c}"))
                    b.consumed("PB", m_t)
                    b.consumed(f"sq{ssl}", m_t)
                    b.produced(f"hu{c}", m_t)
                    m_hu = b.dve(lambda v, c=c, wb=wbc_ap: v.tensor_mul(
                        hu_c(c), hu_c(c), wb),
                        waits=[m_t] + b.r(wkey) + b.w(f"hu{c}"))
                    b.consumed(wkey, m_hu)
                    b.produced(f"hu{c}", m_hu)
                for j in range(M):
                    sl = wd_i[0] % 3
                    wd_i[0] += 1
                    m_sl = b.dma("gpsimd", wdr_s(sl), WD[e, j, :, :],
                                 waits=b.w(f"wdr{sl}"), sem=f"s_wd{sl}")
                    b.produced(f"wdr{sl}", m_sl)
                    pk = "PC" if j % 2 == 0 else "PD"
                    P = PS[pk]
                    dn = None
                    for c in range(CF):
                        for bi, (c0, nb) in enumerate(NBLK):
                            wv_ = [m_sl] + b.r(f"hu{c}")
                            if c == 0 and bi == 0:
                                wv_ += b.w(pk)
                            dn = b.mm(P[:, c0:c0 + nb],
                                      wdr_s(sl)[:, c * 128:(c + 1) * 128],
                                      hu_c(c)[:, c0:c0 + nb],
                                      start=(c == 0), stop=(c == CF - 1),
                                      waits=wv_,
                                      inc=(c == CF - 1 and bi == len(NBLK) - 1))
                            b.consumed(f"hu{c}", dn)
                            b.consumed(f"wdr{sl}", dn)
                    b.produced(pk, dn)
                    m_x = b.dve(lambda v, j=j, P=P: v.tensor_add(
                        xT_c(j), xT_c(j), P[:, :]),
                        waits=[dn] + b.w(f"xT{j}"))
                    b.consumed(pk, m_x)
                    b.produced(f"xT{j}", m_x)
                    if final_store and e == NE - 1:
                        mo = b.dma("sync",
                                   OUT.ap().rearrange("(m p) t -> p m t", p=128)
                                   [:, j:j + 1, :],
                                   xT_c(j), waits=[m_x], sem="s_out")
                        b.consumed(f"xT{j}", mo)
                        out_stores.append(mo)

        out_stores = []

        def pr_s(sl):
            return wring[:, sl * D:(sl + 1) * D]

        def proj_feature_major(Wdram, bias_c, out_key_fn, out_ap_fn):
            for j in range(M):
                sl = pslab_i[0] % 4
                pslab_i[0] += 1
                m_sl = b.dma("gpsimd", pr_s(sl), Wdram[j, :, :],
                             waits=b.w(f"pr{sl}"), sem=f"s_pr{sl}")
                b.produced(f"pr{sl}", m_sl)
                pk = "PA" if j % 2 == 0 else "PB"
                P = PS[pk]
                pm = None
                for m in range(M):
                    for bi, (c0, nb) in enumerate(NBLK):
                        wv_ = [m_sl] + b.r(f"hT{m}")
                        if m == 0 and bi == 0:
                            wv_ += b.w(pk)
                        pm = b.mm(P[:, c0:c0 + nb],
                                  pr_s(sl)[:, m * 128:(m + 1) * 128],
                                  hT_c(m)[:, c0:c0 + nb],
                                  start=(m == 0), stop=(m == M - 1), waits=wv_,
                                  inc=(m == M - 1 and bi == len(NBLK) - 1))
                        b.consumed(f"hT{m}", pm)
                        b.consumed(f"pr{sl}", pm)
                b.produced(pk, pm)
                okey = out_key_fn(j)
                m_ev = b.act(lambda s, j=j, P=P, oap=out_ap_fn(j), bap=bias_c(j):
                             s.activation(oap, P[:, :], AF.Identity, bias=bap,
                                          scale=1.0),
                             waits=[pm, m_bcl] + b.w(okey) + b.r("cpool"))
                b.consumed(pk, m_ev)
                b.produced(okey, m_ev)
                yield j, okey, m_ev

        def qkv_and_ag():
            k_writes, v_writes = [], []
            for j, okey, m_ev in proj_feature_major(
                    WK, bk_c, lambda j: f"evr{j % 2}", lambda j: evr_s(j % 2)):
                m_o = b.dma("sync", k_stage_ap(j), evr_s(j % 2), waits=[m_ev],
                             sem=f"s_kv{j % 2}")
                b.consumed(okey, m_o)
                k_writes.append(m_o)
            m_ccK = b.emit("gpsimd",
                           lambda g: g.collective_compute(
                               "AllGather", ALU.bypass,
                               replica_groups=[list(range(R))],
                               ins=[k_in.ap().opt()],
                               outs=[ag_k.ap().opt()]),
                           waits=k_writes, inc=("s_cc", 1))
            PDb = PD.ap().bitcast(BF16)
            for j, okey, m_ev in proj_feature_major(
                    WV, bv_c, lambda j: f"evr{j % 2}", lambda j: evr_s(j % 2)):
                trs = []
                for s in range(TS):
                    wv_ = [m_ev, m_idb]
                    if s == 0:
                        wv_ += b.w("PD")
                    tr = b.emit("tensor",
                                lambda t, s=s, j=j: t.matmul(
                                    PDb[:, s * 128:(s + 1) * 128],
                                    evr_s(j % 2)[:, s * 128:(s + 1) * 128],
                                    identb[:, :],
                                    start=(s == 0), stop=(s == TS - 1),
                                    is_transpose=True),
                                waits=wv_, inc=("s_pe", 1))
                    b.consumed(okey, tr)
                    trs.append(tr)
                b.produced("PD", trs[-1])
                vk = 2 + (j % 2)
                m_vv = b.act(lambda s, vk=vk: s.activation(
                    evr_s(vk), PDb[:, 0:T], AF.Identity, bias=0.0, scale=1.0),
                    waits=trs + b.w(f"evr{vk}") + b.r("cpool"))
                b.consumed("PD", m_vv)
                b.produced(f"evr{vk}", m_vv)
                m_o = b.dma("sync", v_stage_ap(j),
                            evr_s(vk).rearrange("p (s d) -> p s d", d=128),
                            waits=[m_vv], sem=f"s_kv{vk}")
                b.consumed(f"evr{vk}", m_o)
                v_writes.append(m_o)
            m_ccV = b.emit("gpsimd",
                           lambda g: g.collective_compute(
                               "AllGather", ALU.bypass,
                               replica_groups=[list(range(R))],
                               ins=[v_in.ap().opt()],
                               outs=[ag_v.ap().opt()]),
                           waits=v_writes, inc=("s_cc2", 1))
            for _ in proj_feature_major(WQ, bq_c, lambda j: f"qT{j}",
                                        lambda j: qT_c(j)):
                pass
            return m_ccK, m_ccV

        def attention(m_ccK, m_ccV):
            NKC = R * TS

            def kv_dma(h, r):
                ks = r % 2
                m_k = b.dma("sync", kring_s(ks), k_read_ap(r, h),
                            waits=[m_ccK] + b.w(f"kring{ks}"), sem=f"s_kr{ks}")
                b.produced(f"kring{ks}", m_k)
                m_v = b.dma("sync",
                            vring_s(ks).rearrange("p (c d) -> p c d", d=128),
                            v_read_ap(r, h), waits=[m_ccV] + b.w(f"vring{ks}"),
                            sem=f"s_vr{ks}")
                b.produced(f"vring{ks}", m_v)
                return ks, m_k, m_v

            exp_i = [0]
            pend_tail = [None]

            def emit_div(h):
                rs = h % 2
                m_ri = b.rdy[f"recbc{rs}"]
                m_at = b.rdy[f"qT{h}"]
                m_dv = b.dve(lambda v, h=h, rs=rs: v.tensor_mul(
                    qT_c(h), qT_c(h), recbc_s(rs)),
                    waits=[m_ri, m_at])
                b.consumed(f"recbc{rs}", m_dv)
                b.produced(f"qT{h}", m_dv)

            def emit_tail():
                if pend_tail[0] is None:
                    return
                h, avs = pend_tail[0]
                pend_tail[0] = None
                m_aL = b.rdy["accL"]
                sms = None
                for bi, (c0, nb) in enumerate(NBLK):
                    wv_ = [m_aL, m_oc] + (b.w("PD") if bi == 0 else [])
                    sms = b.mm(PD[0:1, c0:c0 + nb], onescolf[:, :],
                               acc_v[:, c0:c0 + nb],
                               start=True, stop=True, waits=wv_,
                               inc=(bi == len(NBLK) - 1))
                    b.consumed("accL", sms)
                b.produced("PD", sms)
                m_at = b.act(lambda sc, h=h: sc.activation(
                    qT_c(h), PC[:, :], AF.Identity, bias=0.0, scale=1.0),
                    waits=[avs] + b.w(f"qT{h}") + b.r("cpool"))
                b.consumed("PC", m_at)
                b.produced(f"qT{h}", m_at)
                rs = h % 2
                m_rc = b.dve(lambda v, rs=rs: v.reciprocal(rrow_s(rs),
                                                           PD[0:1, :]),
                             waits=[sms] + b.w(f"st{rs}"))
                b.consumed("PD", m_rc)
                b.produced(f"st{rs}", m_rc)
                m_ro = b.dma("gpsimd", bc_dram[2:3, :], rrow_s(rs),
                             waits=[m_rc] + b.w("bcd2"), sem=f"s_rec{rs}")
                b.produced("bcd2", m_ro)
                b.consumed(f"st{rs}", m_ro)
                m_ri = b.dma("gpsimd", recbc_s(rs),
                             bc_dram[2:3, :].to_broadcast([128, T]),
                             waits=[m_ro] + b.w(f"recbc{rs}"), sem=f"s_rec{rs}")
                b.produced(f"recbc{rs}", m_ri)
                b.consumed("bcd2", m_ri)

            for h in range(H):
                pend = {}
                pend[0] = kv_dma(h, 0)
                if R > 1:
                    pend[1] = kv_dma(h, 1)
                avs = None
                av_q = []
                for r in range(R):
                    ks, m_k, m_v = pend.pop(r)
                    for c in range(TS):
                        kc = r * TS + c
                        spk = "PA" if kc % 2 == 0 else "PB"
                        SP = PS[spk]
                        sc_m = None
                        for bi, (c0, nb) in enumerate(NBLK):
                            wv_ = [m_k] + b.r(f"qT{h}") + (b.w(spk) if bi == 0
                                                           else [])
                            sc_m = b.mm(SP[:, c0:c0 + nb],
                                        kring_s(ks)[:, c * 128:(c + 1) * 128],
                                        qT_c(h)[:, c0:c0 + nb],
                                        start=True, stop=True, waits=wv_,
                                        inc=(bi == len(NBLK) - 1))
                            b.consumed(f"qT{h}", sc_m)
                            b.consumed(f"kring{ks}", sc_m)
                        b.produced(spk, sc_m)
                        es = exp_i[0] % 4
                        exp_i[0] += 1
                        m_x = b.act(lambda sc, es=es, SP=SP: sc.activation(
                            expb_s(es), SP[:, :], AF.Exp, bias=EXP_SHIFT,
                            scale=EXP_SCALE),
                            waits=[sc_m] + b.w(f"exp{es}") + b.r("cpool"))
                        b.consumed(spk, m_x)
                        b.produced(f"exp{es}", m_x)
                        if kc == 0:
                            emit_tail()
                        if kc == 0:
                            m_aL = b.dve(lambda v, es=es: v.tensor_copy(
                                acc_v, expb_s(es)),
                                waits=[m_x] + b.w("accL"))
                        else:
                            m_aL = b.dve(lambda v, es=es: v.tensor_add(
                                acc_v, acc_v, expb_s(es)),
                                waits=[m_x] + b.w("accL"))
                        b.produced("accL", m_aL)
                        b.consumed(f"exp{es}", m_aL)
                        av_q.append((es, ks, kc, m_v))
                        if len(av_q) > 2:
                            pes, pks, pkc, pm_v = av_q.pop(0)
                            pc = pkc % TS
                            m_px = b.rdy[f"exp{pes}"]
                            for bi, (c0, nb) in enumerate(NBLK):
                                wv_ = [m_px, pm_v] + (b.w("PC")
                                                      if pkc == 0 and bi == 0
                                                      else [])
                                avs = b.mm(PC[:, c0:c0 + nb],
                                           vring_s(pks)[:, pc * 128:(pc + 1) * 128],
                                           expb_s(pes)[:, c0:c0 + nb],
                                           start=(pkc == 0), stop=False,
                                           waits=wv_,
                                           inc=(bi == len(NBLK) - 1))
                                b.consumed(f"exp{pes}", avs)
                                b.consumed(f"vring{pks}", avs)
                        if h > 0 and r == 0 and c == min(4, TS - 1):
                            emit_div(h - 1)
                    if r + 2 < R:
                        pend[r + 2] = kv_dma(h, r + 2)
                while av_q:
                    pes, pks, pkc, pm_v = av_q.pop(0)
                    pc = pkc % TS
                    m_px = b.rdy[f"exp{pes}"]
                    for bi, (c0, nb) in enumerate(NBLK):
                        wv_ = [m_px, pm_v] + (b.w("PC") if pkc == 0 and bi == 0
                                              else [])
                        avs = b.mm(PC[:, c0:c0 + nb],
                                   vring_s(pks)[:, pc * 128:(pc + 1) * 128],
                                   expb_s(pes)[:, c0:c0 + nb],
                                   start=(pkc == 0), stop=(pkc == NKC - 1),
                                   waits=wv_,
                                   inc=(bi == len(NBLK) - 1))
                        b.consumed(f"exp{pes}", avs)
                        b.consumed(f"vring{pks}", avs)
                b.produced("PC", avs)
                pend_tail[0] = (h, avs)
            emit_tail()
            emit_div(H - 1)
        def outproj():
            for j in range(M):
                sl = pslab_i[0] % 4
                pslab_i[0] += 1
                m_sl = b.dma("gpsimd", pr_s(sl), WO[j, :, :],
                             waits=b.w(f"pr{sl}"), sem=f"s_pr{sl}")
                b.produced(f"pr{sl}", m_sl)
                pk = "PA" if j % 2 == 0 else "PB"
                P = PS[pk]
                pm = None
                for m in range(M):
                    for bi, (c0, nb) in enumerate(NBLK):
                        wv_ = [m_sl] + b.r(f"qT{m}")
                        if m == 0 and bi == 0:
                            wv_ += b.w(pk)
                        pm = b.mm(P[:, c0:c0 + nb],
                                  pr_s(sl)[:, m * 128:(m + 1) * 128],
                                  qT_c(m)[:, c0:c0 + nb],
                                  start=(m == 0), stop=(m == M - 1), waits=wv_,
                                  inc=(m == M - 1 and bi == len(NBLK) - 1))
                        b.consumed(f"qT{m}", pm)
                        b.consumed(f"pr{sl}", pm)
                b.produced(pk, pm)
                m_x = b.dve(lambda v, j=j, P=P: v.scalar_tensor_tensor(
                    xT_c(j), P[:, :], bo_c(j), xT_c(j), ALU.add, ALU.add),
                    waits=[pm, m_bcl] + b.w(f"xT{j}"))
                b.consumed(pk, m_x)
                b.produced(f"xT{j}", m_x)

        WRK = ["wring0", "wring1"]
        PRK = ["pr0", "pr1", "pr2", "pr3"]

        def hoist_wgu():
            ms = []
            for c in range(2):
                sl = wslab_i[0] % 2
                wslab_i[0] += 1
                ms.append(b.dma("gpsimd", wring_s(sl), WGU[0, c, :, :],
                                waits=b.w(f"wring{sl}"), sem=f"s_wr{sl}"))
            wslab_i[0] -= 2
            return ms

        h1 = hoist_wgu()
        layernorm(with_rms=True)
        b.fence(LNK, FFNK)
        ffn("f1", hoisted=h1)
        b.fence(FFNK, LNK)
        b.fence(WRK, PRK)
        layernorm(with_rms=False)
        b.fence(LNK, ATTK)
        m_ccK, m_ccV = qkv_and_ag()
        attention(m_ccK, m_ccV)
        outproj()
        b.fence(ATTK, LNK)
        b.fence(PRK, WRK)
        h2 = hoist_wgu()
        layernorm(with_rms=True)
        b.fence(LNK, FFNK)
        ffn("f2", final_store=True, hoisted=h2)
        b.emit("sync", lambda e: e.nop(), waits=[out_stores[-1]])

        block = ctx.enter_context(nc.Block())
        b.realize(block)
    return nc



def _host_layout(inputs, cfg: CFG):
    D, M, F, E, NE, CF = cfg.D, cfg.M, cfg.F, cfg.E, cfg.NE, cfg.CF
    x = np.asarray(inputs["L0_attn_hidden_states"], np.float32)[0]
    wq = np.asarray(inputs["L0_attn_wq"], np.float32)
    wk = np.asarray(inputs["L0_attn_wk"], np.float32)
    wv = np.asarray(inputs["L0_attn_wv"], np.float32)
    wo = np.asarray(inputs["L0_attn_wo"], np.float32)

    def wslabs(w):
        wt = np.ascontiguousarray(w.T)
        a = wt.reshape(M, 128, M, 128)
        return np.ascontiguousarray(a.transpose(2, 1, 0, 3)).reshape(M, 128, D)

    def bcol(v):
        return np.ascontiguousarray(np.asarray(v, np.float32).reshape(M, 128).T)

    bq = bcol(inputs["L0_attn_bq"]); bk = bcol(inputs["L0_attn_bk"])
    bv = bcol(inputs["L0_attn_bv"]); bo = bcol(inputs["L0_attn_bo"])
    rms = bcol(inputs["L0_ffn_rms_weight"])
    bcols = np.ascontiguousarray(np.concatenate([bq, bk, bv, bo, rms], axis=1))

    rw = np.asarray(inputs["L0_ffn_router_w"], np.float32)
    rw_s = np.ascontiguousarray(rw.reshape(M, 128, 8).transpose(1, 0, 2)
                                .reshape(128, M * 8))

    wg = np.asarray(inputs["L0_ffn_w_gate"], np.float32)
    wu = np.asarray(inputs["L0_ffn_w_up"], np.float32)
    wd = np.asarray(inputs["L0_ffn_w_down"], np.float32)
    swg = np.asarray(inputs["L0_ffn_sw_gate"], np.float32)
    swu = np.asarray(inputs["L0_ffn_sw_up"], np.float32)
    swd = np.asarray(inputs["L0_ffn_sw_down"], np.float32)

    def gu_slabs(g, u):
        def t(w):
            return w.reshape(M, 128, CF, 128).transpose(2, 1, 0, 3)
        out = np.stack([t(g), t(u)], axis=2)
        return out.reshape(CF, 128, 2 * D)

    wgu = np.empty((NE, CF, 128, 2 * D), np.float32)
    for e in range(E):
        wgu[e] = gu_slabs(wg[e], wu[e])
    wgu[E] = gu_slabs(swg, swu)
    wgu = np.ascontiguousarray(wgu)

    def d_slabs(w):
        return (w.reshape(CF, 128, M, 128).transpose(2, 1, 0, 3)
                .reshape(M, 128, F))

    wds = np.empty((NE, M, 128, F), np.float32)
    for e in range(E):
        wds[e] = d_slabs(wd[e])
    wds[E] = d_slabs(swd)
    wds = np.ascontiguousarray(wds)

    shared = dict(wq=wslabs(wq), wk=wslabs(wk), wv=wslabs(wv), wo=wslabs(wo),
                  bcols=bcols, rw=rw_s, wgu=wgu, wd=wds)
    in_maps = []
    for r in range(cfg.R):
        m = dict(shared)
        m["xt"] = np.ascontiguousarray(x[r * cfg.T:(r + 1) * cfg.T, :].T)
        in_maps.append(m)
    return in_maps


def kernel(**inputs):
    cfg = CFG()
    nc = build(cfg)
    in_maps = _host_layout(inputs, cfg)
    res = run_bass_kernel_spmd(nc, in_maps, core_ids=list(range(cfg.R)))
    out = np.empty((1, cfg.S, cfg.D), np.float32)
    for r in range(cfg.R):
        out[0, r * cfg.T:(r + 1) * cfg.T, :] = res.results[r]["outT"].T
    return out


# version 11
# speedup vs baseline: 1.0079x; 1.0070x over previous
import numpy as np
import ml_dtypes
from contextlib import ExitStack
import concourse.bass as bass
import concourse.mybir as mybir
import concourse.bass_utils as _bass_utils
from concourse.bass_utils import run_bass_kernel_spmd

if not getattr(_bass_utils, "_ldw_opt_patched", False):
    _orig_run_command = _bass_utils.run_command

    def _run_command_ldw(cmd, *a, **kw):
        if isinstance(cmd, list):
            cmd = ["--enable-ldw-opt=true" if c == "--enable-ldw-opt=false"
                   else c for c in cmd]
        return _orig_run_command(cmd, *a, **kw)

    _bass_utils.run_command = _run_command_ldw
    _bass_utils._ldw_opt_patched = True

F32 = mybir.dt.float32
BF16 = mybir.dt.bfloat16
AF = mybir.ActivationFunctionType
ALU = mybir.AluOpType
AXX = mybir.AxisListType.X

EXP_SCALE = float(np.float32(1.0 / np.sqrt(128.0)))
EXP_SHIFT = -10.0


class CFG:
    def __init__(self, R=8, T=1024, D=2048, F=1024, E=8):
        self.R, self.T, self.D, self.F, self.E = R, T, D, F, E
        self.M = D // 128
        self.H = D // 128
        self.CF = F // 128
        self.TS = T // 128
        self.NE = E + 1
        self.S = R * T
        assert T % 128 == 0 and D % 128 == 0 and F % 128 == 0

    def nblk(self):
        out, c = [], 0
        while c < self.T:
            n = min(512, self.T - c)
            out.append((c, n))
            c += n
        return out


class Builder:
    ENGINES = ("sync", "gpsimd", "tensor", "vector", "scalar")

    def __init__(self, cfg: CFG):
        self.cfg = cfg
        self.nc = bass.Bass("TRN2", target_bir_lowering=False)
        self.steps = []
        self.sems = {}
        self.planned = {}
        for s in ("s_sync", "s_gp", "s_gpe", "s_pe", "s_act", "s_dve", "s_cc",
                  "s_pre", "s_wr0", "s_wr1", "s_wr2", "s_wd0", "s_wd1", "s_wd2",
                  "s_kr0", "s_kr1", "s_vr0", "s_vr1", "s_kv0", "s_kv1", "s_kv2",
                  "s_kv3", "s_bcA", "s_bcB",
                  "s_rec0", "s_rec1", "s_wbc0", "s_wbc1", "s_wdram", "s_out", "s_cc2",
                  "s_pr0", "s_pr1", "s_pr2", "s_pr3", "s_x0", "s_x1",
                  "s_pre2"):
            self.sems[s] = self.nc.alloc_semaphore(s)
            self.planned[s] = 0
        self.rdy = {}
        self.frees = {}
        self.pe_pending = set()
        self.last_tensor_step = None

    def emit(self, eng, fn, waits=(), inc=None):
        assert eng in self.ENGINES
        w = [x for x in waits if x is not None]
        mi = None
        if inc is not None:
            self.planned[inc[0]] += inc[1]
            mi = (inc[0], self.planned[inc[0]])
        self.steps.append((eng, fn, tuple(w), inc))
        if eng == "tensor":
            self.last_tensor_step = len(self.steps) - 1
            if mi is not None and self.pe_pending:
                for key in self.pe_pending:
                    self.consumed(key, mi)
                self.pe_pending.clear()
        return mi

    def dma(self, eng, out_ap, in_ap, waits=(), sem=None):
        assert sem is not None
        return self.emit(eng, lambda e, o=out_ap, i=in_ap: e.dma_start(out=o, in_=i),
                         waits=waits, inc=(sem, 16))

    def mm(self, out_ap, lhsT, rhs, start, stop, waits=(), inc=False):
        return self.emit(
            "tensor",
            lambda t, o=out_ap, l=lhsT, r=rhs, a=start, b=stop:
                t.matmul(o, l, r, start=a, stop=b),
            waits=waits, inc=("s_pe", 1) if inc else None)

    def act(self, fn, waits=()):
        return self.emit("scalar", fn, waits=waits, inc=("s_act", 1))

    def dve(self, fn, waits=()):
        return self.emit("vector", fn, waits=waits, inc=("s_dve", 1))

    def gpe(self, fn, waits=()):
        return self.emit("gpsimd", fn, waits=waits, inc=("s_gpe", 1))

    def produced(self, key, m):
        self.rdy[key] = m
        self.frees[key] = {}

    def consumed(self, key, m):
        if m is None:
            self.pe_pending.add(key)
            return
        d = self.frees.setdefault(key, {})
        s, v = m
        if d.get(s, 0) < v:
            d[s] = v

    def r(self, key):
        m = self.rdy.get(key)
        return [m] if m is not None else []

    def _pe_flush(self):
        if not self.pe_pending:
            return
        i = self.last_tensor_step
        eng, fn, wts, inc = self.steps[i]
        assert eng == "tensor"
        if inc is None:
            self.planned["s_pe"] += 1
            inc = ("s_pe", 1)
            self.steps[i] = (eng, fn, wts, inc)
        mi = ("s_pe", self.planned["s_pe"])
        for key in self.pe_pending:
            self.consumed(key, mi)
        self.pe_pending.clear()

    def fence(self, from_keys, to_keys):
        acc = {}
        for k in from_keys:
            for (sn, v) in self.w(k):
                if acc.get(sn, 0) < v:
                    acc[sn] = v
        for k in to_keys:
            d = self.frees.setdefault(k, {})
            for sn, v in acc.items():
                if d.get(sn, 0) < v:
                    d[sn] = v

    def w(self, key):
        if key in self.pe_pending:
            self._pe_flush()
        out = list(self.frees.get(key, {}).items())
        m = self.rdy.get(key)
        if m is not None:
            out.append(m)
        return out

    def realize(self, block):
        sems, steps = self.sems, self.steps

        def mk(eng_name):
            def run(e):
                hwm = {}
                for eng, fn, waits, inc in steps:
                    if eng != eng_name:
                        continue
                    for (sn, val) in waits:
                        if hwm.get(sn, 0) < val:
                            e.wait_ge(sems[sn], val)
                            hwm[sn] = val
                    ins = fn(e)
                    if inc is not None:
                        ins.then_inc(sems[inc[0]], inc[1])
            return run

        block.sync(mk('sync'))
        block.gpsimd(mk('gpsimd'))
        block.tensor(mk('tensor'))
        block.vector(mk('vector'))
        block.scalar(mk('scalar'))


def build(cfg: CFG):
    b = Builder(cfg)
    nc = b.nc
    T, D, M, H, CF, TS, NE, E, R, F = (cfg.T, cfg.D, cfg.M, cfg.H, cfg.CF,
                                       cfg.TS, cfg.NE, cfg.E, cfg.R, cfg.F)
    NBLK = cfg.nblk()

    XT = nc.dram_tensor("xt", [D, T], F32, kind="ExternalInput")
    WQ = nc.dram_tensor("wq", [M, 128, D], F32, kind="ExternalInput")
    WK = nc.dram_tensor("wk", [M, 128, D], F32, kind="ExternalInput")
    WV = nc.dram_tensor("wv", [M, 128, D], F32, kind="ExternalInput")
    WO = nc.dram_tensor("wo", [M, 128, D], F32, kind="ExternalInput")
    BCOLS = nc.dram_tensor("bcols", [128, 5 * M], F32, kind="ExternalInput")
    RW = nc.dram_tensor("rw", [128, M * 8], F32, kind="ExternalInput")
    WGU = nc.dram_tensor("wgu", [NE, CF, 128, 2 * D], F32, kind="ExternalInput")
    WD = nc.dram_tensor("wd", [NE, M, 128, F], F32, kind="ExternalInput")
    OUT = nc.dram_tensor("outT", [D, T], F32, kind="ExternalOutput")

    k_in = nc.dram_tensor("k_in", [D * T // 1024, 1024], BF16, kind="Internal")
    v_in = nc.dram_tensor("v_in", [D * T // 1024, 1024], BF16, kind="Internal")
    ag_space = "Shared" if R > 4 else "Local"
    ag_k = nc.dram_tensor("ag_k", [R * D * T // 1024, 1024], BF16,
                          kind="Internal", addr_space=ag_space)
    ag_v = nc.dram_tensor("ag_v", [R * D * T // 1024, 1024], BF16,
                          kind="Internal", addr_space=ag_space)
    bc_dram = nc.dram_tensor("bc_dram", [4, T], F32, kind="Internal")
    w_dram = nc.dram_tensor("w_dram", [8, T], BF16, kind="Internal")

    cvals = [0.0, 1.0, -1.0, 0.5, 1e-5, 1e-6, EXP_SCALE, EXP_SHIFT]
    CPOOL_D = nc.inline_tensor(np.array([cvals], np.float32), name="cpool_d")
    ONESC_D = nc.inline_tensor(np.ones((128, 1), ml_dtypes.bfloat16), name="onesc_d")
    ONESF_D = nc.inline_tensor(np.ones((128, 1), np.float32), name="onesf_d")
    IDB_D = nc.inline_tensor(np.eye(128, dtype=ml_dtypes.bfloat16), name="idb_d")

    with ExitStack() as ctx:
        sbt = lambda n, s, d: ctx.enter_context(nc.sbuf_tensor(n, s, d))
        pst = lambda n, s, d: ctx.enter_context(nc.psum_tensor(n, s, d))

        xT = sbt("xT", [128, M * T], F32)
        hT = sbt("hT", [128, M * T], BF16)
        qT = sbt("qT", [128, H * T], BF16)
        lnscr = sbt("lnscr", [128, T], F32)
        sq = sbt("sq", [128, 2 * T], BF16)
        cpool = sbt("cpool", [128, len(cvals)], F32)
        onescol = sbt("onescol", [128, 1], BF16)
        onescolf = sbt("onescolf", [128, 1], F32)
        identb = sbt("identb", [128, 128], BF16)
        bcols = sbt("bcols_sb", [128, 5 * M], F32)
        rw_sb = sbt("rw_sb", [128, M * 8], BF16)
        stats = sbt("stats", [1, 4 * T], F32)
        l64 = sbt("l64", [128, 64], F32)
        e64 = sbt("e64", [128, 64], F32)
        k64 = sbt("k64", [128, 64], F32)
        m8 = sbt("m8", [128, 2 * 8], F32)
        w_sb = sbt("w_sb", [128, 64], BF16)
        wT_bf = sbt("wT_bf", [8, T], BF16)
        WSLOT = 2 * D
        wring = sbt("wring", [128, 2 * WSLOT], BF16)
        AR = max(8 * T,
                 2 * CF * T + 6 * T + 6 * F,
                 36 * T)
        arena = sbt("arena", [128, AR], mybir.dt.uint8)
        ar_f32 = arena.ap().bitcast(F32)
        ar_bf = arena.ap().bitcast(BF16)
        nb2 = AR // 2
        nf4 = AR // 4
        A2bc = ar_f32[:, 0:T]
        B2bc = ar_f32[:, T:2 * T]
        hu_v = ar_bf[:, 0:CF * T]
        wbc_v = ar_bf[:, CF * T:CF * T + 3 * T]
        wdr_v = ar_bf[:, CF * T + 3 * T:CF * T + 3 * T + 3 * F]
        kring_v = ar_bf[:, 0:2 * T]
        vring_v = ar_bf[:, 2 * T:4 * T]
        expb_v = ar_bf[:, 4 * T:8 * T]
        evr_v = ar_bf[:, 8 * T:12 * T]
        recbc_v = ar_f32[:, 6 * T:8 * T]
        acc_v = ar_f32[:, 8 * T:9 * T]

        PA = pst("PA", [128, T], F32)
        PB = pst("PB", [128, T], F32)
        PC = pst("PC", [128, T], F32)
        PD = pst("PD", [128, T], F32)
        PS = {"PA": PA, "PB": PB, "PC": PC, "PD": PD}

        for i, v in enumerate(cvals):
            nc.const_aps.register(F32, v, cpool[:, i:i + 1])

        xT_c = lambda c: xT[:, c * T:(c + 1) * T]
        hT_c = lambda c: hT[:, c * T:(c + 1) * T]
        qT_c = lambda c: qT[:, c * T:(c + 1) * T]
        sq_s = lambda s: sq[:, s * T:(s + 1) * T]
        wring_s = lambda s, n=None: wring[:, s * WSLOT:s * WSLOT + (n or WSLOT)]
        wdr_s = lambda s: wdr_v[:, s * F:(s + 1) * F]
        hu_c = lambda c: hu_v[:, c * T:(c + 1) * T]
        wbc_s = lambda s: wbc_v[:, s * T:(s + 1) * T]
        kring_s = lambda s: kring_v[:, s * T:(s + 1) * T]
        vring_s = lambda s: vring_v[:, s * T:(s + 1) * T]
        expb_s = lambda s: expb_v[:, s * T:(s + 1) * T]
        evr_s = lambda s: evr_v[:, s * T:(s + 1) * T]
        recbc_s = lambda s: recbc_v[:, s * T:(s + 1) * T]
        rrow_s = lambda s: stats[0:1, s * T:(s + 1) * T]
        lnscr_bf = lnscr.ap().bitcast(BF16)[:, 0:T]
        LNK = ["A2bc", "B2bc"]
        FFNK = ([f"hu{c}" for c in range(CF)] + ["wbc0", "wbc1", "wbc2"]
                + [f"wdr{i}" for i in range(3)])
        ATTK = (["kring0", "kring1", "vring0", "vring1"]
                + [f"exp{i}" for i in range(4)] + [f"evr{i}" for i in range(4)]
                + ["recbc0", "recbc1", "accL", "accR"])

        bq_c = lambda c: bcols[:, 0 * M + c:0 * M + c + 1]
        bk_c = lambda c: bcols[:, 1 * M + c:1 * M + c + 1]
        bv_c = lambda c: bcols[:, 2 * M + c:2 * M + c + 1]
        bo_c = lambda c: bcols[:, 3 * M + c:3 * M + c + 1]
        rms_c = lambda c: bcols[:, 4 * M + c:4 * M + c + 1]

        kf = k_in.ap().rearrange("a b -> (a b)")
        vf = v_in.ap().rearrange("a b -> (a b)")
        agkf = ag_k.ap().rearrange("a b -> (a b)")
        agvf = ag_v.ap().rearrange("a b -> (a b)")

        def k_stage_ap(h):
            return kf[h * 128 * T:(h * 128 + 128) * T].rearrange(
                "(p k) -> p k", k=T)

        def v_stage_ap(h):
            vv = vf.rearrange("(k d) -> k d", d=D)
            return vv.rearrange("(s p) d -> p s d", p=128)[:, :, h * 128:(h + 1) * 128]

        def k_read_ap(r, h):
            base = r * D * T
            return agkf[base + h * 128 * T:base + (h * 128 + 128) * T].rearrange(
                "(p k) -> p k", k=T)

        def v_read_ap(r, h):
            base = r * D * T
            vv = agvf[base:base + D * T].rearrange("(k d) -> k d", d=D)
            return vv.rearrange("(c p) d -> p c d", p=128)[:, :, h * 128:(h + 1) * 128]

        m_cp = b.dma("sync", cpool[:, :],
                     CPOOL_D[:, :].to_broadcast([128, len(cvals)]), sem="s_pre")
        m_oc = b.dma("sync", onescol[:, :], ONESC_D[:, :], sem="s_pre")
        b.dma("sync", onescolf[:, :], ONESF_D[:, :], sem="s_pre")
        m_idb = b.dma("sync", identb[:, :], IDB_D[:, :], sem="s_pre")
        m_bcl = b.dma("sync", bcols[:, :], BCOLS[:, :], sem="s_pre")
        m_xt0 = b.dma("sync", xT[:, 0:(M // 2) * T],
                      XT[0:D // 2, :].rearrange("(m p) t -> p m t", p=128),
                      sem="s_x0")
        m_xt1 = b.dma("sync", xT[:, (M // 2) * T:],
                      XT[D // 2:D, :].rearrange("(m p) t -> p m t", p=128),
                      sem="s_x1")
        m_rw = b.dma("gpsimd", rw_sb[:, :], RW[:, :], sem="s_pre2")
        m_all = ("s_pre", b.planned["s_pre"])
        m_cp = m_oc = m_idb = m_bcl = m_all
        for c in range(M):
            b.produced(f"xT{c}", m_xt0 if c < M // 2 else m_xt1)
        b.produced("cpool", m_cp)

        def strow(i):
            return stats[0:1, i * T:(i + 1) * T]

        def layernorm(with_rms):
            cast_m = {}
            sum_m = suma_m = None
            for c in range(M):
                cm = b.dve(lambda v, c=c: v.tensor_copy(hT_c(c), xT_c(c)),
                           waits=b.r(f"xT{c}") + b.w(f"hT{c}"))
                b.consumed(f"xT{c}", cm)
                cast_m[c] = cm
                sl = c % 2
                sm = b.act(lambda s, c=c, sl=sl: s.activation(
                    sq_s(sl), xT_c(c), AF.Square),
                    waits=b.r(f"xT{c}") + b.w(f"sq{sl}") + b.r("cpool"))
                b.consumed(f"xT{c}", sm)
                b.produced(f"sq{sl}", sm)
                for bi, (c0, nb) in enumerate(NBLK):
                    last = (c == M - 1) and (bi == len(NBLK) - 1)
                    wa = [cm, m_oc] + (b.w("PA") if c == 0 and bi == 0 else [])
                    ma = b.mm(PA[0:1, c0:c0 + nb], onescol[:, :],
                              hT_c(c)[:, c0:c0 + nb],
                              start=(c == 0), stop=(c == M - 1), waits=wa,
                              inc=last)
                    b.consumed(f"hT{c}", ma)
                    wb = [sm, m_oc] + (b.w("PB") if c == 0 and bi == 0 else [])
                    mb = b.mm(PB[0:1, c0:c0 + nb], onescol[:, :],
                              sq_s(sl)[:, c0:c0 + nb],
                              start=(c == 0), stop=(c == M - 1), waits=wb,
                              inc=last)
                    b.consumed(f"sq{sl}", mb)
                    if last:
                        sum_m, suma_m = mb, ma
            b.produced("PA", suma_m)
            b.produced("PB", sum_m)
            r0, r1, r2, r3 = strow(0), strow(1), strow(2), strow(3)
            m_mu = b.dve(lambda v: v.tensor_scalar_mul(r0, PA[0:1, :], 1.0 / D),
                         waits=[suma_m, sum_m] + b.w("st0"))
            b.produced("st0", m_mu)
            b.consumed("PA", m_mu)
            m_e2 = b.dve(lambda v: v.tensor_scalar_mul(r1, PB[0:1, :], 1.0 / D),
                         waits=[sum_m] + b.w("st1"))
            b.produced("st1", m_e2)
            b.consumed("PB", m_e2)
            m_m2 = b.dve(lambda v: v.tensor_mul(r2, r0, r0),
                         waits=[m_mu] + b.w("st2"))
            b.produced("st2", m_m2)
            b.consumed("st0", m_m2)
            m_v = b.dve(lambda v: v.tensor_sub(r1, r1, r2),
                        waits=[m_e2, m_m2] + b.w("st1"))
            b.produced("st1", m_v)
            b.consumed("st2", m_v)
            m_sd = b.act(lambda sc: sc.activation(r2, r1, AF.Sqrt, bias=1e-5,
                                                  scale=1.0),
                         waits=[m_v] + b.r("cpool") + b.w("st2"))
            b.produced("st2", m_sd)
            b.consumed("st1", m_sd)
            m_rst = b.dve(lambda v: v.reciprocal(r2, r2),
                          waits=[m_sd] + b.w("st2"))
            b.produced("st2", m_rst)
            if with_rms:
                m_q = b.dve(lambda v: v.tensor_mul(r3, r2, r2),
                            waits=[m_rst] + b.w("st3"))
                b.produced("st3", m_q)
                b.consumed("st2", m_q)
                m_vr = b.dve(lambda v: v.tensor_mul(r3, r1, r3),
                             waits=[m_q, m_v] + b.w("st3"))
                b.produced("st3", m_vr)
                b.consumed("st1", m_vr)
                m_sr = b.act(lambda sc: sc.activation(r3, r3, AF.Sqrt, bias=1e-6,
                                                      scale=1.0),
                             waits=[m_vr] + b.r("cpool") + b.w("st3"))
                b.produced("st3", m_sr)
                m_rmr = b.dve(lambda v: v.reciprocal(r3, r3),
                              waits=[m_sr] + b.w("st3"))
                b.produced("st3", m_rmr)
                m_a2f = b.dve(lambda v: v.tensor_mul(r3, r2, r3),
                              waits=[m_rmr, m_rst] + b.w("st3"))
                b.produced("st3", m_a2f)
                b.consumed("st2", m_a2f)
                a2row, k_a2row = r3, "st3"
            else:
                a2row, k_a2row, m_a2f = r2, "st2", m_rst
            m_b2 = b.dve(lambda v: v.scalar_tensor_tensor(
                r0, r0, -1.0, a2row, ALU.mult, ALU.mult),
                waits=[m_mu, m_a2f] + b.w("st0"))
            b.produced("st0", m_b2)
            b.consumed(k_a2row, m_b2)
            b2, k_b2 = r0, "st0"
            m_oA = b.dma("sync", bc_dram[0:1, :], a2row,
                         waits=[m_a2f, m_b2] + b.w("bcd0"), sem="s_bcA")
            b.produced("bcd0", m_oA)
            b.consumed(k_a2row, m_oA)
            if not with_rms:
                b.consumed("st2", m_oA)
            m_oB = b.dma("sync", bc_dram[1:2, :], b2, waits=[m_b2] + b.w("bcd1"), sem="s_bcB")
            b.produced("bcd1", m_oB)
            b.consumed(k_b2, m_oB)
            m_iA = b.dma("sync", A2bc[:, :], bc_dram[0:1, :].to_broadcast([128, T]),
                         waits=[m_oA] + b.w("A2bc"), sem="s_bcA")
            b.produced("A2bc", m_iA)
            b.consumed("bcd0", m_iA)
            m_iB = b.dma("sync", B2bc[:, :], bc_dram[1:2, :].to_broadcast([128, T]),
                         waits=[m_oB] + b.w("B2bc"), sem="s_bcB")
            b.produced("B2bc", m_iB)
            b.consumed("bcd1", m_iB)
            for c in range(M):
                wv_ = (b.r(f"xT{c}") + [m_iA, sum_m, suma_m, cast_m[c]]
                       + b.w("lnscr"))
                if with_rms:
                    m_t1 = b.dve(lambda v, c=c: v.scalar_tensor_tensor(
                        lnscr[:, :], xT_c(c), rms_c(c), A2bc[:, :],
                        ALU.mult, ALU.mult), waits=wv_ + [m_bcl])
                    m_h = b.dve(lambda v, c=c: v.scalar_tensor_tensor(
                        hT_c(c), B2bc[:, :], rms_c(c), lnscr[:, :],
                        ALU.mult, ALU.add),
                        waits=[m_t1, m_iB] + b.w(f"hT{c}"))
                else:
                    m_t1 = b.dve(lambda v, c=c: v.tensor_mul(
                        lnscr[:, :], xT_c(c), A2bc[:, :]), waits=wv_)
                    m_h = b.dve(lambda v, c=c: v.tensor_add(
                        hT_c(c), lnscr[:, :], B2bc[:, :]),
                        waits=[m_t1, m_iB] + b.w(f"hT{c}"))
                b.consumed(f"xT{c}", m_t1)
                b.produced("lnscr", m_t1)
                b.consumed("lnscr", m_h)
                b.produced(f"hT{c}", m_h)
                b.consumed("A2bc", m_t1)
                b.consumed("B2bc", m_h)

        def router():
            NS = TS
            lg = None
            for sub in range(NS):
                for m in range(M):
                    wv_ = b.r(f"hT{m}") + [m_rw]
                    if sub == 0 and m == 0:
                        wv_ += b.w("PC")
                    lg = b.mm(PC[:, 8 * sub:8 * sub + 8],
                              hT_c(m)[:, sub * 128:(sub + 1) * 128],
                              rw_sb[:, m * 8:(m + 1) * 8],
                              start=(sub == 0 and m == 0),
                              stop=(sub == NS - 1 and m == M - 1), waits=wv_,
                              inc=(sub == NS - 1 and m == M - 1))
                    b.consumed(f"hT{m}", lg)
            b.produced("PC", lg)
            W8 = 8 * NS
            lv = l64[:, 0:W8]
            ev = e64[:, 0:W8]
            kv = k64[:, 0:W8]
            l3 = lambda ap: ap.rearrange("p (s e) -> p s e", e=8)
            bc = lambda ap8: ap8[:, :, None].to_broadcast([128, NS, 8])
            m1 = m8[:, 0:NS]
            m1e_ = m8[:, 0:NS]
            m2 = m8[:, 8:8 + NS]
            den = m8[:, 0:NS]
            m_ev0 = b.act(lambda sc: sc.activation(lv, PC[:, 0:W8], AF.Identity,
                                                   bias=0.0, scale=1.0),
                          waits=[lg] + b.w("l64") + b.r("cpool"))
            b.consumed("PC", m_ev0)
            b.produced("l64", m_ev0)
            m_m1 = b.dve(lambda v: v.reduce_max(
                m1, l3(lv), axis=AXX), waits=[m_ev0] + b.w("m8"))
            m_ms = b.dve(lambda v: v.tensor_sub(l3(lv), l3(lv), bc(m1)),
                         waits=[m_m1] + b.w("l64"))
            b.produced("l64", m_ms)
            m_e = b.act(lambda sc: sc.activation(ev, lv, AF.Exp),
                        waits=[m_ms] + b.w("e64") + b.r("cpool"))
            b.consumed("l64", m_e)
            b.produced("e64", m_e)
            m_em = b.dve(lambda v: v.reduce_max(m1e_, l3(ev), axis=AXX),
                         waits=[m_e, m_ms, m_m1])
            m_k1 = b.dve(lambda v: v.tensor_tensor(l3(kv), l3(ev), bc(m1e_),
                                                   ALU.is_ge),
                         waits=[m_em] + b.w("k64"))
            m_k1m = b.dve(lambda v: v.tensor_mul(l3(kv), l3(kv), bc(m1e_)),
                          waits=[m_k1])
            m_e2 = b.dve(lambda v: v.tensor_sub(kv, ev, kv), waits=[m_k1m])
            m_m2 = b.dve(lambda v: v.reduce_max(m2, l3(kv), axis=AXX),
                         waits=[m_e2])
            m_kp = b.dve(lambda v: v.tensor_tensor(l3(kv), l3(ev), bc(m2),
                                                   ALU.is_ge), waits=[m_m2])
            m_wr = b.dve(lambda v: v.tensor_mul(kv, ev, kv), waits=[m_kp])
            b.consumed("e64", m_wr)
            m_dn = b.dve(lambda v: v.tensor_add(den, m1e_, m2),
                         waits=[m_m2, m_em])
            m_rc = b.dve(lambda v: v.reciprocal(den, den), waits=[m_dn])
            m_rh = b.dve(lambda v: v.tensor_scalar_mul(den, den, 0.5),
                         waits=[m_rc])
            m_w = b.dve(lambda v: v.tensor_mul(l3(w_sb[:, 0:W8]), l3(kv),
                                               bc(den)),
                        waits=[m_rh, m_wr] + b.w("w_sb"))
            b.produced("w_sb", m_w)
            b.produced("m8", m_w)
            b.produced("k64", m_w)
            tr_ms = []
            PDw = PD.ap().bitcast(BF16)
            for sub in range(NS):
                wv_ = [m_w, m_idb]
                if sub == 0:
                    wv_ += b.w("PD")
                first_in_bank = (sub * 128) % 1024 == 0
                last_in_bank = ((sub + 1) * 128) % 1024 == 0 or sub == NS - 1
                tr = b.emit("tensor",
                            lambda t, sub=sub, a=first_in_bank, z=last_in_bank:
                                t.matmul(PDw[0:8, sub * 128:(sub + 1) * 128],
                                         w_sb[:, 8 * sub:8 * sub + 8],
                                         identb[:, :],
                                         start=a, stop=z, is_transpose=True),
                            waits=wv_, inc=("s_pe", 1))
                b.consumed("w_sb", tr)
                tr_ms.append(tr)
            b.produced("PD", tr_ms[-1])
            m_ev = b.act(lambda sc: sc.activation(wT_bf[:, :], PDw[0:8, 0:T],
                                                  AF.Identity, bias=0.0,
                                                  scale=1.0),
                         waits=tr_ms + b.w("wT_bf") + b.r("cpool"))
            b.consumed("PD", m_ev)
            b.produced("wT_bf", m_ev)
            m_st = b.dma("sync", w_dram[0:8, :], wT_bf[:, :],
                         waits=[m_ev] + b.w("w_dram"), sem="s_wdram")
            b.produced("w_dram", m_st)
            b.consumed("wT_bf", m_st)
            return m_st

        wslab_i = [0]
        pslab_i = [0]
        wd_i = [0]

        def ffn(tag, final_store=False, hoisted=None):
            m_route = router()
            m_w8 = b.emit("gpsimd", lambda g: g.memset(wbc_s(2), 0.5),
                          waits=b.w("wbc2"), inc=("s_gpe", 1))
            b.produced("wbc2", m_w8)
            for e in range(NE):
                if e < E:
                    ws = e % 2
                    wkey = f"wbc{ws}"
                    m_bc = b.dma("scalar", wbc_s(ws),
                                 w_dram[e:e + 1, :].to_broadcast([128, T]),
                                 waits=[m_route] + b.w(wkey), sem=f"s_wbc{ws}")
                    b.produced(wkey, m_bc)
                    wbc_ap = wbc_s(ws)
                else:
                    wkey = "wbc2"
                    wbc_ap = wbc_s(2)
                for c in range(CF):
                    sl = wslab_i[0] % 2
                    wslab_i[0] += 1
                    if hoisted is not None and e == 0 and c < len(hoisted):
                        m_sl = hoisted[c]
                    else:
                        m_sl = b.dma("gpsimd", wring_s(sl),
                                     WGU[e, c, :, :], waits=b.w(f"wring{sl}"),
                                     sem=f"s_wr{sl}")
                    b.produced(f"wring{sl}", m_sl)
                    lg = lu = None
                    for m in range(M):
                        for bi, (c0, nb) in enumerate(NBLK):
                            wv_ = [m_sl] + b.r(f"hT{m}")
                            if m == 0 and bi == 0:
                                wv_ += b.w("PA")
                            lg = b.mm(PA[:, c0:c0 + nb],
                                      wring_s(sl)[:, m * 128:(m + 1) * 128],
                                      hT_c(m)[:, c0:c0 + nb],
                                      start=(m == 0), stop=(m == M - 1),
                                      waits=wv_,
                                      inc=(m == M - 1 and bi == len(NBLK) - 1))
                            b.consumed(f"hT{m}", lg)
                            b.consumed(f"wring{sl}", lg)
                    for m in range(M):
                        for bi, (c0, nb) in enumerate(NBLK):
                            wv_ = [m_sl] + b.r(f"hT{m}")
                            if m == 0 and bi == 0:
                                wv_ += b.w("PB")
                            lu = b.mm(PB[:, c0:c0 + nb],
                                      wring_s(sl)[:, D + m * 128:D + (m + 1) * 128],
                                      hT_c(m)[:, c0:c0 + nb],
                                      start=(m == 0), stop=(m == M - 1),
                                      waits=wv_,
                                      inc=(m == M - 1 and bi == len(NBLK) - 1))
                            b.consumed(f"hT{m}", lu)
                            b.consumed(f"wring{sl}", lu)
                    b.produced("PA", lg)
                    b.produced("PB", lu)
                    ssl = c % 2
                    m_si = b.act(lambda s, sl2=ssl: s.activation(
                        sq_s(sl2), PA[:, :], AF.Silu),
                        waits=[lg] + b.w(f"sq{ssl}") + b.r("cpool"))
                    b.consumed("PA", m_si)
                    b.produced(f"sq{ssl}", m_si)
                    m_t = b.dve(lambda v, sl2=ssl, c=c: v.tensor_mul(
                        hu_c(c), sq_s(sl2), PB[:, :]),
                        waits=[m_si, lu] + b.w(f"hu{c}"))
                    b.consumed("PB", m_t)
                    b.consumed(f"sq{ssl}", m_t)
                    b.produced(f"hu{c}", m_t)
                    m_hu = b.dve(lambda v, c=c, wb=wbc_ap: v.tensor_mul(
                        hu_c(c), hu_c(c), wb),
                        waits=[m_t] + b.r(wkey) + b.w(f"hu{c}"))
                    b.consumed(wkey, m_hu)
                    b.produced(f"hu{c}", m_hu)
                for j in range(M):
                    sl = wd_i[0] % 3
                    wd_i[0] += 1
                    m_sl = b.dma("gpsimd", wdr_s(sl), WD[e, j, :, :],
                                 waits=b.w(f"wdr{sl}"), sem=f"s_wd{sl}")
                    b.produced(f"wdr{sl}", m_sl)
                    pk = "PC" if j % 2 == 0 else "PD"
                    P = PS[pk]
                    dn = None
                    for c in range(CF):
                        for bi, (c0, nb) in enumerate(NBLK):
                            wv_ = [m_sl] + b.r(f"hu{c}")
                            if c == 0 and bi == 0:
                                wv_ += b.w(pk)
                            dn = b.mm(P[:, c0:c0 + nb],
                                      wdr_s(sl)[:, c * 128:(c + 1) * 128],
                                      hu_c(c)[:, c0:c0 + nb],
                                      start=(c == 0), stop=(c == CF - 1),
                                      waits=wv_,
                                      inc=(c == CF - 1 and bi == len(NBLK) - 1))
                            b.consumed(f"hu{c}", dn)
                            b.consumed(f"wdr{sl}", dn)
                    b.produced(pk, dn)
                    m_x = b.dve(lambda v, j=j, P=P: v.tensor_add(
                        xT_c(j), xT_c(j), P[:, :]),
                        waits=[dn] + b.w(f"xT{j}"))
                    b.consumed(pk, m_x)
                    b.produced(f"xT{j}", m_x)
                    if final_store and e == NE - 1:
                        mo = b.dma("sync",
                                   OUT.ap().rearrange("(m p) t -> p m t", p=128)
                                   [:, j:j + 1, :],
                                   xT_c(j), waits=[m_x], sem="s_out")
                        b.consumed(f"xT{j}", mo)
                        out_stores.append(mo)

        out_stores = []

        def pr_s(sl):
            return wring[:, sl * D:(sl + 1) * D]

        def proj_feature_major(Wdram, bias_c, out_key_fn, out_ap_fn):
            for j in range(M):
                sl = pslab_i[0] % 4
                pslab_i[0] += 1
                m_sl = b.dma("gpsimd", pr_s(sl), Wdram[j, :, :],
                             waits=b.w(f"pr{sl}"), sem=f"s_pr{sl}")
                b.produced(f"pr{sl}", m_sl)
                pk = "PA" if j % 2 == 0 else "PB"
                P = PS[pk]
                pm = None
                for m in range(M):
                    for bi, (c0, nb) in enumerate(NBLK):
                        wv_ = [m_sl] + b.r(f"hT{m}")
                        if m == 0 and bi == 0:
                            wv_ += b.w(pk)
                        pm = b.mm(P[:, c0:c0 + nb],
                                  pr_s(sl)[:, m * 128:(m + 1) * 128],
                                  hT_c(m)[:, c0:c0 + nb],
                                  start=(m == 0), stop=(m == M - 1), waits=wv_,
                                  inc=(m == M - 1 and bi == len(NBLK) - 1))
                        b.consumed(f"hT{m}", pm)
                        b.consumed(f"pr{sl}", pm)
                b.produced(pk, pm)
                okey = out_key_fn(j)
                m_ev = b.act(lambda s, j=j, P=P, oap=out_ap_fn(j), bap=bias_c(j):
                             s.activation(oap, P[:, :], AF.Identity, bias=bap,
                                          scale=1.0),
                             waits=[pm, m_bcl] + b.w(okey) + b.r("cpool"))
                b.consumed(pk, m_ev)
                b.produced(okey, m_ev)
                yield j, okey, m_ev

        def qkv_and_ag():
            k_writes, v_writes = [], []
            for j, okey, m_ev in proj_feature_major(
                    WK, bk_c, lambda j: f"evr{j % 2}", lambda j: evr_s(j % 2)):
                m_o = b.dma("sync", k_stage_ap(j), evr_s(j % 2), waits=[m_ev],
                             sem=f"s_kv{j % 2}")
                b.consumed(okey, m_o)
                k_writes.append(m_o)
            m_ccK = b.emit("gpsimd",
                           lambda g: g.collective_compute(
                               "AllGather", ALU.bypass,
                               replica_groups=[list(range(R))],
                               ins=[k_in.ap().opt()],
                               outs=[ag_k.ap().opt()]),
                           waits=k_writes, inc=("s_cc", 1))
            PDb = PD.ap().bitcast(BF16)
            for j, okey, m_ev in proj_feature_major(
                    WV, bv_c, lambda j: f"evr{j % 2}", lambda j: evr_s(j % 2)):
                trs = []
                for s in range(TS):
                    wv_ = [m_ev, m_idb]
                    if s == 0:
                        wv_ += b.w("PD")
                    tr = b.emit("tensor",
                                lambda t, s=s, j=j: t.matmul(
                                    PDb[:, s * 128:(s + 1) * 128],
                                    evr_s(j % 2)[:, s * 128:(s + 1) * 128],
                                    identb[:, :],
                                    start=(s == 0), stop=(s == TS - 1),
                                    is_transpose=True),
                                waits=wv_, inc=("s_pe", 1))
                    b.consumed(okey, tr)
                    trs.append(tr)
                b.produced("PD", trs[-1])
                vk = 2 + (j % 2)
                m_vv = b.act(lambda s, vk=vk: s.activation(
                    evr_s(vk), PDb[:, 0:T], AF.Identity, bias=0.0, scale=1.0),
                    waits=trs + b.w(f"evr{vk}") + b.r("cpool"))
                b.consumed("PD", m_vv)
                b.produced(f"evr{vk}", m_vv)
                m_o = b.dma("sync", v_stage_ap(j),
                            evr_s(vk).rearrange("p (s d) -> p s d", d=128),
                            waits=[m_vv], sem=f"s_kv{vk}")
                b.consumed(f"evr{vk}", m_o)
                v_writes.append(m_o)
            m_ccV = b.emit("gpsimd",
                           lambda g: g.collective_compute(
                               "AllGather", ALU.bypass,
                               replica_groups=[list(range(R))],
                               ins=[v_in.ap().opt()],
                               outs=[ag_v.ap().opt()]),
                           waits=v_writes, inc=("s_cc2", 1))
            for _ in proj_feature_major(WQ, bq_c, lambda j: f"qT{j}",
                                        lambda j: qT_c(j)):
                pass
            return m_ccK, m_ccV

        def attention(m_ccK, m_ccV):
            NKC = R * TS

            def kv_dma(h, r):
                ks = r % 2
                m_k = b.dma("sync", kring_s(ks), k_read_ap(r, h),
                            waits=[m_ccK] + b.w(f"kring{ks}"), sem=f"s_kr{ks}")
                b.produced(f"kring{ks}", m_k)
                m_v = b.dma("sync",
                            vring_s(ks).rearrange("p (c d) -> p c d", d=128),
                            v_read_ap(r, h), waits=[m_ccV] + b.w(f"vring{ks}"),
                            sem=f"s_vr{ks}")
                b.produced(f"vring{ks}", m_v)
                return ks, m_k, m_v

            exp_i = [0]
            pend_tail = [None]

            def emit_div(h):
                rs = h % 2
                m_ri = b.rdy[f"recbc{rs}"]
                m_at = b.rdy[f"qT{h}"]
                m_dv = b.dve(lambda v, h=h, rs=rs: v.tensor_mul(
                    qT_c(h), qT_c(h), recbc_s(rs)),
                    waits=[m_ri, m_at])
                b.consumed(f"recbc{rs}", m_dv)
                b.produced(f"qT{h}", m_dv)

            def emit_tail():
                if pend_tail[0] is None:
                    return
                h, avs = pend_tail[0]
                pend_tail[0] = None
                m_aL = b.rdy["accL"]
                sms = None
                for bi, (c0, nb) in enumerate(NBLK):
                    wv_ = [m_aL, m_oc] + (b.w("PD") if bi == 0 else [])
                    sms = b.mm(PD[0:1, c0:c0 + nb], onescolf[:, :],
                               acc_v[:, c0:c0 + nb],
                               start=True, stop=True, waits=wv_,
                               inc=(bi == len(NBLK) - 1))
                    b.consumed("accL", sms)
                b.produced("PD", sms)
                m_at = b.act(lambda sc, h=h: sc.activation(
                    qT_c(h), PC[:, :], AF.Identity, bias=0.0, scale=1.0),
                    waits=[avs] + b.w(f"qT{h}") + b.r("cpool"))
                b.consumed("PC", m_at)
                b.produced(f"qT{h}", m_at)
                rs = h % 2
                m_rc = b.dve(lambda v, rs=rs: v.reciprocal(rrow_s(rs),
                                                           PD[0:1, :]),
                             waits=[sms] + b.w(f"st{rs}"))
                b.consumed("PD", m_rc)
                b.produced(f"st{rs}", m_rc)
                m_ro = b.dma("gpsimd", bc_dram[2:3, :], rrow_s(rs),
                             waits=[m_rc] + b.w("bcd2"), sem=f"s_rec{rs}")
                b.produced("bcd2", m_ro)
                b.consumed(f"st{rs}", m_ro)
                m_ri = b.dma("gpsimd", recbc_s(rs),
                             bc_dram[2:3, :].to_broadcast([128, T]),
                             waits=[m_ro] + b.w(f"recbc{rs}"), sem=f"s_rec{rs}")
                b.produced(f"recbc{rs}", m_ri)
                b.consumed("bcd2", m_ri)

            for h in range(H):
                pend = {}
                pend[0] = kv_dma(h, 0)
                if R > 1:
                    pend[1] = kv_dma(h, 1)
                avs = None
                av_q = []
                for r in range(R):
                    ks, m_k, m_v = pend.pop(r)
                    for c in range(TS):
                        kc = r * TS + c
                        spk = "PA" if kc % 2 == 0 else "PB"
                        SP = PS[spk]
                        sc_m = None
                        for bi, (c0, nb) in enumerate(NBLK):
                            wv_ = [m_k] + b.r(f"qT{h}") + (b.w(spk) if bi == 0
                                                           else [])
                            sc_m = b.mm(SP[:, c0:c0 + nb],
                                        kring_s(ks)[:, c * 128:(c + 1) * 128],
                                        qT_c(h)[:, c0:c0 + nb],
                                        start=True, stop=True, waits=wv_,
                                        inc=(bi == len(NBLK) - 1))
                            b.consumed(f"qT{h}", sc_m)
                            b.consumed(f"kring{ks}", sc_m)
                        b.produced(spk, sc_m)
                        es = exp_i[0] % 4
                        exp_i[0] += 1
                        m_x = b.act(lambda sc, es=es, SP=SP: sc.activation(
                            expb_s(es), SP[:, :], AF.Exp, bias=EXP_SHIFT,
                            scale=EXP_SCALE),
                            waits=[sc_m] + b.w(f"exp{es}") + b.r("cpool"))
                        b.consumed(spk, m_x)
                        b.produced(f"exp{es}", m_x)
                        if kc == 0:
                            emit_tail()
                        if kc == 0:
                            m_aL = b.dve(lambda v, es=es: v.tensor_copy(
                                acc_v, expb_s(es)),
                                waits=[m_x] + b.w("accL"))
                        else:
                            m_aL = b.dve(lambda v, es=es: v.tensor_add(
                                acc_v, acc_v, expb_s(es)),
                                waits=[m_x] + b.w("accL"))
                        b.produced("accL", m_aL)
                        b.consumed(f"exp{es}", m_aL)
                        av_q.append((es, ks, kc, m_v))
                        if len(av_q) > 2:
                            pes, pks, pkc, pm_v = av_q.pop(0)
                            pc = pkc % TS
                            m_px = b.rdy[f"exp{pes}"]
                            for bi, (c0, nb) in enumerate(NBLK):
                                wv_ = [m_px, pm_v] + (b.w("PC")
                                                      if pkc == 0 and bi == 0
                                                      else [])
                                avs = b.mm(PC[:, c0:c0 + nb],
                                           vring_s(pks)[:, pc * 128:(pc + 1) * 128],
                                           expb_s(pes)[:, c0:c0 + nb],
                                           start=(pkc == 0), stop=False,
                                           waits=wv_,
                                           inc=(bi == len(NBLK) - 1))
                                b.consumed(f"exp{pes}", avs)
                                b.consumed(f"vring{pks}", avs)
                        if h > 0 and r == 0 and c == min(4, TS - 1):
                            emit_div(h - 1)
                    if r + 2 < R:
                        pend[r + 2] = kv_dma(h, r + 2)
                while av_q:
                    pes, pks, pkc, pm_v = av_q.pop(0)
                    pc = pkc % TS
                    m_px = b.rdy[f"exp{pes}"]
                    for bi, (c0, nb) in enumerate(NBLK):
                        wv_ = [m_px, pm_v] + (b.w("PC") if pkc == 0 and bi == 0
                                              else [])
                        avs = b.mm(PC[:, c0:c0 + nb],
                                   vring_s(pks)[:, pc * 128:(pc + 1) * 128],
                                   expb_s(pes)[:, c0:c0 + nb],
                                   start=(pkc == 0), stop=(pkc == NKC - 1),
                                   waits=wv_,
                                   inc=(bi == len(NBLK) - 1))
                        b.consumed(f"exp{pes}", avs)
                        b.consumed(f"vring{pks}", avs)
                b.produced("PC", avs)
                pend_tail[0] = (h, avs)
            emit_tail()
            emit_div(H - 1)
        def outproj():
            for j in range(M):
                sl = pslab_i[0] % 4
                pslab_i[0] += 1
                m_sl = b.dma("gpsimd", pr_s(sl), WO[j, :, :],
                             waits=b.w(f"pr{sl}"), sem=f"s_pr{sl}")
                b.produced(f"pr{sl}", m_sl)
                pk = "PA" if j % 2 == 0 else "PB"
                P = PS[pk]
                pm = None
                for m in range(M):
                    for bi, (c0, nb) in enumerate(NBLK):
                        wv_ = [m_sl] + b.r(f"qT{m}")
                        if m == 0 and bi == 0:
                            wv_ += b.w(pk)
                        pm = b.mm(P[:, c0:c0 + nb],
                                  pr_s(sl)[:, m * 128:(m + 1) * 128],
                                  qT_c(m)[:, c0:c0 + nb],
                                  start=(m == 0), stop=(m == M - 1), waits=wv_,
                                  inc=(m == M - 1 and bi == len(NBLK) - 1))
                        b.consumed(f"qT{m}", pm)
                        b.consumed(f"pr{sl}", pm)
                b.produced(pk, pm)
                m_x = b.dve(lambda v, j=j, P=P: v.scalar_tensor_tensor(
                    xT_c(j), P[:, :], bo_c(j), xT_c(j), ALU.add, ALU.add),
                    waits=[pm, m_bcl] + b.w(f"xT{j}"))
                b.consumed(pk, m_x)
                b.produced(f"xT{j}", m_x)

        WRK = ["wring0", "wring1"]
        PRK = ["pr0", "pr1", "pr2", "pr3"]

        def hoist_wgu():
            ms = []
            for c in range(2):
                sl = wslab_i[0] % 2
                wslab_i[0] += 1
                ms.append(b.dma("gpsimd", wring_s(sl), WGU[0, c, :, :],
                                waits=b.w(f"wring{sl}"), sem=f"s_wr{sl}"))
            wslab_i[0] -= 2
            return ms

        h1 = hoist_wgu()
        layernorm(with_rms=True)
        b.fence(LNK, FFNK)
        ffn("f1", hoisted=h1)
        b.fence(FFNK, LNK)
        b.fence(WRK, PRK)
        layernorm(with_rms=False)
        b.fence(LNK, ATTK)
        m_ccK, m_ccV = qkv_and_ag()
        attention(m_ccK, m_ccV)
        outproj()
        b.fence(ATTK, LNK)
        b.fence(PRK, WRK)
        h2 = hoist_wgu()
        layernorm(with_rms=True)
        b.fence(LNK, FFNK)
        ffn("f2", final_store=True, hoisted=h2)
        b.emit("sync", lambda e: e.nop(), waits=[out_stores[-1]])

        block = ctx.enter_context(nc.Block())
        b.realize(block)
    return nc



def _host_layout(inputs, cfg: CFG):
    D, M, F, E, NE, CF = cfg.D, cfg.M, cfg.F, cfg.E, cfg.NE, cfg.CF
    x = np.asarray(inputs["L0_attn_hidden_states"], np.float32)[0]
    wq = np.asarray(inputs["L0_attn_wq"], np.float32)
    wk = np.asarray(inputs["L0_attn_wk"], np.float32)
    wv = np.asarray(inputs["L0_attn_wv"], np.float32)
    wo = np.asarray(inputs["L0_attn_wo"], np.float32)

    def wslabs(w):
        wt = np.ascontiguousarray(w.T)
        a = wt.reshape(M, 128, M, 128)
        return np.ascontiguousarray(a.transpose(2, 1, 0, 3)).reshape(M, 128, D)

    def bcol(v):
        return np.ascontiguousarray(np.asarray(v, np.float32).reshape(M, 128).T)

    bq = bcol(inputs["L0_attn_bq"]); bk = bcol(inputs["L0_attn_bk"])
    bv = bcol(inputs["L0_attn_bv"]); bo = bcol(inputs["L0_attn_bo"])
    rms = bcol(inputs["L0_ffn_rms_weight"])
    bcols = np.ascontiguousarray(np.concatenate([bq, bk, bv, bo, rms], axis=1))

    rw = np.asarray(inputs["L0_ffn_router_w"], np.float32)
    rw_s = np.ascontiguousarray(rw.reshape(M, 128, 8).transpose(1, 0, 2)
                                .reshape(128, M * 8))

    wg = np.asarray(inputs["L0_ffn_w_gate"], np.float32)
    wu = np.asarray(inputs["L0_ffn_w_up"], np.float32)
    wd = np.asarray(inputs["L0_ffn_w_down"], np.float32)
    swg = np.asarray(inputs["L0_ffn_sw_gate"], np.float32)
    swu = np.asarray(inputs["L0_ffn_sw_up"], np.float32)
    swd = np.asarray(inputs["L0_ffn_sw_down"], np.float32)

    def gu_slabs(g, u):
        def t(w):
            return w.reshape(M, 128, CF, 128).transpose(2, 1, 0, 3)
        out = np.stack([t(g), t(u)], axis=2)
        return out.reshape(CF, 128, 2 * D)

    wgu = np.empty((NE, CF, 128, 2 * D), np.float32)
    for e in range(E):
        wgu[e] = gu_slabs(wg[e], wu[e])
    wgu[E] = gu_slabs(swg, swu)
    wgu = np.ascontiguousarray(wgu)

    def d_slabs(w):
        return (w.reshape(CF, 128, M, 128).transpose(2, 1, 0, 3)
                .reshape(M, 128, F))

    wds = np.empty((NE, M, 128, F), np.float32)
    for e in range(E):
        wds[e] = d_slabs(wd[e])
    wds[E] = d_slabs(swd)
    wds = np.ascontiguousarray(wds)

    shared = dict(wq=wslabs(wq), wk=wslabs(wk), wv=wslabs(wv), wo=wslabs(wo),
                  bcols=bcols, rw=rw_s, wgu=wgu, wd=wds)
    in_maps = []
    for r in range(cfg.R):
        m = dict(shared)
        m["xt"] = np.ascontiguousarray(x[r * cfg.T:(r + 1) * cfg.T, :].T)
        in_maps.append(m)
    return in_maps


def kernel(**inputs):
    cfg = CFG()
    nc = build(cfg)
    in_maps = _host_layout(inputs, cfg)
    res = run_bass_kernel_spmd(nc, in_maps, core_ids=list(range(cfg.R)))
    out = np.empty((1, cfg.S, cfg.D), np.float32)
    for r in range(cfg.R):
        out[0, r * cfg.T:(r + 1) * cfg.T, :] = res.results[r]["outT"].T
    return out


# version 12
# speedup vs baseline: 1.0124x; 1.0045x over previous
import numpy as np
import ml_dtypes
from contextlib import ExitStack
import concourse.bass as bass
import concourse.mybir as mybir
import concourse.bass_utils as _bass_utils
from concourse.bass_utils import run_bass_kernel_spmd

if not getattr(_bass_utils, "_ldw_opt_patched", False):
    _orig_run_command = _bass_utils.run_command

    def _run_command_ldw(cmd, *a, **kw):
        if isinstance(cmd, list):
            cmd = ["--enable-ldw-opt=true" if c == "--enable-ldw-opt=false"
                   else c for c in cmd]
        return _orig_run_command(cmd, *a, **kw)

    _bass_utils.run_command = _run_command_ldw
    _bass_utils._ldw_opt_patched = True

F32 = mybir.dt.float32
BF16 = mybir.dt.bfloat16
AF = mybir.ActivationFunctionType
ALU = mybir.AluOpType
AXX = mybir.AxisListType.X

EXP_SCALE = float(np.float32(1.0 / np.sqrt(128.0)))
EXP_SHIFT = -10.0


class CFG:
    def __init__(self, R=8, T=1024, D=2048, F=1024, E=8):
        self.R, self.T, self.D, self.F, self.E = R, T, D, F, E
        self.M = D // 128
        self.H = D // 128
        self.CF = F // 128
        self.TS = T // 128
        self.NE = E + 1
        self.S = R * T
        assert T % 128 == 0 and D % 128 == 0 and F % 128 == 0

    def nblk(self):
        out, c = [], 0
        while c < self.T:
            n = min(512, self.T - c)
            out.append((c, n))
            c += n
        return out


class Builder:
    ENGINES = ("sync", "gpsimd", "tensor", "vector", "scalar")

    def __init__(self, cfg: CFG):
        self.cfg = cfg
        self.nc = bass.Bass("TRN2", target_bir_lowering=False)
        self.steps = []
        self.sems = {}
        self.planned = {}
        for s in ("s_sync", "s_gp", "s_gpe", "s_pe", "s_act", "s_dve", "s_cc",
                  "s_pre", "s_wr0", "s_wr1", "s_wr2", "s_wd0", "s_wd1", "s_wd2",
                  "s_kr0", "s_kr1", "s_vr0", "s_vr1", "s_kv0", "s_kv1", "s_kv2",
                  "s_kv3", "s_bcA", "s_bcB",
                  "s_rec0", "s_rec1", "s_wbc0", "s_wbc1", "s_wdram", "s_out", "s_cc2",
                  "s_pr0", "s_pr1", "s_pr2", "s_pr3", "s_x0", "s_x1",
                  "s_pre2"):
            self.sems[s] = self.nc.alloc_semaphore(s)
            self.planned[s] = 0
        self.rdy = {}
        self.frees = {}
        self.pe_pending = set()
        self.last_tensor_step = None

    def emit(self, eng, fn, waits=(), inc=None):
        assert eng in self.ENGINES
        w = [x for x in waits if x is not None]
        mi = None
        if inc is not None:
            self.planned[inc[0]] += inc[1]
            mi = (inc[0], self.planned[inc[0]])
        self.steps.append((eng, fn, tuple(w), inc))
        if eng == "tensor":
            self.last_tensor_step = len(self.steps) - 1
            if mi is not None and self.pe_pending:
                for key in self.pe_pending:
                    self.consumed(key, mi)
                self.pe_pending.clear()
        return mi

    def dma(self, eng, out_ap, in_ap, waits=(), sem=None):
        assert sem is not None
        return self.emit(eng, lambda e, o=out_ap, i=in_ap: e.dma_start(out=o, in_=i),
                         waits=waits, inc=(sem, 16))

    def mm(self, out_ap, lhsT, rhs, start, stop, waits=(), inc=False):
        return self.emit(
            "tensor",
            lambda t, o=out_ap, l=lhsT, r=rhs, a=start, b=stop:
                t.matmul(o, l, r, start=a, stop=b),
            waits=waits, inc=("s_pe", 1) if inc else None)

    def act(self, fn, waits=()):
        return self.emit("scalar", fn, waits=waits, inc=("s_act", 1))

    def dve(self, fn, waits=()):
        return self.emit("vector", fn, waits=waits, inc=("s_dve", 1))

    def gpe(self, fn, waits=()):
        return self.emit("gpsimd", fn, waits=waits, inc=("s_gpe", 1))

    def produced(self, key, m):
        self.rdy[key] = m
        self.frees[key] = {}

    def consumed(self, key, m):
        if m is None:
            self.pe_pending.add(key)
            return
        d = self.frees.setdefault(key, {})
        s, v = m
        if d.get(s, 0) < v:
            d[s] = v

    def r(self, key):
        m = self.rdy.get(key)
        return [m] if m is not None else []

    def _pe_flush(self):
        if not self.pe_pending:
            return
        i = self.last_tensor_step
        eng, fn, wts, inc = self.steps[i]
        assert eng == "tensor"
        if inc is None:
            self.planned["s_pe"] += 1
            inc = ("s_pe", 1)
            self.steps[i] = (eng, fn, wts, inc)
        mi = ("s_pe", self.planned["s_pe"])
        for key in self.pe_pending:
            self.consumed(key, mi)
        self.pe_pending.clear()

    def fence(self, from_keys, to_keys):
        acc = {}
        for k in from_keys:
            for (sn, v) in self.w(k):
                if acc.get(sn, 0) < v:
                    acc[sn] = v
        for k in to_keys:
            d = self.frees.setdefault(k, {})
            for sn, v in acc.items():
                if d.get(sn, 0) < v:
                    d[sn] = v

    def w(self, key):
        if key in self.pe_pending:
            self._pe_flush()
        out = list(self.frees.get(key, {}).items())
        m = self.rdy.get(key)
        if m is not None:
            out.append(m)
        return out

    def realize(self, block):
        sems, steps = self.sems, self.steps

        def mk(eng_name):
            def run(e):
                hwm = {}
                for eng, fn, waits, inc in steps:
                    if eng != eng_name:
                        continue
                    for (sn, val) in waits:
                        if hwm.get(sn, 0) < val:
                            e.wait_ge(sems[sn], val)
                            hwm[sn] = val
                    ins = fn(e)
                    if inc is not None:
                        ins.then_inc(sems[inc[0]], inc[1])
            return run

        block.sync(mk('sync'))
        block.gpsimd(mk('gpsimd'))
        block.tensor(mk('tensor'))
        block.vector(mk('vector'))
        block.scalar(mk('scalar'))


def build(cfg: CFG):
    b = Builder(cfg)
    nc = b.nc
    T, D, M, H, CF, TS, NE, E, R, F = (cfg.T, cfg.D, cfg.M, cfg.H, cfg.CF,
                                       cfg.TS, cfg.NE, cfg.E, cfg.R, cfg.F)
    NBLK = cfg.nblk()

    XT = nc.dram_tensor("xt", [D, T], F32, kind="ExternalInput")
    WQ = nc.dram_tensor("wq", [M, 128, D], F32, kind="ExternalInput")
    WK = nc.dram_tensor("wk", [M, 128, D], F32, kind="ExternalInput")
    WV = nc.dram_tensor("wv", [M, 128, D], F32, kind="ExternalInput")
    WO = nc.dram_tensor("wo", [M, 128, D], F32, kind="ExternalInput")
    BCOLS = nc.dram_tensor("bcols", [128, 5 * M], F32, kind="ExternalInput")
    RW = nc.dram_tensor("rw", [128, M * 8], F32, kind="ExternalInput")
    WGU = nc.dram_tensor("wgu", [NE, CF, 128, 2 * D], F32, kind="ExternalInput")
    WD = nc.dram_tensor("wd", [NE, M, 128, F], F32, kind="ExternalInput")
    OUT = nc.dram_tensor("outT", [D, T], F32, kind="ExternalOutput")

    k_in = nc.dram_tensor("k_in", [D * T // 1024, 1024], BF16, kind="Internal")
    v_in = nc.dram_tensor("v_in", [D * T // 1024, 1024], BF16, kind="Internal")
    ag_space = "Shared" if R > 4 else "Local"
    ag_k = nc.dram_tensor("ag_k", [R * D * T // 1024, 1024], BF16,
                          kind="Internal", addr_space=ag_space)
    ag_v = nc.dram_tensor("ag_v", [R * D * T // 1024, 1024], BF16,
                          kind="Internal", addr_space=ag_space)
    bc_dram = nc.dram_tensor("bc_dram", [4, T], F32, kind="Internal")
    w_dram = nc.dram_tensor("w_dram", [8, T], BF16, kind="Internal")

    cvals = [0.0, 1.0, -1.0, 0.5, 1e-5, 1e-6, EXP_SCALE, EXP_SHIFT]
    CPOOL_D = nc.inline_tensor(np.array([cvals], np.float32), name="cpool_d")
    ONESC_D = nc.inline_tensor(np.ones((128, 1), ml_dtypes.bfloat16), name="onesc_d")
    ONESF_D = nc.inline_tensor(np.ones((128, 1), np.float32), name="onesf_d")
    IDB_D = nc.inline_tensor(np.eye(128, dtype=ml_dtypes.bfloat16), name="idb_d")

    with ExitStack() as ctx:
        sbt = lambda n, s, d: ctx.enter_context(nc.sbuf_tensor(n, s, d))
        pst = lambda n, s, d: ctx.enter_context(nc.psum_tensor(n, s, d))

        xT = sbt("xT", [128, M * T], F32)
        hT = sbt("hT", [128, M * T], BF16)
        qT = sbt("qT", [128, H * T], BF16)
        lnscr = sbt("lnscr", [128, T], F32)
        sq = sbt("sq", [128, 2 * T], BF16)
        cpool = sbt("cpool", [128, len(cvals)], F32)
        onescol = sbt("onescol", [128, 1], BF16)
        onescolf = sbt("onescolf", [128, 1], F32)
        identb = sbt("identb", [128, 128], BF16)
        bcols = sbt("bcols_sb", [128, 5 * M], F32)
        rw_sb = sbt("rw_sb", [128, M * 8], BF16)
        stats = sbt("stats", [1, 4 * T], F32)
        l64 = sbt("l64", [128, 64], F32)
        e64 = sbt("e64", [128, 64], F32)
        k64 = sbt("k64", [128, 64], F32)
        m8 = sbt("m8", [128, 2 * 8], F32)
        w_sb = sbt("w_sb", [128, 64], BF16)
        wT_bf = sbt("wT_bf", [8, T], BF16)
        WSLOT = 2 * D
        wring = sbt("wring", [128, 2 * WSLOT], BF16)
        AR = max(8 * T,
                 2 * CF * T + 6 * T + 6 * F,
                 36 * T)
        arena = sbt("arena", [128, AR], mybir.dt.uint8)
        ar_f32 = arena.ap().bitcast(F32)
        ar_bf = arena.ap().bitcast(BF16)
        nb2 = AR // 2
        nf4 = AR // 4
        A2bc = ar_f32[:, 0:T]
        B2bc = ar_f32[:, T:2 * T]
        hu_v = ar_bf[:, 0:CF * T]
        wbc_v = ar_bf[:, CF * T:CF * T + 3 * T]
        wdr_v = ar_bf[:, CF * T + 3 * T:CF * T + 3 * T + 3 * F]
        kring_v = ar_bf[:, 0:2 * T]
        vring_v = ar_bf[:, 2 * T:4 * T]
        expb_v = ar_bf[:, 4 * T:8 * T]
        evr_v = ar_bf[:, 8 * T:12 * T]
        recbc_v = ar_f32[:, 6 * T:8 * T]
        acc_v = ar_f32[:, 8 * T:9 * T]

        PA = pst("PA", [128, T], F32)
        PB = pst("PB", [128, T], F32)
        PC = pst("PC", [128, T], F32)
        PD = pst("PD", [128, T], F32)
        PS = {"PA": PA, "PB": PB, "PC": PC, "PD": PD}

        for i, v in enumerate(cvals):
            nc.const_aps.register(F32, v, cpool[:, i:i + 1])

        xT_c = lambda c: xT[:, c * T:(c + 1) * T]
        hT_c = lambda c: hT[:, c * T:(c + 1) * T]
        qT_c = lambda c: qT[:, c * T:(c + 1) * T]
        sq_s = lambda s: sq[:, s * T:(s + 1) * T]
        wring_s = lambda s, n=None: wring[:, s * WSLOT:s * WSLOT + (n or WSLOT)]
        wdr_s = lambda s: wdr_v[:, s * F:(s + 1) * F]
        hu_c = lambda c: hu_v[:, c * T:(c + 1) * T]
        wbc_s = lambda s: wbc_v[:, s * T:(s + 1) * T]
        kring_s = lambda s: kring_v[:, s * T:(s + 1) * T]
        vring_s = lambda s: vring_v[:, s * T:(s + 1) * T]
        expb_s = lambda s: expb_v[:, s * T:(s + 1) * T]
        evr_s = lambda s: evr_v[:, s * T:(s + 1) * T]
        recbc_s = lambda s: recbc_v[:, s * T:(s + 1) * T]
        rrow_s = lambda s: stats[0:1, s * T:(s + 1) * T]
        lnscr_bf = lnscr.ap().bitcast(BF16)[:, 0:T]
        LNK = ["A2bc", "B2bc"]
        FFNK = ([f"hu{c}" for c in range(CF)] + ["wbc0", "wbc1", "wbc2"]
                + [f"wdr{i}" for i in range(3)])
        ATTK = (["kring0", "kring1", "vring0", "vring1"]
                + [f"exp{i}" for i in range(4)] + [f"evr{i}" for i in range(4)]
                + ["recbc0", "recbc1", "accL", "accR"])

        bq_c = lambda c: bcols[:, 0 * M + c:0 * M + c + 1]
        bk_c = lambda c: bcols[:, 1 * M + c:1 * M + c + 1]
        bv_c = lambda c: bcols[:, 2 * M + c:2 * M + c + 1]
        bo_c = lambda c: bcols[:, 3 * M + c:3 * M + c + 1]
        rms_c = lambda c: bcols[:, 4 * M + c:4 * M + c + 1]

        kf = k_in.ap().rearrange("a b -> (a b)")
        vf = v_in.ap().rearrange("a b -> (a b)")
        agkf = ag_k.ap().rearrange("a b -> (a b)")
        agvf = ag_v.ap().rearrange("a b -> (a b)")

        def k_stage_ap(h):
            return kf[h * 128 * T:(h * 128 + 128) * T].rearrange(
                "(p k) -> p k", k=T)

        def v_stage_ap(h):
            vv = vf.rearrange("(k d) -> k d", d=D)
            return vv.rearrange("(s p) d -> p s d", p=128)[:, :, h * 128:(h + 1) * 128]

        def k_read_ap(r, h):
            base = r * D * T
            return agkf[base + h * 128 * T:base + (h * 128 + 128) * T].rearrange(
                "(p k) -> p k", k=T)

        def v_read_ap(r, h):
            base = r * D * T
            vv = agvf[base:base + D * T].rearrange("(k d) -> k d", d=D)
            return vv.rearrange("(c p) d -> p c d", p=128)[:, :, h * 128:(h + 1) * 128]

        m_cp = b.dma("sync", cpool[:, :],
                     CPOOL_D[:, :].to_broadcast([128, len(cvals)]), sem="s_pre")
        m_oc = b.dma("sync", onescol[:, :], ONESC_D[:, :], sem="s_pre")
        b.dma("sync", onescolf[:, :], ONESF_D[:, :], sem="s_pre")
        m_idb = b.dma("sync", identb[:, :], IDB_D[:, :], sem="s_pre")
        m_bcl = b.dma("sync", bcols[:, :], BCOLS[:, :], sem="s_pre")
        m_xt0 = b.dma("sync", xT[:, 0:(M // 2) * T],
                      XT[0:D // 2, :].rearrange("(m p) t -> p m t", p=128),
                      sem="s_x0")
        m_xt1 = b.dma("sync", xT[:, (M // 2) * T:],
                      XT[D // 2:D, :].rearrange("(m p) t -> p m t", p=128),
                      sem="s_x1")
        m_rw = b.dma("gpsimd", rw_sb[:, :], RW[:, :], sem="s_pre2")
        m_all = ("s_pre", b.planned["s_pre"])
        m_cp = m_oc = m_idb = m_bcl = m_all
        for c in range(M):
            b.produced(f"xT{c}", m_xt0 if c < M // 2 else m_xt1)
        b.produced("cpool", m_cp)

        def strow(i):
            return stats[0:1, i * T:(i + 1) * T]

        def layernorm(with_rms):
            cast_m = {}
            sum_m = suma_m = None
            for c in range(M):
                cm = b.dve(lambda v, c=c: v.tensor_copy(hT_c(c), xT_c(c)),
                           waits=b.r(f"xT{c}") + b.w(f"hT{c}"))
                b.consumed(f"xT{c}", cm)
                cast_m[c] = cm
                sl = c % 2
                sm = b.act(lambda s, c=c, sl=sl: s.activation(
                    sq_s(sl), xT_c(c), AF.Square),
                    waits=b.r(f"xT{c}") + b.w(f"sq{sl}") + b.r("cpool"))
                b.consumed(f"xT{c}", sm)
                b.produced(f"sq{sl}", sm)
                for bi, (c0, nb) in enumerate(NBLK):
                    last = (c == M - 1) and (bi == len(NBLK) - 1)
                    wa = [cm, m_oc] + (b.w("PA") if c == 0 and bi == 0 else [])
                    ma = b.mm(PA[0:1, c0:c0 + nb], onescol[:, :],
                              hT_c(c)[:, c0:c0 + nb],
                              start=(c == 0), stop=(c == M - 1), waits=wa,
                              inc=last)
                    b.consumed(f"hT{c}", ma)
                    wb = [sm, m_oc] + (b.w("PB") if c == 0 and bi == 0 else [])
                    mb = b.mm(PB[0:1, c0:c0 + nb], onescol[:, :],
                              sq_s(sl)[:, c0:c0 + nb],
                              start=(c == 0), stop=(c == M - 1), waits=wb,
                              inc=last)
                    b.consumed(f"sq{sl}", mb)
                    if last:
                        sum_m, suma_m = mb, ma
            b.produced("PA", suma_m)
            b.produced("PB", sum_m)
            r0, r1, r2, r3 = strow(0), strow(1), strow(2), strow(3)
            m_mu = b.dve(lambda v: v.tensor_scalar_mul(r0, PA[0:1, :], 1.0 / D),
                         waits=[suma_m, sum_m] + b.w("st0"))
            b.produced("st0", m_mu)
            b.consumed("PA", m_mu)
            m_e2 = b.dve(lambda v: v.tensor_scalar_mul(r1, PB[0:1, :], 1.0 / D),
                         waits=[sum_m] + b.w("st1"))
            b.produced("st1", m_e2)
            b.consumed("PB", m_e2)
            m_m2 = b.dve(lambda v: v.tensor_mul(r2, r0, r0),
                         waits=[m_mu] + b.w("st2"))
            b.produced("st2", m_m2)
            b.consumed("st0", m_m2)
            m_v = b.dve(lambda v: v.tensor_sub(r1, r1, r2),
                        waits=[m_e2, m_m2] + b.w("st1"))
            b.produced("st1", m_v)
            b.consumed("st2", m_v)
            m_sd = b.act(lambda sc: sc.activation(r2, r1, AF.Sqrt, bias=1e-5,
                                                  scale=1.0),
                         waits=[m_v] + b.r("cpool") + b.w("st2"))
            b.produced("st2", m_sd)
            b.consumed("st1", m_sd)
            m_rst = b.dve(lambda v: v.reciprocal(r2, r2),
                          waits=[m_sd] + b.w("st2"))
            b.produced("st2", m_rst)
            if with_rms:
                m_q = b.dve(lambda v: v.tensor_mul(r3, r2, r2),
                            waits=[m_rst] + b.w("st3"))
                b.produced("st3", m_q)
                b.consumed("st2", m_q)
                m_vr = b.dve(lambda v: v.tensor_mul(r3, r1, r3),
                             waits=[m_q, m_v] + b.w("st3"))
                b.produced("st3", m_vr)
                b.consumed("st1", m_vr)
                m_sr = b.act(lambda sc: sc.activation(r3, r3, AF.Sqrt, bias=1e-6,
                                                      scale=1.0),
                             waits=[m_vr] + b.r("cpool") + b.w("st3"))
                b.produced("st3", m_sr)
                m_rmr = b.dve(lambda v: v.reciprocal(r3, r3),
                              waits=[m_sr] + b.w("st3"))
                b.produced("st3", m_rmr)
                m_a2f = b.dve(lambda v: v.tensor_mul(r3, r2, r3),
                              waits=[m_rmr, m_rst] + b.w("st3"))
                b.produced("st3", m_a2f)
                b.consumed("st2", m_a2f)
                a2row, k_a2row = r3, "st3"
            else:
                a2row, k_a2row, m_a2f = r2, "st2", m_rst
            m_b2 = b.dve(lambda v: v.scalar_tensor_tensor(
                r0, r0, -1.0, a2row, ALU.mult, ALU.mult),
                waits=[m_mu, m_a2f] + b.w("st0"))
            b.produced("st0", m_b2)
            b.consumed(k_a2row, m_b2)
            b2, k_b2 = r0, "st0"
            m_oA = b.dma("sync", bc_dram[0:1, :], a2row,
                         waits=[m_a2f, m_b2] + b.w("bcd0"), sem="s_bcA")
            b.produced("bcd0", m_oA)
            b.consumed(k_a2row, m_oA)
            if not with_rms:
                b.consumed("st2", m_oA)
            m_oB = b.dma("sync", bc_dram[1:2, :], b2, waits=[m_b2] + b.w("bcd1"), sem="s_bcB")
            b.produced("bcd1", m_oB)
            b.consumed(k_b2, m_oB)
            m_iA = b.dma("sync", A2bc[:, :], bc_dram[0:1, :].to_broadcast([128, T]),
                         waits=[m_oA] + b.w("A2bc"), sem="s_bcA")
            b.produced("A2bc", m_iA)
            b.consumed("bcd0", m_iA)
            m_iB = b.dma("sync", B2bc[:, :], bc_dram[1:2, :].to_broadcast([128, T]),
                         waits=[m_oB] + b.w("B2bc"), sem="s_bcB")
            b.produced("B2bc", m_iB)
            b.consumed("bcd1", m_iB)
            for c in range(M):
                wv_ = (b.r(f"xT{c}") + [m_iA, sum_m, suma_m, cast_m[c]]
                       + b.w("lnscr"))
                if with_rms:
                    m_t1 = b.dve(lambda v, c=c: v.scalar_tensor_tensor(
                        lnscr[:, :], xT_c(c), rms_c(c), A2bc[:, :],
                        ALU.mult, ALU.mult), waits=wv_ + [m_bcl])
                    m_h = b.dve(lambda v, c=c: v.scalar_tensor_tensor(
                        hT_c(c), B2bc[:, :], rms_c(c), lnscr[:, :],
                        ALU.mult, ALU.add),
                        waits=[m_t1, m_iB] + b.w(f"hT{c}"))
                else:
                    m_t1 = b.dve(lambda v, c=c: v.tensor_mul(
                        lnscr[:, :], xT_c(c), A2bc[:, :]), waits=wv_)
                    m_h = b.dve(lambda v, c=c: v.tensor_add(
                        hT_c(c), lnscr[:, :], B2bc[:, :]),
                        waits=[m_t1, m_iB] + b.w(f"hT{c}"))
                b.consumed(f"xT{c}", m_t1)
                b.produced("lnscr", m_t1)
                b.consumed("lnscr", m_h)
                b.produced(f"hT{c}", m_h)
                b.consumed("A2bc", m_t1)
                b.consumed("B2bc", m_h)

        def router():
            NS = TS
            lg = None
            for sub in range(NS):
                for m in range(M):
                    wv_ = b.r(f"hT{m}") + [m_rw]
                    if sub == 0 and m == 0:
                        wv_ += b.w("PC")
                    lg = b.mm(PC[:, 8 * sub:8 * sub + 8],
                              hT_c(m)[:, sub * 128:(sub + 1) * 128],
                              rw_sb[:, m * 8:(m + 1) * 8],
                              start=(sub == 0 and m == 0),
                              stop=(sub == NS - 1 and m == M - 1), waits=wv_,
                              inc=(sub == NS - 1 and m == M - 1))
                    b.consumed(f"hT{m}", lg)
            b.produced("PC", lg)
            W8 = 8 * NS
            lv = l64[:, 0:W8]
            ev = e64[:, 0:W8]
            kv = k64[:, 0:W8]
            l3 = lambda ap: ap.rearrange("p (s e) -> p s e", e=8)
            bc = lambda ap8: ap8[:, :, None].to_broadcast([128, NS, 8])
            m1 = m8[:, 0:NS]
            m1e_ = m8[:, 0:NS]
            m2 = m8[:, 8:8 + NS]
            den = m8[:, 0:NS]
            m_ev0 = b.act(lambda sc: sc.activation(lv, PC[:, 0:W8], AF.Identity,
                                                   bias=0.0, scale=1.0),
                          waits=[lg] + b.w("l64") + b.r("cpool"))
            b.consumed("PC", m_ev0)
            b.produced("l64", m_ev0)
            m_m1 = b.dve(lambda v: v.reduce_max(
                m1, l3(lv), axis=AXX), waits=[m_ev0] + b.w("m8"))
            m_ms = b.dve(lambda v: v.tensor_sub(l3(lv), l3(lv), bc(m1)),
                         waits=[m_m1] + b.w("l64"))
            b.produced("l64", m_ms)
            m_e = b.act(lambda sc: sc.activation(ev, lv, AF.Exp),
                        waits=[m_ms] + b.w("e64") + b.r("cpool"))
            b.consumed("l64", m_e)
            b.produced("e64", m_e)
            m_em = b.dve(lambda v: v.reduce_max(m1e_, l3(ev), axis=AXX),
                         waits=[m_e, m_ms, m_m1])
            m_k1 = b.dve(lambda v: v.tensor_tensor(l3(kv), l3(ev), bc(m1e_),
                                                   ALU.is_ge),
                         waits=[m_em] + b.w("k64"))
            m_k1m = b.dve(lambda v: v.tensor_mul(l3(kv), l3(kv), bc(m1e_)),
                          waits=[m_k1])
            m_e2 = b.dve(lambda v: v.tensor_sub(kv, ev, kv), waits=[m_k1m])
            m_m2 = b.dve(lambda v: v.reduce_max(m2, l3(kv), axis=AXX),
                         waits=[m_e2])
            m_kp = b.dve(lambda v: v.tensor_tensor(l3(kv), l3(ev), bc(m2),
                                                   ALU.is_ge), waits=[m_m2])
            m_wr = b.dve(lambda v: v.tensor_mul(kv, ev, kv), waits=[m_kp])
            b.consumed("e64", m_wr)
            m_dn = b.dve(lambda v: v.tensor_add(den, m1e_, m2),
                         waits=[m_m2, m_em])
            m_rc = b.dve(lambda v: v.reciprocal(den, den), waits=[m_dn])
            m_rh = b.dve(lambda v: v.tensor_scalar_mul(den, den, 0.5),
                         waits=[m_rc])
            m_w = b.dve(lambda v: v.tensor_mul(l3(w_sb[:, 0:W8]), l3(kv),
                                               bc(den)),
                        waits=[m_rh, m_wr] + b.w("w_sb"))
            b.produced("w_sb", m_w)
            b.produced("m8", m_w)
            b.produced("k64", m_w)
            tr_ms = []
            PDw = PD.ap().bitcast(BF16)
            for sub in range(NS):
                wv_ = [m_w, m_idb]
                if sub == 0:
                    wv_ += b.w("PD")
                first_in_bank = (sub * 128) % 1024 == 0
                last_in_bank = ((sub + 1) * 128) % 1024 == 0 or sub == NS - 1
                tr = b.emit("tensor",
                            lambda t, sub=sub, a=first_in_bank, z=last_in_bank:
                                t.matmul(PDw[0:8, sub * 128:(sub + 1) * 128],
                                         w_sb[:, 8 * sub:8 * sub + 8],
                                         identb[:, :],
                                         start=a, stop=z, is_transpose=True),
                            waits=wv_, inc=("s_pe", 1))
                b.consumed("w_sb", tr)
                tr_ms.append(tr)
            b.produced("PD", tr_ms[-1])
            m_ev = b.act(lambda sc: sc.activation(wT_bf[:, :], PDw[0:8, 0:T],
                                                  AF.Identity, bias=0.0,
                                                  scale=1.0),
                         waits=tr_ms + b.w("wT_bf") + b.r("cpool"))
            b.consumed("PD", m_ev)
            b.produced("wT_bf", m_ev)
            m_st = b.dma("sync", w_dram[0:8, :], wT_bf[:, :],
                         waits=[m_ev] + b.w("w_dram"), sem="s_wdram")
            b.produced("w_dram", m_st)
            b.consumed("wT_bf", m_st)
            return m_st

        wslab_i = [0]
        pslab_i = [0]
        wd_i = [0]

        def ffn(tag, final_store=False, hoisted=None):
            m_route = router()
            m_w8 = b.emit("gpsimd", lambda g: g.memset(wbc_s(2), 0.5),
                          waits=b.w("wbc2"), inc=("s_gpe", 1))
            b.produced("wbc2", m_w8)
            for e in range(NE):
                if e < E:
                    ws = e % 2
                    wkey = f"wbc{ws}"
                    m_bc = b.dma("scalar", wbc_s(ws),
                                 w_dram[e:e + 1, :].to_broadcast([128, T]),
                                 waits=[m_route] + b.w(wkey), sem=f"s_wbc{ws}")
                    b.produced(wkey, m_bc)
                    wbc_ap = wbc_s(ws)
                else:
                    wkey = "wbc2"
                    wbc_ap = wbc_s(2)
                for c in range(CF):
                    sl = wslab_i[0] % 2
                    wslab_i[0] += 1
                    if hoisted is not None and e == 0 and c < len(hoisted):
                        m_sl = hoisted[c]
                    else:
                        m_sl = b.dma("gpsimd", wring_s(sl),
                                     WGU[e, c, :, :], waits=b.w(f"wring{sl}"),
                                     sem=f"s_wr{sl}")
                    b.produced(f"wring{sl}", m_sl)
                    lg = lu = None
                    for m in range(M):
                        for bi, (c0, nb) in enumerate(NBLK):
                            wv_ = [m_sl] + b.r(f"hT{m}")
                            if m == 0 and bi == 0:
                                wv_ += b.w("PA")
                            lg = b.mm(PA[:, c0:c0 + nb],
                                      wring_s(sl)[:, m * 128:(m + 1) * 128],
                                      hT_c(m)[:, c0:c0 + nb],
                                      start=(m == 0), stop=(m == M - 1),
                                      waits=wv_,
                                      inc=(m == M - 1 and bi == len(NBLK) - 1))
                            b.consumed(f"hT{m}", lg)
                            b.consumed(f"wring{sl}", lg)
                    for m in range(M):
                        for bi, (c0, nb) in enumerate(NBLK):
                            wv_ = [m_sl] + b.r(f"hT{m}")
                            if m == 0 and bi == 0:
                                wv_ += b.w("PB")
                            lu = b.mm(PB[:, c0:c0 + nb],
                                      wring_s(sl)[:, D + m * 128:D + (m + 1) * 128],
                                      hT_c(m)[:, c0:c0 + nb],
                                      start=(m == 0), stop=(m == M - 1),
                                      waits=wv_,
                                      inc=(m == M - 1 and bi == len(NBLK) - 1))
                            b.consumed(f"hT{m}", lu)
                            b.consumed(f"wring{sl}", lu)
                    b.produced("PA", lg)
                    b.produced("PB", lu)
                    ssl = c % 2
                    m_si = b.act(lambda s, sl2=ssl: s.activation(
                        sq_s(sl2), PA[:, :], AF.Silu),
                        waits=[lg] + b.w(f"sq{ssl}") + b.r("cpool"))
                    b.consumed("PA", m_si)
                    b.produced(f"sq{ssl}", m_si)
                    m_t = b.dve(lambda v, sl2=ssl, c=c: v.tensor_mul(
                        hu_c(c), sq_s(sl2), PB[:, :]),
                        waits=[m_si, lu] + b.w(f"hu{c}"))
                    b.consumed("PB", m_t)
                    b.consumed(f"sq{ssl}", m_t)
                    b.produced(f"hu{c}", m_t)
                    m_hu = b.dve(lambda v, c=c, wb=wbc_ap: v.tensor_mul(
                        hu_c(c), hu_c(c), wb),
                        waits=[m_t] + b.r(wkey) + b.w(f"hu{c}"))
                    b.consumed(wkey, m_hu)
                    b.produced(f"hu{c}", m_hu)
                for j in range(M):
                    sl = wd_i[0] % 3
                    wd_i[0] += 1
                    m_sl = b.dma("gpsimd", wdr_s(sl), WD[e, j, :, :],
                                 waits=b.w(f"wdr{sl}"), sem=f"s_wd{sl}")
                    b.produced(f"wdr{sl}", m_sl)
                    pk = "PC" if j % 2 == 0 else "PD"
                    P = PS[pk]
                    dn = None
                    for c in range(CF):
                        for bi, (c0, nb) in enumerate(NBLK):
                            wv_ = [m_sl] + b.r(f"hu{c}")
                            if c == 0 and bi == 0:
                                wv_ += b.w(pk)
                            dn = b.mm(P[:, c0:c0 + nb],
                                      wdr_s(sl)[:, c * 128:(c + 1) * 128],
                                      hu_c(c)[:, c0:c0 + nb],
                                      start=(c == 0), stop=(c == CF - 1),
                                      waits=wv_,
                                      inc=(c == CF - 1 and bi == len(NBLK) - 1))
                            b.consumed(f"hu{c}", dn)
                            b.consumed(f"wdr{sl}", dn)
                    b.produced(pk, dn)
                    m_x = b.dve(lambda v, j=j, P=P: v.tensor_add(
                        xT_c(j), xT_c(j), P[:, :]),
                        waits=[dn] + b.w(f"xT{j}"))
                    b.consumed(pk, m_x)
                    b.produced(f"xT{j}", m_x)
                    if final_store and e == NE - 1:
                        mo = b.dma("sync",
                                   OUT.ap().rearrange("(m p) t -> p m t", p=128)
                                   [:, j:j + 1, :],
                                   xT_c(j), waits=[m_x], sem="s_out")
                        b.consumed(f"xT{j}", mo)
                        out_stores.append(mo)

        out_stores = []

        def pr_s(sl):
            return wring[:, sl * D:(sl + 1) * D]

        def hoist_proj(Wdram, n=3):
            ms = []
            for j in range(n):
                sl = pslab_i[0] % 4
                pslab_i[0] += 1
                ms.append(b.dma("gpsimd", pr_s(sl), Wdram[j, :, :],
                                waits=b.w(f"pr{sl}"), sem=f"s_pr{sl}"))
            pslab_i[0] -= n
            return ms

        def proj_feature_major(Wdram, bias_c, out_key_fn, out_ap_fn,
                               hoisted=None):
            for j in range(M):
                sl = pslab_i[0] % 4
                pslab_i[0] += 1
                if hoisted is not None and j < len(hoisted):
                    m_sl = hoisted[j]
                else:
                    m_sl = b.dma("gpsimd", pr_s(sl), Wdram[j, :, :],
                                 waits=b.w(f"pr{sl}"), sem=f"s_pr{sl}")
                b.produced(f"pr{sl}", m_sl)
                pk = "PA" if j % 2 == 0 else "PB"
                P = PS[pk]
                pm = None
                for m in range(M):
                    for bi, (c0, nb) in enumerate(NBLK):
                        wv_ = [m_sl] + b.r(f"hT{m}")
                        if m == 0 and bi == 0:
                            wv_ += b.w(pk)
                        pm = b.mm(P[:, c0:c0 + nb],
                                  pr_s(sl)[:, m * 128:(m + 1) * 128],
                                  hT_c(m)[:, c0:c0 + nb],
                                  start=(m == 0), stop=(m == M - 1), waits=wv_,
                                  inc=(m == M - 1 and bi == len(NBLK) - 1))
                        b.consumed(f"hT{m}", pm)
                        b.consumed(f"pr{sl}", pm)
                b.produced(pk, pm)
                okey = out_key_fn(j)
                m_ev = b.act(lambda s, j=j, P=P, oap=out_ap_fn(j), bap=bias_c(j):
                             s.activation(oap, P[:, :], AF.Identity, bias=bap,
                                          scale=1.0),
                             waits=[pm, m_bcl] + b.w(okey) + b.r("cpool"))
                b.consumed(pk, m_ev)
                b.produced(okey, m_ev)
                yield j, okey, m_ev

        def qkv_and_ag():
            k_writes, v_writes = [], []
            for j, okey, m_ev in proj_feature_major(
                    WK, bk_c, lambda j: f"evr{j % 2}", lambda j: evr_s(j % 2)):
                m_o = b.dma("sync", k_stage_ap(j), evr_s(j % 2), waits=[m_ev],
                             sem=f"s_kv{j % 2}")
                b.consumed(okey, m_o)
                k_writes.append(m_o)
            hv = hoist_proj(WV, n=3)
            m_ccK = b.emit("gpsimd",
                           lambda g: g.collective_compute(
                               "AllGather", ALU.bypass,
                               replica_groups=[list(range(R))],
                               ins=[k_in.ap().opt()],
                               outs=[ag_k.ap().opt()]),
                           waits=k_writes, inc=("s_cc", 1))
            PDb = PD.ap().bitcast(BF16)
            for j, okey, m_ev in proj_feature_major(
                    WV, bv_c, lambda j: f"evr{j % 2}", lambda j: evr_s(j % 2),
                    hoisted=hv):
                trs = []
                for s in range(TS):
                    wv_ = [m_ev, m_idb]
                    if s == 0:
                        wv_ += b.w("PD")
                    tr = b.emit("tensor",
                                lambda t, s=s, j=j: t.matmul(
                                    PDb[:, s * 128:(s + 1) * 128],
                                    evr_s(j % 2)[:, s * 128:(s + 1) * 128],
                                    identb[:, :],
                                    start=(s == 0), stop=(s == TS - 1),
                                    is_transpose=True),
                                waits=wv_, inc=("s_pe", 1))
                    b.consumed(okey, tr)
                    trs.append(tr)
                b.produced("PD", trs[-1])
                vk = 2 + (j % 2)
                m_vv = b.act(lambda s, vk=vk: s.activation(
                    evr_s(vk), PDb[:, 0:T], AF.Identity, bias=0.0, scale=1.0),
                    waits=trs + b.w(f"evr{vk}") + b.r("cpool"))
                b.consumed("PD", m_vv)
                b.produced(f"evr{vk}", m_vv)
                m_o = b.dma("sync", v_stage_ap(j),
                            evr_s(vk).rearrange("p (s d) -> p s d", d=128),
                            waits=[m_vv], sem=f"s_kv{vk}")
                b.consumed(f"evr{vk}", m_o)
                v_writes.append(m_o)
            hq = hoist_proj(WQ, n=3)
            m_ccV = b.emit("gpsimd",
                           lambda g: g.collective_compute(
                               "AllGather", ALU.bypass,
                               replica_groups=[list(range(R))],
                               ins=[v_in.ap().opt()],
                               outs=[ag_v.ap().opt()]),
                           waits=v_writes, inc=("s_cc2", 1))
            for _ in proj_feature_major(WQ, bq_c, lambda j: f"qT{j}",
                                        lambda j: qT_c(j), hoisted=hq):
                pass
            return m_ccK, m_ccV

        def attention(m_ccK, m_ccV):
            NKC = R * TS

            def kv_dma(h, r):
                ks = r % 2
                m_k = b.dma("sync", kring_s(ks), k_read_ap(r, h),
                            waits=[m_ccK] + b.w(f"kring{ks}"), sem=f"s_kr{ks}")
                b.produced(f"kring{ks}", m_k)
                m_v = b.dma("sync",
                            vring_s(ks).rearrange("p (c d) -> p c d", d=128),
                            v_read_ap(r, h), waits=[m_ccV] + b.w(f"vring{ks}"),
                            sem=f"s_vr{ks}")
                b.produced(f"vring{ks}", m_v)
                return ks, m_k, m_v

            exp_i = [0]
            pend_tail = [None]

            def emit_div(h):
                rs = h % 2
                m_ri = b.rdy[f"recbc{rs}"]
                m_at = b.rdy[f"qT{h}"]
                m_dv = b.dve(lambda v, h=h, rs=rs: v.tensor_mul(
                    qT_c(h), qT_c(h), recbc_s(rs)),
                    waits=[m_ri, m_at])
                b.consumed(f"recbc{rs}", m_dv)
                b.produced(f"qT{h}", m_dv)

            def emit_tail():
                if pend_tail[0] is None:
                    return
                h, avs = pend_tail[0]
                pend_tail[0] = None
                m_aL = b.rdy["accL"]
                sms = None
                for bi, (c0, nb) in enumerate(NBLK):
                    wv_ = [m_aL, m_oc] + (b.w("PD") if bi == 0 else [])
                    sms = b.mm(PD[0:1, c0:c0 + nb], onescolf[:, :],
                               acc_v[:, c0:c0 + nb],
                               start=True, stop=True, waits=wv_,
                               inc=(bi == len(NBLK) - 1))
                    b.consumed("accL", sms)
                b.produced("PD", sms)
                m_at = b.act(lambda sc, h=h: sc.activation(
                    qT_c(h), PC[:, :], AF.Identity, bias=0.0, scale=1.0),
                    waits=[avs] + b.w(f"qT{h}") + b.r("cpool"))
                b.consumed("PC", m_at)
                b.produced(f"qT{h}", m_at)
                rs = h % 2
                m_rc = b.dve(lambda v, rs=rs: v.reciprocal(rrow_s(rs),
                                                           PD[0:1, :]),
                             waits=[sms] + b.w(f"st{rs}"))
                b.consumed("PD", m_rc)
                b.produced(f"st{rs}", m_rc)
                m_ro = b.dma("gpsimd", bc_dram[2:3, :], rrow_s(rs),
                             waits=[m_rc] + b.w("bcd2"), sem=f"s_rec{rs}")
                b.produced("bcd2", m_ro)
                b.consumed(f"st{rs}", m_ro)
                m_ri = b.dma("gpsimd", recbc_s(rs),
                             bc_dram[2:3, :].to_broadcast([128, T]),
                             waits=[m_ro] + b.w(f"recbc{rs}"), sem=f"s_rec{rs}")
                b.produced(f"recbc{rs}", m_ri)
                b.consumed("bcd2", m_ri)

            for h in range(H):
                pend = {}
                pend[0] = kv_dma(h, 0)
                if R > 1:
                    pend[1] = kv_dma(h, 1)
                avs = None
                av_q = []
                for r in range(R):
                    ks, m_k, m_v = pend.pop(r)
                    for c in range(TS):
                        kc = r * TS + c
                        spk = "PA" if kc % 2 == 0 else "PB"
                        SP = PS[spk]
                        sc_m = None
                        for bi, (c0, nb) in enumerate(NBLK):
                            wv_ = [m_k] + b.r(f"qT{h}") + (b.w(spk) if bi == 0
                                                           else [])
                            sc_m = b.mm(SP[:, c0:c0 + nb],
                                        kring_s(ks)[:, c * 128:(c + 1) * 128],
                                        qT_c(h)[:, c0:c0 + nb],
                                        start=True, stop=True, waits=wv_,
                                        inc=(bi == len(NBLK) - 1))
                            b.consumed(f"qT{h}", sc_m)
                            b.consumed(f"kring{ks}", sc_m)
                        b.produced(spk, sc_m)
                        es = exp_i[0] % 4
                        exp_i[0] += 1
                        m_x = b.act(lambda sc, es=es, SP=SP: sc.activation(
                            expb_s(es), SP[:, :], AF.Exp, bias=EXP_SHIFT,
                            scale=EXP_SCALE),
                            waits=[sc_m] + b.w(f"exp{es}") + b.r("cpool"))
                        b.consumed(spk, m_x)
                        b.produced(f"exp{es}", m_x)
                        if kc == 0:
                            emit_tail()
                        if kc == 0:
                            m_aL = b.dve(lambda v, es=es: v.tensor_copy(
                                acc_v, expb_s(es)),
                                waits=[m_x] + b.w("accL"))
                        else:
                            m_aL = b.dve(lambda v, es=es: v.tensor_add(
                                acc_v, acc_v, expb_s(es)),
                                waits=[m_x] + b.w("accL"))
                        b.produced("accL", m_aL)
                        b.consumed(f"exp{es}", m_aL)
                        av_q.append((es, ks, kc, m_v))
                        if len(av_q) > 2:
                            pes, pks, pkc, pm_v = av_q.pop(0)
                            pc = pkc % TS
                            m_px = b.rdy[f"exp{pes}"]
                            for bi, (c0, nb) in enumerate(NBLK):
                                wv_ = [m_px, pm_v] + (b.w("PC")
                                                      if pkc == 0 and bi == 0
                                                      else [])
                                avs = b.mm(PC[:, c0:c0 + nb],
                                           vring_s(pks)[:, pc * 128:(pc + 1) * 128],
                                           expb_s(pes)[:, c0:c0 + nb],
                                           start=(pkc == 0), stop=False,
                                           waits=wv_,
                                           inc=(bi == len(NBLK) - 1))
                                b.consumed(f"exp{pes}", avs)
                                b.consumed(f"vring{pks}", avs)
                        if h > 0 and r == 0 and c == min(4, TS - 1):
                            emit_div(h - 1)
                    if r + 2 < R:
                        pend[r + 2] = kv_dma(h, r + 2)
                while av_q:
                    pes, pks, pkc, pm_v = av_q.pop(0)
                    pc = pkc % TS
                    m_px = b.rdy[f"exp{pes}"]
                    for bi, (c0, nb) in enumerate(NBLK):
                        wv_ = [m_px, pm_v] + (b.w("PC") if pkc == 0 and bi == 0
                                              else [])
                        avs = b.mm(PC[:, c0:c0 + nb],
                                   vring_s(pks)[:, pc * 128:(pc + 1) * 128],
                                   expb_s(pes)[:, c0:c0 + nb],
                                   start=(pkc == 0), stop=(pkc == NKC - 1),
                                   waits=wv_,
                                   inc=(bi == len(NBLK) - 1))
                        b.consumed(f"exp{pes}", avs)
                        b.consumed(f"vring{pks}", avs)
                b.produced("PC", avs)
                pend_tail[0] = (h, avs)
            emit_tail()
            emit_div(H - 1)
        def outproj():
            for j in range(M):
                sl = pslab_i[0] % 4
                pslab_i[0] += 1
                m_sl = b.dma("gpsimd", pr_s(sl), WO[j, :, :],
                             waits=b.w(f"pr{sl}"), sem=f"s_pr{sl}")
                b.produced(f"pr{sl}", m_sl)
                pk = "PA" if j % 2 == 0 else "PB"
                P = PS[pk]
                pm = None
                for m in range(M):
                    for bi, (c0, nb) in enumerate(NBLK):
                        wv_ = [m_sl] + b.r(f"qT{m}")
                        if m == 0 and bi == 0:
                            wv_ += b.w(pk)
                        pm = b.mm(P[:, c0:c0 + nb],
                                  pr_s(sl)[:, m * 128:(m + 1) * 128],
                                  qT_c(m)[:, c0:c0 + nb],
                                  start=(m == 0), stop=(m == M - 1), waits=wv_,
                                  inc=(m == M - 1 and bi == len(NBLK) - 1))
                        b.consumed(f"qT{m}", pm)
                        b.consumed(f"pr{sl}", pm)
                b.produced(pk, pm)
                m_x = b.dve(lambda v, j=j, P=P: v.scalar_tensor_tensor(
                    xT_c(j), P[:, :], bo_c(j), xT_c(j), ALU.add, ALU.add),
                    waits=[pm, m_bcl] + b.w(f"xT{j}"))
                b.consumed(pk, m_x)
                b.produced(f"xT{j}", m_x)

        WRK = ["wring0", "wring1"]
        PRK = ["pr0", "pr1", "pr2", "pr3"]

        def hoist_wgu():
            ms = []
            for c in range(2):
                sl = wslab_i[0] % 2
                wslab_i[0] += 1
                ms.append(b.dma("gpsimd", wring_s(sl), WGU[0, c, :, :],
                                waits=b.w(f"wring{sl}"), sem=f"s_wr{sl}"))
            wslab_i[0] -= 2
            return ms

        h1 = hoist_wgu()
        layernorm(with_rms=True)
        b.fence(LNK, FFNK)
        ffn("f1", hoisted=h1)
        b.fence(FFNK, LNK)
        b.fence(WRK, PRK)
        layernorm(with_rms=False)
        b.fence(LNK, ATTK)
        m_ccK, m_ccV = qkv_and_ag()
        attention(m_ccK, m_ccV)
        outproj()
        b.fence(ATTK, LNK)
        b.fence(PRK, WRK)
        h2 = hoist_wgu()
        layernorm(with_rms=True)
        b.fence(LNK, FFNK)
        ffn("f2", final_store=True, hoisted=h2)
        b.emit("sync", lambda e: e.nop(), waits=[out_stores[-1]])

        block = ctx.enter_context(nc.Block())
        b.realize(block)
    return nc



def _host_layout(inputs, cfg: CFG):
    D, M, F, E, NE, CF = cfg.D, cfg.M, cfg.F, cfg.E, cfg.NE, cfg.CF
    x = np.asarray(inputs["L0_attn_hidden_states"], np.float32)[0]
    wq = np.asarray(inputs["L0_attn_wq"], np.float32)
    wk = np.asarray(inputs["L0_attn_wk"], np.float32)
    wv = np.asarray(inputs["L0_attn_wv"], np.float32)
    wo = np.asarray(inputs["L0_attn_wo"], np.float32)

    def wslabs(w):
        wt = np.ascontiguousarray(w.T)
        a = wt.reshape(M, 128, M, 128)
        return np.ascontiguousarray(a.transpose(2, 1, 0, 3)).reshape(M, 128, D)

    def bcol(v):
        return np.ascontiguousarray(np.asarray(v, np.float32).reshape(M, 128).T)

    bq = bcol(inputs["L0_attn_bq"]); bk = bcol(inputs["L0_attn_bk"])
    bv = bcol(inputs["L0_attn_bv"]); bo = bcol(inputs["L0_attn_bo"])
    rms = bcol(inputs["L0_ffn_rms_weight"])
    bcols = np.ascontiguousarray(np.concatenate([bq, bk, bv, bo, rms], axis=1))

    rw = np.asarray(inputs["L0_ffn_router_w"], np.float32)
    rw_s = np.ascontiguousarray(rw.reshape(M, 128, 8).transpose(1, 0, 2)
                                .reshape(128, M * 8))

    wg = np.asarray(inputs["L0_ffn_w_gate"], np.float32)
    wu = np.asarray(inputs["L0_ffn_w_up"], np.float32)
    wd = np.asarray(inputs["L0_ffn_w_down"], np.float32)
    swg = np.asarray(inputs["L0_ffn_sw_gate"], np.float32)
    swu = np.asarray(inputs["L0_ffn_sw_up"], np.float32)
    swd = np.asarray(inputs["L0_ffn_sw_down"], np.float32)

    def gu_slabs(g, u):
        def t(w):
            return w.reshape(M, 128, CF, 128).transpose(2, 1, 0, 3)
        out = np.stack([t(g), t(u)], axis=2)
        return out.reshape(CF, 128, 2 * D)

    wgu = np.empty((NE, CF, 128, 2 * D), np.float32)
    for e in range(E):
        wgu[e] = gu_slabs(wg[e], wu[e])
    wgu[E] = gu_slabs(swg, swu)
    wgu = np.ascontiguousarray(wgu)

    def d_slabs(w):
        return (w.reshape(CF, 128, M, 128).transpose(2, 1, 0, 3)
                .reshape(M, 128, F))

    wds = np.empty((NE, M, 128, F), np.float32)
    for e in range(E):
        wds[e] = d_slabs(wd[e])
    wds[E] = d_slabs(swd)
    wds = np.ascontiguousarray(wds)

    shared = dict(wq=wslabs(wq), wk=wslabs(wk), wv=wslabs(wv), wo=wslabs(wo),
                  bcols=bcols, rw=rw_s, wgu=wgu, wd=wds)
    in_maps = []
    for r in range(cfg.R):
        m = dict(shared)
        m["xt"] = np.ascontiguousarray(x[r * cfg.T:(r + 1) * cfg.T, :].T)
        in_maps.append(m)
    return in_maps


def kernel(**inputs):
    cfg = CFG()
    nc = build(cfg)
    in_maps = _host_layout(inputs, cfg)
    res = run_bass_kernel_spmd(nc, in_maps, core_ids=list(range(cfg.R)))
    out = np.empty((1, cfg.S, cfg.D), np.float32)
    for r in range(cfg.R):
        out[0, r * cfg.T:(r + 1) * cfg.T, :] = res.results[r]["outT"].T
    return out


# version 13
# speedup vs baseline: 1.0148x; 1.0023x over previous
import numpy as np
import ml_dtypes
from contextlib import ExitStack
import concourse.bass as bass
import concourse.mybir as mybir
import concourse.bass_utils as _bass_utils
from concourse.bass_utils import run_bass_kernel_spmd

if not getattr(_bass_utils, "_ldw_opt_patched", False):
    _orig_run_command = _bass_utils.run_command

    def _run_command_ldw(cmd, *a, **kw):
        if isinstance(cmd, list):
            cmd = ["--enable-ldw-opt=true" if c == "--enable-ldw-opt=false"
                   else c for c in cmd]
        return _orig_run_command(cmd, *a, **kw)

    _bass_utils.run_command = _run_command_ldw
    _bass_utils._ldw_opt_patched = True

F32 = mybir.dt.float32
BF16 = mybir.dt.bfloat16
AF = mybir.ActivationFunctionType
ALU = mybir.AluOpType
AXX = mybir.AxisListType.X

EXP_SCALE = float(np.float32(1.0 / np.sqrt(128.0)))
EXP_SHIFT = -10.0


class CFG:
    def __init__(self, R=8, T=1024, D=2048, F=1024, E=8):
        self.R, self.T, self.D, self.F, self.E = R, T, D, F, E
        self.M = D // 128
        self.H = D // 128
        self.CF = F // 128
        self.TS = T // 128
        self.NE = E + 1
        self.S = R * T
        assert T % 128 == 0 and D % 128 == 0 and F % 128 == 0

    def nblk(self):
        out, c = [], 0
        while c < self.T:
            n = min(512, self.T - c)
            out.append((c, n))
            c += n
        return out


class Builder:
    ENGINES = ("sync", "gpsimd", "tensor", "vector", "scalar")

    def __init__(self, cfg: CFG):
        self.cfg = cfg
        self.nc = bass.Bass("TRN2", target_bir_lowering=False)
        self.steps = []
        self.sems = {}
        self.planned = {}
        for s in ("s_sync", "s_gp", "s_gpe", "s_pe", "s_act", "s_dve", "s_cc",
                  "s_pre", "s_wr0", "s_wr1", "s_wr2", "s_wd0", "s_wd1", "s_wd2",
                  "s_kr0", "s_kr1", "s_vr0", "s_vr1", "s_kv0", "s_kv1", "s_kv2",
                  "s_kv3", "s_bcA", "s_bcB",
                  "s_rec0", "s_rec1", "s_wbc0", "s_wbc1", "s_wdram", "s_out", "s_cc2",
                  "s_pr0", "s_pr1", "s_pr2", "s_pr3", "s_x0", "s_x1",
                  "s_pre2"):
            self.sems[s] = self.nc.alloc_semaphore(s)
            self.planned[s] = 0
        self.rdy = {}
        self.frees = {}
        self.pe_pending = set()
        self.last_tensor_step = None

    def emit(self, eng, fn, waits=(), inc=None):
        assert eng in self.ENGINES
        w = [x for x in waits if x is not None]
        mi = None
        if inc is not None:
            self.planned[inc[0]] += inc[1]
            mi = (inc[0], self.planned[inc[0]])
        self.steps.append((eng, fn, tuple(w), inc))
        if eng == "tensor":
            self.last_tensor_step = len(self.steps) - 1
            if mi is not None and self.pe_pending:
                for key in self.pe_pending:
                    self.consumed(key, mi)
                self.pe_pending.clear()
        return mi

    def dma(self, eng, out_ap, in_ap, waits=(), sem=None):
        assert sem is not None
        return self.emit(eng, lambda e, o=out_ap, i=in_ap: e.dma_start(out=o, in_=i),
                         waits=waits, inc=(sem, 16))

    def mm(self, out_ap, lhsT, rhs, start, stop, waits=(), inc=False):
        return self.emit(
            "tensor",
            lambda t, o=out_ap, l=lhsT, r=rhs, a=start, b=stop:
                t.matmul(o, l, r, start=a, stop=b),
            waits=waits, inc=("s_pe", 1) if inc else None)

    def act(self, fn, waits=()):
        return self.emit("scalar", fn, waits=waits, inc=("s_act", 1))

    def dve(self, fn, waits=()):
        return self.emit("vector", fn, waits=waits, inc=("s_dve", 1))

    def gpe(self, fn, waits=()):
        return self.emit("gpsimd", fn, waits=waits, inc=("s_gpe", 1))

    def produced(self, key, m):
        self.rdy[key] = m
        self.frees[key] = {}

    def consumed(self, key, m):
        if m is None:
            self.pe_pending.add(key)
            return
        d = self.frees.setdefault(key, {})
        s, v = m
        if d.get(s, 0) < v:
            d[s] = v

    def r(self, key):
        m = self.rdy.get(key)
        return [m] if m is not None else []

    def _pe_flush(self):
        if not self.pe_pending:
            return
        i = self.last_tensor_step
        eng, fn, wts, inc = self.steps[i]
        assert eng == "tensor"
        if inc is None:
            self.planned["s_pe"] += 1
            inc = ("s_pe", 1)
            self.steps[i] = (eng, fn, wts, inc)
        mi = ("s_pe", self.planned["s_pe"])
        for key in self.pe_pending:
            self.consumed(key, mi)
        self.pe_pending.clear()

    def fence(self, from_keys, to_keys):
        acc = {}
        for k in from_keys:
            for (sn, v) in self.w(k):
                if acc.get(sn, 0) < v:
                    acc[sn] = v
        for k in to_keys:
            d = self.frees.setdefault(k, {})
            for sn, v in acc.items():
                if d.get(sn, 0) < v:
                    d[sn] = v

    def w(self, key):
        if key in self.pe_pending:
            self._pe_flush()
        out = list(self.frees.get(key, {}).items())
        m = self.rdy.get(key)
        if m is not None:
            out.append(m)
        return out

    def realize(self, block):
        sems, steps = self.sems, self.steps

        def mk(eng_name):
            def run(e):
                hwm = {}
                for eng, fn, waits, inc in steps:
                    if eng != eng_name:
                        continue
                    for (sn, val) in waits:
                        if hwm.get(sn, 0) < val:
                            e.wait_ge(sems[sn], val)
                            hwm[sn] = val
                    ins = fn(e)
                    if inc is not None:
                        ins.then_inc(sems[inc[0]], inc[1])
            return run

        block.sync(mk('sync'))
        block.gpsimd(mk('gpsimd'))
        block.tensor(mk('tensor'))
        block.vector(mk('vector'))
        block.scalar(mk('scalar'))


def build(cfg: CFG):
    b = Builder(cfg)
    nc = b.nc
    T, D, M, H, CF, TS, NE, E, R, F = (cfg.T, cfg.D, cfg.M, cfg.H, cfg.CF,
                                       cfg.TS, cfg.NE, cfg.E, cfg.R, cfg.F)
    NBLK = cfg.nblk()

    XT = nc.dram_tensor("xt", [D, T], F32, kind="ExternalInput")
    WQ = nc.dram_tensor("wq", [M, 128, D], F32, kind="ExternalInput")
    WK = nc.dram_tensor("wk", [M, 128, D], F32, kind="ExternalInput")
    WV = nc.dram_tensor("wv", [M, 128, D], F32, kind="ExternalInput")
    WO = nc.dram_tensor("wo", [M, 128, D], F32, kind="ExternalInput")
    BCOLS = nc.dram_tensor("bcols", [128, 5 * M], F32, kind="ExternalInput")
    RW = nc.dram_tensor("rw", [128, M * 8], F32, kind="ExternalInput")
    WGU = nc.dram_tensor("wgu", [NE, CF, 128, 2 * D], F32, kind="ExternalInput")
    WD = nc.dram_tensor("wd", [NE, M, 128, F], F32, kind="ExternalInput")
    OUT = nc.dram_tensor("outT", [D, T], F32, kind="ExternalOutput")

    k_in = nc.dram_tensor("k_in", [D * T // 1024, 1024], BF16, kind="Internal")
    v_in = nc.dram_tensor("v_in", [D * T // 1024, 1024], BF16, kind="Internal")
    ag_space = "Shared" if R > 4 else "Local"
    ag_k = nc.dram_tensor("ag_k", [R * D * T // 1024, 1024], BF16,
                          kind="Internal", addr_space=ag_space)
    ag_v = nc.dram_tensor("ag_v", [R * D * T // 1024, 1024], BF16,
                          kind="Internal", addr_space=ag_space)
    bc_dram = nc.dram_tensor("bc_dram", [4, T], F32, kind="Internal")
    w_dram = nc.dram_tensor("w_dram", [8, T], BF16, kind="Internal")

    cvals = [0.0, 1.0, -1.0, 0.5, 1e-5, 1e-6, EXP_SCALE, EXP_SHIFT]
    CPOOL_D = nc.inline_tensor(np.array([cvals], np.float32), name="cpool_d")
    ONESC_D = nc.inline_tensor(np.ones((128, 1), ml_dtypes.bfloat16), name="onesc_d")
    ONESF_D = nc.inline_tensor(np.ones((128, 1), np.float32), name="onesf_d")
    ONESR_D = nc.inline_tensor(np.ones((1, 128), ml_dtypes.bfloat16), name="onesr_d")
    IDB_D = nc.inline_tensor(np.eye(128, dtype=ml_dtypes.bfloat16), name="idb_d")

    with ExitStack() as ctx:
        sbt = lambda n, s, d: ctx.enter_context(nc.sbuf_tensor(n, s, d))
        pst = lambda n, s, d: ctx.enter_context(nc.psum_tensor(n, s, d))

        xT = sbt("xT", [128, M * T], F32)
        hT = sbt("hT", [128, M * T], BF16)
        qT = sbt("qT", [128, H * T], BF16)
        lnscr = sbt("lnscr", [128, T], F32)
        sq = sbt("sq", [128, 2 * T], BF16)
        cpool = sbt("cpool", [128, len(cvals)], F32)
        onescol = sbt("onescol", [128, 1], BF16)
        onescolf = sbt("onescolf", [128, 1], F32)
        identb = sbt("identb", [128, 128], BF16)
        bcols = sbt("bcols_sb", [128, 5 * M], F32)
        rw_sb = sbt("rw_sb", [128, M * 8], BF16)
        stats = sbt("stats", [1, 4 * T], F32)
        e64 = sbt("e64", [128, 64], F32)
        onesrow = sbt("onesrow", [1, 128], BF16)
        k64 = sbt("k64", [128, 64], F32)
        m8 = sbt("m8", [128, 2 * 8], F32)
        w_sb = sbt("w_sb", [128, 64], BF16)
        wT_bf = sbt("wT_bf", [8, T], BF16)
        WSLOT = 2 * D
        wring = sbt("wring", [128, 2 * WSLOT], BF16)
        AR = max(8 * T,
                 2 * CF * T + 6 * T + 6 * F,
                 36 * T)
        arena = sbt("arena", [128, AR], mybir.dt.uint8)
        ar_f32 = arena.ap().bitcast(F32)
        ar_bf = arena.ap().bitcast(BF16)
        nb2 = AR // 2
        nf4 = AR // 4
        A2bc = ar_bf[:, 0:T]
        B2bc = ar_bf[:, T:2 * T]
        hu_v = ar_bf[:, 0:CF * T]
        wbc_v = ar_bf[:, CF * T:CF * T + 3 * T]
        wdr_v = ar_bf[:, CF * T + 3 * T:CF * T + 3 * T + 3 * F]
        kring_v = ar_bf[:, 0:2 * T]
        vring_v = ar_bf[:, 2 * T:4 * T]
        expb_v = ar_bf[:, 4 * T:8 * T]
        evr_v = ar_bf[:, 8 * T:12 * T]
        recbc_v = ar_f32[:, 6 * T:8 * T]
        acc_v = ar_f32[:, 8 * T:9 * T]

        PA = pst("PA", [128, T], F32)
        PB = pst("PB", [128, T], F32)
        PC = pst("PC", [128, T], F32)
        PD = pst("PD", [128, T], F32)
        PS = {"PA": PA, "PB": PB, "PC": PC, "PD": PD}

        for i, v in enumerate(cvals):
            nc.const_aps.register(F32, v, cpool[:, i:i + 1])

        xT_c = lambda c: xT[:, c * T:(c + 1) * T]
        hT_c = lambda c: hT[:, c * T:(c + 1) * T]
        qT_c = lambda c: qT[:, c * T:(c + 1) * T]
        sq_s = lambda s: sq[:, s * T:(s + 1) * T]
        wring_s = lambda s, n=None: wring[:, s * WSLOT:s * WSLOT + (n or WSLOT)]
        wdr_s = lambda s: wdr_v[:, s * F:(s + 1) * F]
        hu_c = lambda c: hu_v[:, c * T:(c + 1) * T]
        wbc_s = lambda s: wbc_v[:, s * T:(s + 1) * T]
        kring_s = lambda s: kring_v[:, s * T:(s + 1) * T]
        vring_s = lambda s: vring_v[:, s * T:(s + 1) * T]
        expb_s = lambda s: expb_v[:, s * T:(s + 1) * T]
        evr_s = lambda s: evr_v[:, s * T:(s + 1) * T]
        recbc_s = lambda s: recbc_v[:, s * T:(s + 1) * T]
        rrow_s = lambda s: stats[0:1, s * T:(s + 1) * T]
        lnscr_bf = lnscr.ap().bitcast(BF16)[:, 0:T]
        LNK = ["A2bc", "B2bc"]
        FFNK = ([f"hu{c}" for c in range(CF)] + ["wbc0", "wbc1", "wbc2"]
                + [f"wdr{i}" for i in range(3)])
        ATTK = (["kring0", "kring1", "vring0", "vring1"]
                + [f"exp{i}" for i in range(4)] + [f"evr{i}" for i in range(4)]
                + ["recbc0", "recbc1", "accL", "accR"])

        bq_c = lambda c: bcols[:, 0 * M + c:0 * M + c + 1]
        bk_c = lambda c: bcols[:, 1 * M + c:1 * M + c + 1]
        bv_c = lambda c: bcols[:, 2 * M + c:2 * M + c + 1]
        bo_c = lambda c: bcols[:, 3 * M + c:3 * M + c + 1]
        rms_c = lambda c: bcols[:, 4 * M + c:4 * M + c + 1]

        kf = k_in.ap().rearrange("a b -> (a b)")
        vf = v_in.ap().rearrange("a b -> (a b)")
        agkf = ag_k.ap().rearrange("a b -> (a b)")
        agvf = ag_v.ap().rearrange("a b -> (a b)")

        def k_stage_ap(h):
            return kf[h * 128 * T:(h * 128 + 128) * T].rearrange(
                "(p k) -> p k", k=T)

        def v_stage_ap(h):
            vv = vf.rearrange("(k d) -> k d", d=D)
            return vv.rearrange("(s p) d -> p s d", p=128)[:, :, h * 128:(h + 1) * 128]

        def k_read_ap(r, h):
            base = r * D * T
            return agkf[base + h * 128 * T:base + (h * 128 + 128) * T].rearrange(
                "(p k) -> p k", k=T)

        def v_read_ap(r, h):
            base = r * D * T
            vv = agvf[base:base + D * T].rearrange("(k d) -> k d", d=D)
            return vv.rearrange("(c p) d -> p c d", p=128)[:, :, h * 128:(h + 1) * 128]

        m_cp = b.dma("sync", cpool[:, :],
                     CPOOL_D[:, :].to_broadcast([128, len(cvals)]), sem="s_pre")
        m_oc = b.dma("sync", onescol[:, :], ONESC_D[:, :], sem="s_pre")
        b.dma("sync", onescolf[:, :], ONESF_D[:, :], sem="s_pre")
        b.dma("sync", onesrow[:, :], ONESR_D[:, :], sem="s_pre")
        m_idb = b.dma("sync", identb[:, :], IDB_D[:, :], sem="s_pre")
        m_bcl = b.dma("sync", bcols[:, :], BCOLS[:, :], sem="s_pre")
        m_xt0 = b.dma("sync", xT[:, 0:(M // 2) * T],
                      XT[0:D // 2, :].rearrange("(m p) t -> p m t", p=128),
                      sem="s_x0")
        m_xt1 = b.dma("sync", xT[:, (M // 2) * T:],
                      XT[D // 2:D, :].rearrange("(m p) t -> p m t", p=128),
                      sem="s_x1")
        m_rw = b.dma("gpsimd", rw_sb[:, :], RW[:, :], sem="s_pre2")
        m_all = ("s_pre", b.planned["s_pre"])
        m_cp = m_oc = m_idb = m_bcl = m_all
        for c in range(M):
            b.produced(f"xT{c}", m_xt0 if c < M // 2 else m_xt1)
        b.produced("cpool", m_cp)

        def strow(i):
            return stats[0:1, i * T:(i + 1) * T]

        def layernorm(with_rms):
            cast_m = {}
            sum_m = suma_m = None
            for c in range(M):
                cm = b.dve(lambda v, c=c: v.tensor_copy(hT_c(c), xT_c(c)),
                           waits=b.r(f"xT{c}") + b.w(f"hT{c}"))
                b.consumed(f"xT{c}", cm)
                cast_m[c] = cm
                sl = c % 2
                sm = b.act(lambda s, c=c, sl=sl: s.activation(
                    sq_s(sl), xT_c(c), AF.Square),
                    waits=b.r(f"xT{c}") + b.w(f"sq{sl}") + b.r("cpool"))
                b.consumed(f"xT{c}", sm)
                b.produced(f"sq{sl}", sm)
                for bi, (c0, nb) in enumerate(NBLK):
                    last = (c == M - 1) and (bi == len(NBLK) - 1)
                    wa = [cm, m_oc] + (b.w("PA") if c == 0 and bi == 0 else [])
                    ma = b.mm(PA[0:1, c0:c0 + nb], onescol[:, :],
                              hT_c(c)[:, c0:c0 + nb],
                              start=(c == 0), stop=(c == M - 1), waits=wa,
                              inc=last)
                    b.consumed(f"hT{c}", ma)
                    wb = [sm, m_oc] + (b.w("PB") if c == 0 and bi == 0 else [])
                    mb = b.mm(PB[0:1, c0:c0 + nb], onescol[:, :],
                              sq_s(sl)[:, c0:c0 + nb],
                              start=(c == 0), stop=(c == M - 1), waits=wb,
                              inc=last)
                    b.consumed(f"sq{sl}", mb)
                    if last:
                        sum_m, suma_m = mb, ma
            b.produced("PA", suma_m)
            b.produced("PB", sum_m)
            r0, r1, r2, r3 = strow(0), strow(1), strow(2), strow(3)
            m_mu = b.dve(lambda v: v.tensor_scalar_mul(r0, PA[0:1, :], 1.0 / D),
                         waits=[suma_m, sum_m] + b.w("st0"))
            b.produced("st0", m_mu)
            b.consumed("PA", m_mu)
            m_e2 = b.dve(lambda v: v.tensor_scalar_mul(r1, PB[0:1, :], 1.0 / D),
                         waits=[sum_m] + b.w("st1"))
            b.produced("st1", m_e2)
            b.consumed("PB", m_e2)
            m_m2 = b.dve(lambda v: v.tensor_mul(r2, r0, r0),
                         waits=[m_mu] + b.w("st2"))
            b.produced("st2", m_m2)
            b.consumed("st0", m_m2)
            m_v = b.dve(lambda v: v.tensor_sub(r1, r1, r2),
                        waits=[m_e2, m_m2] + b.w("st1"))
            b.produced("st1", m_v)
            b.consumed("st2", m_v)
            m_sd = b.act(lambda sc: sc.activation(r2, r1, AF.Sqrt, bias=1e-5,
                                                  scale=1.0),
                         waits=[m_v] + b.r("cpool") + b.w("st2"))
            b.produced("st2", m_sd)
            b.consumed("st1", m_sd)
            m_rst = b.dve(lambda v: v.reciprocal(r2, r2),
                          waits=[m_sd] + b.w("st2"))
            b.produced("st2", m_rst)
            if with_rms:
                m_q = b.dve(lambda v: v.tensor_mul(r3, r2, r2),
                            waits=[m_rst] + b.w("st3"))
                b.produced("st3", m_q)
                b.consumed("st2", m_q)
                m_vr = b.dve(lambda v: v.tensor_mul(r3, r1, r3),
                             waits=[m_q, m_v] + b.w("st3"))
                b.produced("st3", m_vr)
                b.consumed("st1", m_vr)
                m_sr = b.act(lambda sc: sc.activation(r3, r3, AF.Sqrt, bias=1e-6,
                                                      scale=1.0),
                             waits=[m_vr] + b.r("cpool") + b.w("st3"))
                b.produced("st3", m_sr)
                m_rmr = b.dve(lambda v: v.reciprocal(r3, r3),
                              waits=[m_sr] + b.w("st3"))
                b.produced("st3", m_rmr)
                m_a2f = b.dve(lambda v: v.tensor_mul(r3, r2, r3),
                              waits=[m_rmr, m_rst] + b.w("st3"))
                b.produced("st3", m_a2f)
                b.consumed("st2", m_a2f)
                a2row, k_a2row = r3, "st3"
            else:
                a2row, k_a2row, m_a2f = r2, "st2", m_rst
            m_b2 = b.dve(lambda v: v.scalar_tensor_tensor(
                r0, r0, -1.0, a2row, ALU.mult, ALU.mult),
                waits=[m_mu, m_a2f] + b.w("st0"))
            b.produced("st0", m_b2)
            b.consumed(k_a2row, m_b2)
            b2, k_b2 = r0, "st0"
            a2bf = sq[0:1, 0:T]
            b2bf = sq[0:1, T:2 * T]
            m_cA = b.dve(lambda v: v.tensor_copy(a2bf, a2row),
                         waits=[m_a2f, m_b2] + b.w("sq0"))
            b.produced("sq0", m_cA)
            b.consumed(k_a2row, m_cA)
            m_cB = b.dve(lambda v: v.tensor_copy(b2bf, b2),
                         waits=[m_b2] + b.w("sq1"))
            b.produced("sq1", m_cB)
            b.consumed(k_b2, m_cB)
            mA = mB = None
            for bi, (c0, nb) in enumerate(NBLK):
                wv_ = [m_cA, m_all] + (b.w("PA") if bi == 0 else [])
                mA = b.mm(PA[:, c0:c0 + nb], onesrow[:, :], a2bf[:, c0:c0 + nb],
                          start=True, stop=True, waits=wv_,
                          inc=(bi == len(NBLK) - 1))
                b.consumed("sq0", mA)
                wv_ = [m_cB, m_all] + (b.w("PB") if bi == 0 else [])
                mB = b.mm(PB[:, c0:c0 + nb], onesrow[:, :], b2bf[:, c0:c0 + nb],
                          start=True, stop=True, waits=wv_,
                          inc=(bi == len(NBLK) - 1))
                b.consumed("sq1", mB)
            b.produced("PA", mA)
            b.produced("PB", mB)
            m_iA = b.act(lambda sc: sc.activation(A2bc, PA[:, :], AF.Identity,
                                                  bias=0.0, scale=1.0),
                         waits=[mA] + b.w("A2bc") + b.r("cpool"))
            b.consumed("PA", m_iA)
            b.produced("A2bc", m_iA)
            m_iB = b.act(lambda sc: sc.activation(B2bc, PB[:, :], AF.Identity,
                                                  bias=0.0, scale=1.0),
                         waits=[mB] + b.w("B2bc") + b.r("cpool"))
            b.consumed("PB", m_iB)
            b.produced("B2bc", m_iB)
            scr_bf = lnscr.ap().bitcast(BF16)[:, 0:T]
            for c in range(M):
                wv_ = ([cast_m[c], m_iA, sum_m, suma_m] + b.w("lnscr"))
                m_t1 = b.dve(lambda v, c=c: v.tensor_mul(
                    scr_bf, hT_c(c), A2bc), waits=wv_)
                b.produced("lnscr", m_t1)
                b.consumed("A2bc", m_t1)
                b.consumed(f"hT{c}", m_t1)
                if with_rms:
                    m_t2 = b.dve(lambda v: v.tensor_add(
                        scr_bf, scr_bf, B2bc), waits=[m_t1, m_iB])
                    b.produced("lnscr", m_t2)
                    b.consumed("B2bc", m_t2)
                    m_h = b.dve(lambda v, c=c: v.tensor_scalar_mul(
                        hT_c(c), scr_bf, rms_c(c)),
                        waits=[m_t2, m_bcl] + b.w(f"hT{c}"))
                    b.consumed("lnscr", m_h)
                else:
                    m_h = b.dve(lambda v, c=c: v.tensor_add(
                        hT_c(c), scr_bf, B2bc),
                        waits=[m_t1, m_iB] + b.w(f"hT{c}"))
                    b.consumed("B2bc", m_h)
                    b.consumed("lnscr", m_h)
                b.produced(f"hT{c}", m_h)

        def router():
            NS = TS
            lg = None
            for sub in range(NS):
                for m in range(M):
                    wv_ = b.r(f"hT{m}") + [m_rw]
                    if sub == 0 and m == 0:
                        wv_ += b.w("PC")
                    lg = b.mm(PC[:, 8 * sub:8 * sub + 8],
                              hT_c(m)[:, sub * 128:(sub + 1) * 128],
                              rw_sb[:, m * 8:(m + 1) * 8],
                              start=(sub == 0 and m == 0),
                              stop=(sub == NS - 1 and m == M - 1), waits=wv_,
                              inc=(sub == NS - 1 and m == M - 1))
                    b.consumed(f"hT{m}", lg)
            b.produced("PC", lg)
            W8 = 8 * NS
            lv = k64[:, 0:W8]
            ev = e64[:, 0:W8]
            kv = k64[:, 0:W8]
            l3 = lambda ap: ap.rearrange("p (s e) -> p s e", e=8)
            bc = lambda ap8: ap8[:, :, None].to_broadcast([128, NS, 8])
            m1 = m8[:, 0:NS]
            m1e_ = m8[:, 0:NS]
            m2 = m8[:, 8:8 + NS]
            den = m8[:, 0:NS]
            m_ev0 = b.act(lambda sc: sc.activation(lv, PC[:, 0:W8], AF.Identity,
                                                   bias=0.0, scale=1.0),
                          waits=[lg] + b.w("k64") + b.r("cpool"))
            b.consumed("PC", m_ev0)
            b.produced("k64", m_ev0)
            m_m1 = b.dve(lambda v: v.reduce_max(
                m1, l3(lv), axis=AXX), waits=[m_ev0] + b.w("m8"))
            m_ms = b.dve(lambda v: v.tensor_sub(l3(lv), l3(lv), bc(m1)),
                         waits=[m_m1] + b.w("k64"))
            b.produced("k64", m_ms)
            m_e = b.act(lambda sc: sc.activation(ev, lv, AF.Exp),
                        waits=[m_ms] + b.w("e64") + b.r("cpool"))
            b.consumed("k64", m_e)
            b.produced("e64", m_e)
            m_em = b.dve(lambda v: v.reduce_max(m1e_, l3(ev), axis=AXX),
                         waits=[m_e, m_ms, m_m1])
            m_k1 = b.dve(lambda v: v.tensor_tensor(l3(kv), l3(ev), bc(m1e_),
                                                   ALU.is_ge),
                         waits=[m_em] + b.w("k64"))
            m_k1m = b.dve(lambda v: v.tensor_mul(l3(kv), l3(kv), bc(m1e_)),
                          waits=[m_k1])
            m_e2 = b.dve(lambda v: v.tensor_sub(kv, ev, kv), waits=[m_k1m])
            m_m2 = b.dve(lambda v: v.reduce_max(m2, l3(kv), axis=AXX),
                         waits=[m_e2])
            m_kp = b.dve(lambda v: v.tensor_tensor(l3(kv), l3(ev), bc(m2),
                                                   ALU.is_ge), waits=[m_m2])
            m_wr = b.dve(lambda v: v.tensor_mul(kv, ev, kv), waits=[m_kp])
            b.consumed("e64", m_wr)
            m_dn = b.dve(lambda v: v.tensor_add(den, m1e_, m2),
                         waits=[m_m2, m_em])
            m_rc = b.dve(lambda v: v.reciprocal(den, den), waits=[m_dn])
            m_rh = b.dve(lambda v: v.tensor_scalar_mul(den, den, 0.5),
                         waits=[m_rc])
            m_w = b.dve(lambda v: v.tensor_mul(l3(w_sb[:, 0:W8]), l3(kv),
                                               bc(den)),
                        waits=[m_rh, m_wr] + b.w("w_sb"))
            b.produced("w_sb", m_w)
            b.produced("m8", m_w)
            b.produced("k64", m_w)
            tr_ms = []
            PDw = PD.ap().bitcast(BF16)
            for sub in range(NS):
                wv_ = [m_w, m_idb]
                if sub == 0:
                    wv_ += b.w("PD")
                first_in_bank = (sub * 128) % 1024 == 0
                last_in_bank = ((sub + 1) * 128) % 1024 == 0 or sub == NS - 1
                tr = b.emit("tensor",
                            lambda t, sub=sub, a=first_in_bank, z=last_in_bank:
                                t.matmul(PDw[0:8, sub * 128:(sub + 1) * 128],
                                         w_sb[:, 8 * sub:8 * sub + 8],
                                         identb[:, :],
                                         start=a, stop=z, is_transpose=True),
                            waits=wv_, inc=("s_pe", 1))
                b.consumed("w_sb", tr)
                tr_ms.append(tr)
            b.produced("PD", tr_ms[-1])
            m_ev = b.act(lambda sc: sc.activation(wT_bf[:, :], PDw[0:8, 0:T],
                                                  AF.Identity, bias=0.0,
                                                  scale=1.0),
                         waits=tr_ms + b.w("wT_bf") + b.r("cpool"))
            b.consumed("PD", m_ev)
            b.produced("wT_bf", m_ev)
            m_st = b.dma("sync", w_dram[0:8, :], wT_bf[:, :],
                         waits=[m_ev] + b.w("w_dram"), sem="s_wdram")
            b.produced("w_dram", m_st)
            b.consumed("wT_bf", m_st)
            return m_st

        wslab_i = [0]
        pslab_i = [0]
        wd_i = [0]

        def ffn(tag, final_store=False, hoisted=None):
            m_route = router()
            m_w8 = b.emit("gpsimd", lambda g: g.memset(wbc_s(2), 0.5),
                          waits=b.w("wbc2"), inc=("s_gpe", 1))
            b.produced("wbc2", m_w8)
            for e in range(NE):
                if e < E:
                    ws = e % 2
                    wkey = f"wbc{ws}"
                    m_bc = b.dma("scalar", wbc_s(ws),
                                 w_dram[e:e + 1, :].to_broadcast([128, T]),
                                 waits=[m_route] + b.w(wkey), sem=f"s_wbc{ws}")
                    b.produced(wkey, m_bc)
                    wbc_ap = wbc_s(ws)
                else:
                    wkey = "wbc2"
                    wbc_ap = wbc_s(2)
                for c in range(CF):
                    sl = wslab_i[0] % 2
                    wslab_i[0] += 1
                    if hoisted is not None and e == 0 and c < len(hoisted):
                        m_sl = hoisted[c]
                    else:
                        m_sl = b.dma("gpsimd", wring_s(sl),
                                     WGU[e, c, :, :], waits=b.w(f"wring{sl}"),
                                     sem=f"s_wr{sl}")
                    b.produced(f"wring{sl}", m_sl)
                    lg = lu = None
                    for m in range(M):
                        for bi, (c0, nb) in enumerate(NBLK):
                            wv_ = [m_sl] + b.r(f"hT{m}")
                            if m == 0 and bi == 0:
                                wv_ += b.w("PA")
                            lg = b.mm(PA[:, c0:c0 + nb],
                                      wring_s(sl)[:, m * 128:(m + 1) * 128],
                                      hT_c(m)[:, c0:c0 + nb],
                                      start=(m == 0), stop=(m == M - 1),
                                      waits=wv_,
                                      inc=(m == M - 1 and bi == len(NBLK) - 1))
                            b.consumed(f"hT{m}", lg)
                            b.consumed(f"wring{sl}", lg)
                    for m in range(M):
                        for bi, (c0, nb) in enumerate(NBLK):
                            wv_ = [m_sl] + b.r(f"hT{m}")
                            if m == 0 and bi == 0:
                                wv_ += b.w("PB")
                            lu = b.mm(PB[:, c0:c0 + nb],
                                      wring_s(sl)[:, D + m * 128:D + (m + 1) * 128],
                                      hT_c(m)[:, c0:c0 + nb],
                                      start=(m == 0), stop=(m == M - 1),
                                      waits=wv_,
                                      inc=(m == M - 1 and bi == len(NBLK) - 1))
                            b.consumed(f"hT{m}", lu)
                            b.consumed(f"wring{sl}", lu)
                    b.produced("PA", lg)
                    b.produced("PB", lu)
                    ssl = c % 2
                    m_si = b.act(lambda s, sl2=ssl: s.activation(
                        sq_s(sl2), PA[:, :], AF.Silu),
                        waits=[lg] + b.w(f"sq{ssl}") + b.r("cpool"))
                    b.consumed("PA", m_si)
                    b.produced(f"sq{ssl}", m_si)
                    m_t = b.dve(lambda v, sl2=ssl, c=c: v.tensor_mul(
                        hu_c(c), sq_s(sl2), PB[:, :]),
                        waits=[m_si, lu] + b.w(f"hu{c}"))
                    b.consumed("PB", m_t)
                    b.consumed(f"sq{ssl}", m_t)
                    b.produced(f"hu{c}", m_t)
                    m_hu = b.dve(lambda v, c=c, wb=wbc_ap: v.tensor_mul(
                        hu_c(c), hu_c(c), wb),
                        waits=[m_t] + b.r(wkey) + b.w(f"hu{c}"))
                    b.consumed(wkey, m_hu)
                    b.produced(f"hu{c}", m_hu)
                for j in range(M):
                    sl = wd_i[0] % 3
                    wd_i[0] += 1
                    m_sl = b.dma("gpsimd", wdr_s(sl), WD[e, j, :, :],
                                 waits=b.w(f"wdr{sl}"), sem=f"s_wd{sl}")
                    b.produced(f"wdr{sl}", m_sl)
                    pk = "PC" if j % 2 == 0 else "PD"
                    P = PS[pk]
                    dn = None
                    for c in range(CF):
                        for bi, (c0, nb) in enumerate(NBLK):
                            wv_ = [m_sl] + b.r(f"hu{c}")
                            if c == 0 and bi == 0:
                                wv_ += b.w(pk)
                            dn = b.mm(P[:, c0:c0 + nb],
                                      wdr_s(sl)[:, c * 128:(c + 1) * 128],
                                      hu_c(c)[:, c0:c0 + nb],
                                      start=(c == 0), stop=(c == CF - 1),
                                      waits=wv_,
                                      inc=(c == CF - 1 and bi == len(NBLK) - 1))
                            b.consumed(f"hu{c}", dn)
                            b.consumed(f"wdr{sl}", dn)
                    b.produced(pk, dn)
                    m_x = b.dve(lambda v, j=j, P=P: v.tensor_add(
                        xT_c(j), xT_c(j), P[:, :]),
                        waits=[dn] + b.w(f"xT{j}"))
                    b.consumed(pk, m_x)
                    b.produced(f"xT{j}", m_x)
                    if final_store and e == NE - 1:
                        mo = b.dma("sync",
                                   OUT.ap().rearrange("(m p) t -> p m t", p=128)
                                   [:, j:j + 1, :],
                                   xT_c(j), waits=[m_x], sem="s_out")
                        b.consumed(f"xT{j}", mo)
                        out_stores.append(mo)

        out_stores = []

        def pr_s(sl):
            return wring[:, sl * D:(sl + 1) * D]

        def hoist_proj(Wdram, n=3):
            ms = []
            for j in range(n):
                sl = pslab_i[0] % 4
                pslab_i[0] += 1
                ms.append(b.dma("gpsimd", pr_s(sl), Wdram[j, :, :],
                                waits=b.w(f"pr{sl}"), sem=f"s_pr{sl}"))
            pslab_i[0] -= n
            return ms

        def proj_feature_major(Wdram, bias_c, out_key_fn, out_ap_fn,
                               hoisted=None):
            for j in range(M):
                sl = pslab_i[0] % 4
                pslab_i[0] += 1
                if hoisted is not None and j < len(hoisted):
                    m_sl = hoisted[j]
                else:
                    m_sl = b.dma("gpsimd", pr_s(sl), Wdram[j, :, :],
                                 waits=b.w(f"pr{sl}"), sem=f"s_pr{sl}")
                b.produced(f"pr{sl}", m_sl)
                pk = "PA" if j % 2 == 0 else "PB"
                P = PS[pk]
                pm = None
                for m in range(M):
                    for bi, (c0, nb) in enumerate(NBLK):
                        wv_ = [m_sl] + b.r(f"hT{m}")
                        if m == 0 and bi == 0:
                            wv_ += b.w(pk)
                        pm = b.mm(P[:, c0:c0 + nb],
                                  pr_s(sl)[:, m * 128:(m + 1) * 128],
                                  hT_c(m)[:, c0:c0 + nb],
                                  start=(m == 0), stop=(m == M - 1), waits=wv_,
                                  inc=(m == M - 1 and bi == len(NBLK) - 1))
                        b.consumed(f"hT{m}", pm)
                        b.consumed(f"pr{sl}", pm)
                b.produced(pk, pm)
                okey = out_key_fn(j)
                m_ev = b.act(lambda s, j=j, P=P, oap=out_ap_fn(j), bap=bias_c(j):
                             s.activation(oap, P[:, :], AF.Identity, bias=bap,
                                          scale=1.0),
                             waits=[pm, m_bcl] + b.w(okey) + b.r("cpool"))
                b.consumed(pk, m_ev)
                b.produced(okey, m_ev)
                yield j, okey, m_ev

        def qkv_and_ag():
            k_writes, v_writes = [], []
            for j, okey, m_ev in proj_feature_major(
                    WK, bk_c, lambda j: f"evr{j % 2}", lambda j: evr_s(j % 2)):
                m_o = b.dma("sync", k_stage_ap(j), evr_s(j % 2), waits=[m_ev],
                             sem=f"s_kv{j % 2}")
                b.consumed(okey, m_o)
                k_writes.append(m_o)
            hv = hoist_proj(WV, n=3)
            m_ccK = b.emit("gpsimd",
                           lambda g: g.collective_compute(
                               "AllGather", ALU.bypass,
                               replica_groups=[list(range(R))],
                               ins=[k_in.ap().opt()],
                               outs=[ag_k.ap().opt()]),
                           waits=k_writes, inc=("s_cc", 1))
            PDb = PD.ap().bitcast(BF16)
            b.fence(["PD"], ["PDh0", "PDh1"])
            for j, okey, m_ev in proj_feature_major(
                    WV, bv_c, lambda j: f"evr{j % 2}", lambda j: evr_s(j % 2),
                    hoisted=hv):
                hk = f"PDh{j % 2}"
                off = (j % 2) * T
                trs = []
                for s in range(TS):
                    wv_ = [m_ev, m_idb]
                    if s == 0:
                        wv_ += b.w(hk)
                    tr = b.emit("tensor",
                                lambda t, s=s, j=j, off=off: t.matmul(
                                    PDb[:, off + s * 128:off + (s + 1) * 128],
                                    evr_s(j % 2)[:, s * 128:(s + 1) * 128],
                                    identb[:, :],
                                    start=(s == 0), stop=(s == TS - 1),
                                    is_transpose=True),
                                waits=wv_, inc=("s_pe", 1))
                    b.consumed(okey, tr)
                    trs.append(tr)
                b.produced(hk, trs[-1])
                vk = 2 + (j % 2)
                m_vv = b.act(lambda s, vk=vk, off=off: s.activation(
                    evr_s(vk), PDb[:, off:off + T], AF.Identity, bias=0.0,
                    scale=1.0),
                    waits=trs + b.w(f"evr{vk}") + b.r("cpool"))
                b.consumed(hk, m_vv)
                b.produced(f"evr{vk}", m_vv)
                m_o = b.dma("sync", v_stage_ap(j),
                            evr_s(vk).rearrange("p (s d) -> p s d", d=128),
                            waits=[m_vv], sem=f"s_kv{vk}")
                b.consumed(f"evr{vk}", m_o)
                v_writes.append(m_o)
            b.fence(["PDh0", "PDh1"], ["PD"])
            hq = hoist_proj(WQ, n=3)
            m_ccV = b.emit("gpsimd",
                           lambda g: g.collective_compute(
                               "AllGather", ALU.bypass,
                               replica_groups=[list(range(R))],
                               ins=[v_in.ap().opt()],
                               outs=[ag_v.ap().opt()]),
                           waits=v_writes, inc=("s_cc2", 1))
            for _ in proj_feature_major(WQ, bq_c, lambda j: f"qT{j}",
                                        lambda j: qT_c(j), hoisted=hq):
                pass
            return m_ccK, m_ccV

        def attention(m_ccK, m_ccV):
            NKC = R * TS

            def kv_dma(h, r):
                ks = r % 2
                m_k = b.dma("sync", kring_s(ks), k_read_ap(r, h),
                            waits=[m_ccK] + b.w(f"kring{ks}"), sem=f"s_kr{ks}")
                b.produced(f"kring{ks}", m_k)
                m_v = b.dma("sync",
                            vring_s(ks).rearrange("p (c d) -> p c d", d=128),
                            v_read_ap(r, h), waits=[m_ccV] + b.w(f"vring{ks}"),
                            sem=f"s_vr{ks}")
                b.produced(f"vring{ks}", m_v)
                return ks, m_k, m_v

            exp_i = [0]
            pend_tail = [None]

            def emit_div(h):
                rs = h % 2
                m_ri = b.rdy[f"recbc{rs}"]
                m_at = b.rdy[f"qT{h}"]
                m_dv = b.dve(lambda v, h=h, rs=rs: v.tensor_mul(
                    qT_c(h), qT_c(h), recbc_s(rs)),
                    waits=[m_ri, m_at])
                b.consumed(f"recbc{rs}", m_dv)
                b.produced(f"qT{h}", m_dv)

            def emit_tail():
                if pend_tail[0] is None:
                    return
                h, avs = pend_tail[0]
                pend_tail[0] = None
                m_aL = b.rdy["accL"]
                sms = None
                for bi, (c0, nb) in enumerate(NBLK):
                    wv_ = [m_aL, m_oc] + (b.w("PD") if bi == 0 else [])
                    sms = b.mm(PD[0:1, c0:c0 + nb], onescolf[:, :],
                               acc_v[:, c0:c0 + nb],
                               start=True, stop=True, waits=wv_,
                               inc=(bi == len(NBLK) - 1))
                    b.consumed("accL", sms)
                b.produced("PD", sms)
                m_at = b.act(lambda sc, h=h: sc.activation(
                    qT_c(h), PC[:, :], AF.Identity, bias=0.0, scale=1.0),
                    waits=[avs] + b.w(f"qT{h}") + b.r("cpool"))
                b.consumed("PC", m_at)
                b.produced(f"qT{h}", m_at)
                rs = h % 2
                m_rc = b.dve(lambda v, rs=rs: v.reciprocal(rrow_s(rs),
                                                           PD[0:1, :]),
                             waits=[sms] + b.w(f"st{rs}"))
                b.consumed("PD", m_rc)
                b.produced(f"st{rs}", m_rc)
                m_ro = b.dma("gpsimd", bc_dram[2:3, :], rrow_s(rs),
                             waits=[m_rc] + b.w("bcd2"), sem=f"s_rec{rs}")
                b.produced("bcd2", m_ro)
                b.consumed(f"st{rs}", m_ro)
                m_ri = b.dma("gpsimd", recbc_s(rs),
                             bc_dram[2:3, :].to_broadcast([128, T]),
                             waits=[m_ro] + b.w(f"recbc{rs}"), sem=f"s_rec{rs}")
                b.produced(f"recbc{rs}", m_ri)
                b.consumed("bcd2", m_ri)

            for h in range(H):
                pend = {}
                pend[0] = kv_dma(h, 0)
                if R > 1:
                    pend[1] = kv_dma(h, 1)
                avs = None
                av_q = []
                for r in range(R):
                    ks, m_k, m_v = pend.pop(r)
                    for c in range(TS):
                        kc = r * TS + c
                        spk = "PA" if kc % 2 == 0 else "PB"
                        SP = PS[spk]
                        sc_m = None
                        for bi, (c0, nb) in enumerate(NBLK):
                            wv_ = [m_k] + b.r(f"qT{h}") + (b.w(spk) if bi == 0
                                                           else [])
                            sc_m = b.mm(SP[:, c0:c0 + nb],
                                        kring_s(ks)[:, c * 128:(c + 1) * 128],
                                        qT_c(h)[:, c0:c0 + nb],
                                        start=True, stop=True, waits=wv_,
                                        inc=(bi == len(NBLK) - 1))
                            b.consumed(f"qT{h}", sc_m)
                            b.consumed(f"kring{ks}", sc_m)
                        b.produced(spk, sc_m)
                        es = exp_i[0] % 4
                        exp_i[0] += 1
                        m_x = b.act(lambda sc, es=es, SP=SP: sc.activation(
                            expb_s(es), SP[:, :], AF.Exp, bias=EXP_SHIFT,
                            scale=EXP_SCALE),
                            waits=[sc_m] + b.w(f"exp{es}") + b.r("cpool"))
                        b.consumed(spk, m_x)
                        b.produced(f"exp{es}", m_x)
                        if kc == 0:
                            emit_tail()
                        if kc == 0:
                            m_aL = b.dve(lambda v, es=es: v.tensor_copy(
                                acc_v, expb_s(es)),
                                waits=[m_x] + b.w("accL"))
                        else:
                            m_aL = b.dve(lambda v, es=es: v.tensor_add(
                                acc_v, acc_v, expb_s(es)),
                                waits=[m_x] + b.w("accL"))
                        b.produced("accL", m_aL)
                        b.consumed(f"exp{es}", m_aL)
                        av_q.append((es, ks, kc, m_v))
                        if len(av_q) > 2:
                            pes, pks, pkc, pm_v = av_q.pop(0)
                            pc = pkc % TS
                            m_px = b.rdy[f"exp{pes}"]
                            for bi, (c0, nb) in enumerate(NBLK):
                                wv_ = [m_px, pm_v] + (b.w("PC")
                                                      if pkc == 0 and bi == 0
                                                      else [])
                                avs = b.mm(PC[:, c0:c0 + nb],
                                           vring_s(pks)[:, pc * 128:(pc + 1) * 128],
                                           expb_s(pes)[:, c0:c0 + nb],
                                           start=(pkc == 0), stop=False,
                                           waits=wv_,
                                           inc=(bi == len(NBLK) - 1))
                                b.consumed(f"exp{pes}", avs)
                                b.consumed(f"vring{pks}", avs)
                        if h > 0 and r == 0 and c == min(4, TS - 1):
                            emit_div(h - 1)
                    if r + 2 < R:
                        pend[r + 2] = kv_dma(h, r + 2)
                while av_q:
                    pes, pks, pkc, pm_v = av_q.pop(0)
                    pc = pkc % TS
                    m_px = b.rdy[f"exp{pes}"]
                    for bi, (c0, nb) in enumerate(NBLK):
                        wv_ = [m_px, pm_v] + (b.w("PC") if pkc == 0 and bi == 0
                                              else [])
                        avs = b.mm(PC[:, c0:c0 + nb],
                                   vring_s(pks)[:, pc * 128:(pc + 1) * 128],
                                   expb_s(pes)[:, c0:c0 + nb],
                                   start=(pkc == 0), stop=(pkc == NKC - 1),
                                   waits=wv_,
                                   inc=(bi == len(NBLK) - 1))
                        b.consumed(f"exp{pes}", avs)
                        b.consumed(f"vring{pks}", avs)
                b.produced("PC", avs)
                pend_tail[0] = (h, avs)
            emit_tail()
            emit_div(H - 1)
        def outproj(hoisted=None):
            for j in range(M):
                sl = pslab_i[0] % 4
                pslab_i[0] += 1
                if hoisted is not None and j < len(hoisted):
                    m_sl = hoisted[j]
                else:
                    m_sl = b.dma("gpsimd", pr_s(sl), WO[j, :, :],
                                 waits=b.w(f"pr{sl}"), sem=f"s_pr{sl}")
                b.produced(f"pr{sl}", m_sl)
                pk = "PA" if j % 2 == 0 else "PB"
                P = PS[pk]
                pm = None
                for m in range(M):
                    for bi, (c0, nb) in enumerate(NBLK):
                        wv_ = [m_sl] + b.r(f"qT{m}")
                        if m == 0 and bi == 0:
                            wv_ += b.w(pk)
                        pm = b.mm(P[:, c0:c0 + nb],
                                  pr_s(sl)[:, m * 128:(m + 1) * 128],
                                  qT_c(m)[:, c0:c0 + nb],
                                  start=(m == 0), stop=(m == M - 1), waits=wv_,
                                  inc=(m == M - 1 and bi == len(NBLK) - 1))
                        b.consumed(f"qT{m}", pm)
                        b.consumed(f"pr{sl}", pm)
                b.produced(pk, pm)
                m_x = b.dve(lambda v, j=j, P=P: v.scalar_tensor_tensor(
                    xT_c(j), P[:, :], bo_c(j), xT_c(j), ALU.add, ALU.add),
                    waits=[pm, m_bcl] + b.w(f"xT{j}"))
                b.consumed(pk, m_x)
                b.produced(f"xT{j}", m_x)

        WRK = ["wring0", "wring1"]
        PRK = ["pr0", "pr1", "pr2", "pr3"]

        def hoist_wgu():
            ms = []
            for c in range(2):
                sl = wslab_i[0] % 2
                wslab_i[0] += 1
                ms.append(b.dma("gpsimd", wring_s(sl), WGU[0, c, :, :],
                                waits=b.w(f"wring{sl}"), sem=f"s_wr{sl}"))
            wslab_i[0] -= 2
            return ms

        h1 = hoist_wgu()
        layernorm(with_rms=True)
        b.fence(LNK, FFNK)
        ffn("f1", hoisted=h1)
        b.fence(FFNK, LNK)
        b.fence(WRK, PRK)
        layernorm(with_rms=False)
        b.fence(LNK, ATTK)
        m_ccK, m_ccV = qkv_and_ag()
        hw = hoist_proj(WO, n=3)
        attention(m_ccK, m_ccV)
        outproj(hoisted=hw)
        b.fence(ATTK, LNK)
        b.fence(PRK, WRK)
        h2 = hoist_wgu()
        layernorm(with_rms=True)
        b.fence(LNK, FFNK)
        ffn("f2", final_store=True, hoisted=h2)
        b.emit("sync", lambda e: e.nop(), waits=[out_stores[-1]])

        block = ctx.enter_context(nc.Block())
        b.realize(block)
    return nc



def _host_layout(inputs, cfg: CFG):
    D, M, F, E, NE, CF = cfg.D, cfg.M, cfg.F, cfg.E, cfg.NE, cfg.CF
    x = np.asarray(inputs["L0_attn_hidden_states"], np.float32)[0]
    wq = np.asarray(inputs["L0_attn_wq"], np.float32)
    wk = np.asarray(inputs["L0_attn_wk"], np.float32)
    wv = np.asarray(inputs["L0_attn_wv"], np.float32)
    wo = np.asarray(inputs["L0_attn_wo"], np.float32)

    def wslabs(w):
        wt = np.ascontiguousarray(w.T)
        a = wt.reshape(M, 128, M, 128)
        return np.ascontiguousarray(a.transpose(2, 1, 0, 3)).reshape(M, 128, D)

    def bcol(v):
        return np.ascontiguousarray(np.asarray(v, np.float32).reshape(M, 128).T)

    bq = bcol(inputs["L0_attn_bq"]); bk = bcol(inputs["L0_attn_bk"])
    bv = bcol(inputs["L0_attn_bv"]); bo = bcol(inputs["L0_attn_bo"])
    rms = bcol(inputs["L0_ffn_rms_weight"])
    bcols = np.ascontiguousarray(np.concatenate([bq, bk, bv, bo, rms], axis=1))

    rw = np.asarray(inputs["L0_ffn_router_w"], np.float32)
    rw_s = np.ascontiguousarray(rw.reshape(M, 128, 8).transpose(1, 0, 2)
                                .reshape(128, M * 8))

    wg = np.asarray(inputs["L0_ffn_w_gate"], np.float32)
    wu = np.asarray(inputs["L0_ffn_w_up"], np.float32)
    wd = np.asarray(inputs["L0_ffn_w_down"], np.float32)
    swg = np.asarray(inputs["L0_ffn_sw_gate"], np.float32)
    swu = np.asarray(inputs["L0_ffn_sw_up"], np.float32)
    swd = np.asarray(inputs["L0_ffn_sw_down"], np.float32)

    def gu_slabs(g, u):
        def t(w):
            return w.reshape(M, 128, CF, 128).transpose(2, 1, 0, 3)
        out = np.stack([t(g), t(u)], axis=2)
        return out.reshape(CF, 128, 2 * D)

    wgu = np.empty((NE, CF, 128, 2 * D), np.float32)
    for e in range(E):
        wgu[e] = gu_slabs(wg[e], wu[e])
    wgu[E] = gu_slabs(swg, swu)
    wgu = np.ascontiguousarray(wgu)

    def d_slabs(w):
        return (w.reshape(CF, 128, M, 128).transpose(2, 1, 0, 3)
                .reshape(M, 128, F))

    wds = np.empty((NE, M, 128, F), np.float32)
    for e in range(E):
        wds[e] = d_slabs(wd[e])
    wds[E] = d_slabs(swd)
    wds = np.ascontiguousarray(wds)

    shared = dict(wq=wslabs(wq), wk=wslabs(wk), wv=wslabs(wv), wo=wslabs(wo),
                  bcols=bcols, rw=rw_s, wgu=wgu, wd=wds)
    in_maps = []
    for r in range(cfg.R):
        m = dict(shared)
        m["xt"] = np.ascontiguousarray(x[r * cfg.T:(r + 1) * cfg.T, :].T)
        in_maps.append(m)
    return in_maps


def kernel(**inputs):
    cfg = CFG()
    nc = build(cfg)
    in_maps = _host_layout(inputs, cfg)
    res = run_bass_kernel_spmd(nc, in_maps, core_ids=list(range(cfg.R)))
    out = np.empty((1, cfg.S, cfg.D), np.float32)
    for r in range(cfg.R):
        out[0, r * cfg.T:(r + 1) * cfg.T, :] = res.results[r]["outT"].T
    return out


# version 14 (final)
# speedup vs baseline: 1.0202x; 1.0054x over previous
import numpy as np
import ml_dtypes
from contextlib import ExitStack
import concourse.bass as bass
import concourse.mybir as mybir
import concourse.bass_utils as _bass_utils
from concourse.bass_utils import run_bass_kernel_spmd

if not getattr(_bass_utils, "_ldw_opt_patched", False):
    _orig_run_command = _bass_utils.run_command

    def _run_command_ldw(cmd, *a, **kw):
        if isinstance(cmd, list):
            cmd = ["--enable-ldw-opt=true" if c == "--enable-ldw-opt=false"
                   else c for c in cmd]
        return _orig_run_command(cmd, *a, **kw)

    _bass_utils.run_command = _run_command_ldw
    _bass_utils._ldw_opt_patched = True

F32 = mybir.dt.float32
BF16 = mybir.dt.bfloat16
AF = mybir.ActivationFunctionType
ALU = mybir.AluOpType
AXX = mybir.AxisListType.X

EXP_SCALE = float(np.float32(1.0 / np.sqrt(128.0)))
EXP_SHIFT = -10.0


class CFG:
    def __init__(self, R=8, T=1024, D=2048, F=1024, E=8):
        self.R, self.T, self.D, self.F, self.E = R, T, D, F, E
        self.M = D // 128
        self.H = D // 128
        self.CF = F // 128
        self.TS = T // 128
        self.NE = E + 1
        self.S = R * T
        assert T % 128 == 0 and D % 128 == 0 and F % 128 == 0

    def nblk(self):
        out, c = [], 0
        while c < self.T:
            n = min(512, self.T - c)
            out.append((c, n))
            c += n
        return out


class Builder:
    ENGINES = ("sync", "gpsimd", "tensor", "vector", "scalar")

    def __init__(self, cfg: CFG):
        self.cfg = cfg
        self.nc = bass.Bass("TRN2", target_bir_lowering=False)
        self.steps = []
        self.sems = {}
        self.planned = {}
        for s in ("s_sync", "s_gp", "s_gpe", "s_pe", "s_act", "s_dve", "s_cc",
                  "s_pre", "s_wr0", "s_wr1", "s_wr2", "s_wd0", "s_wd1", "s_wd2",
                  "s_kr0", "s_kr1", "s_vr0", "s_vr1", "s_kv0", "s_kv1", "s_kv2",
                  "s_kv3", "s_bcA", "s_bcB",
                  "s_rec0", "s_rec1", "s_wbc0", "s_wbc1", "s_wdram", "s_out", "s_cc2",
                  "s_pr0", "s_pr1", "s_pr2", "s_pr3", "s_x0", "s_x1",
                  "s_pre2"):
            self.sems[s] = self.nc.alloc_semaphore(s)
            self.planned[s] = 0
        self.rdy = {}
        self.frees = {}
        self.pe_pending = set()
        self.last_tensor_step = None

    def emit(self, eng, fn, waits=(), inc=None):
        assert eng in self.ENGINES
        w = [x for x in waits if x is not None]
        mi = None
        if inc is not None:
            self.planned[inc[0]] += inc[1]
            mi = (inc[0], self.planned[inc[0]])
        self.steps.append((eng, fn, tuple(w), inc))
        if eng == "tensor":
            self.last_tensor_step = len(self.steps) - 1
            if mi is not None and self.pe_pending:
                for key in self.pe_pending:
                    self.consumed(key, mi)
                self.pe_pending.clear()
        return mi

    def dma(self, eng, out_ap, in_ap, waits=(), sem=None):
        assert sem is not None
        return self.emit(eng, lambda e, o=out_ap, i=in_ap: e.dma_start(out=o, in_=i),
                         waits=waits, inc=(sem, 16))

    def mm(self, out_ap, lhsT, rhs, start, stop, waits=(), inc=False):
        return self.emit(
            "tensor",
            lambda t, o=out_ap, l=lhsT, r=rhs, a=start, b=stop:
                t.matmul(o, l, r, start=a, stop=b),
            waits=waits, inc=("s_pe", 1) if inc else None)

    def act(self, fn, waits=()):
        return self.emit("scalar", fn, waits=waits, inc=("s_act", 1))

    def dve(self, fn, waits=()):
        return self.emit("vector", fn, waits=waits, inc=("s_dve", 1))

    def gpe(self, fn, waits=()):
        return self.emit("gpsimd", fn, waits=waits, inc=("s_gpe", 1))

    def produced(self, key, m):
        self.rdy[key] = m
        self.frees[key] = {}

    def consumed(self, key, m):
        if m is None:
            self.pe_pending.add(key)
            return
        d = self.frees.setdefault(key, {})
        s, v = m
        if d.get(s, 0) < v:
            d[s] = v

    def r(self, key):
        m = self.rdy.get(key)
        return [m] if m is not None else []

    def _pe_flush(self):
        if not self.pe_pending:
            return
        i = self.last_tensor_step
        eng, fn, wts, inc = self.steps[i]
        assert eng == "tensor"
        if inc is None:
            self.planned["s_pe"] += 1
            inc = ("s_pe", 1)
            self.steps[i] = (eng, fn, wts, inc)
        mi = ("s_pe", self.planned["s_pe"])
        for key in self.pe_pending:
            self.consumed(key, mi)
        self.pe_pending.clear()

    def fence(self, from_keys, to_keys):
        acc = {}
        for k in from_keys:
            for (sn, v) in self.w(k):
                if acc.get(sn, 0) < v:
                    acc[sn] = v
        for k in to_keys:
            d = self.frees.setdefault(k, {})
            for sn, v in acc.items():
                if d.get(sn, 0) < v:
                    d[sn] = v

    def w(self, key):
        if key in self.pe_pending:
            self._pe_flush()
        out = list(self.frees.get(key, {}).items())
        m = self.rdy.get(key)
        if m is not None:
            out.append(m)
        return out

    def realize(self, block):
        sems, steps = self.sems, self.steps

        def mk(eng_name):
            def run(e):
                hwm = {}
                for eng, fn, waits, inc in steps:
                    if eng != eng_name:
                        continue
                    for (sn, val) in waits:
                        if hwm.get(sn, 0) < val:
                            e.wait_ge(sems[sn], val)
                            hwm[sn] = val
                    ins = fn(e)
                    if inc is not None:
                        ins.then_inc(sems[inc[0]], inc[1])
            return run

        block.sync(mk('sync'))
        block.gpsimd(mk('gpsimd'))
        block.tensor(mk('tensor'))
        block.vector(mk('vector'))
        block.scalar(mk('scalar'))


def build(cfg: CFG):
    b = Builder(cfg)
    nc = b.nc
    T, D, M, H, CF, TS, NE, E, R, F = (cfg.T, cfg.D, cfg.M, cfg.H, cfg.CF,
                                       cfg.TS, cfg.NE, cfg.E, cfg.R, cfg.F)
    NBLK = cfg.nblk()

    XT = nc.dram_tensor("xt", [D, T], F32, kind="ExternalInput")
    WQ = nc.dram_tensor("wq", [M, 128, D], F32, kind="ExternalInput")
    WK = nc.dram_tensor("wk", [M, 128, D], F32, kind="ExternalInput")
    WV = nc.dram_tensor("wv", [M, 128, D], F32, kind="ExternalInput")
    WO = nc.dram_tensor("wo", [M, 128, D], F32, kind="ExternalInput")
    BCOLS = nc.dram_tensor("bcols", [128, 5 * M], F32, kind="ExternalInput")
    RW = nc.dram_tensor("rw", [128, M * 8], F32, kind="ExternalInput")
    WGU = nc.dram_tensor("wgu", [NE, CF, 128, 2 * D], F32, kind="ExternalInput")
    WD = nc.dram_tensor("wd", [NE, M, 128, F], F32, kind="ExternalInput")
    OUT = nc.dram_tensor("outT", [D, T], F32, kind="ExternalOutput")

    k_in = nc.dram_tensor("k_in", [D * T // 1024, 1024], BF16, kind="Internal")
    v_in = nc.dram_tensor("v_in", [D * T // 1024, 1024], BF16, kind="Internal")
    ag_space = "Shared" if R > 4 else "Local"
    ag_k = nc.dram_tensor("ag_k", [R * D * T // 1024, 1024], BF16,
                          kind="Internal", addr_space=ag_space)
    ag_v = nc.dram_tensor("ag_v", [R * D * T // 1024, 1024], BF16,
                          kind="Internal", addr_space=ag_space)
    bc_dram = nc.dram_tensor("bc_dram", [4, T], F32, kind="Internal")
    w_dram = nc.dram_tensor("w_dram", [8, T], BF16, kind="Internal")

    cvals = [0.0, 1.0, -1.0, 0.5, 1e-5, 1e-6, EXP_SCALE, EXP_SHIFT]
    CPOOL_D = nc.inline_tensor(np.array([cvals], np.float32), name="cpool_d")
    ONESC_D = nc.inline_tensor(np.ones((128, 1), ml_dtypes.bfloat16), name="onesc_d")
    ONESF_D = nc.inline_tensor(np.ones((128, 1), np.float32), name="onesf_d")
    ONESR_D = nc.inline_tensor(np.ones((1, 128), ml_dtypes.bfloat16), name="onesr_d")
    IDB_D = nc.inline_tensor(np.eye(128, dtype=ml_dtypes.bfloat16), name="idb_d")

    with ExitStack() as ctx:
        sbt = lambda n, s, d: ctx.enter_context(nc.sbuf_tensor(n, s, d))
        pst = lambda n, s, d: ctx.enter_context(nc.psum_tensor(n, s, d))

        xT = sbt("xT", [128, M * T], F32)
        hT = sbt("hT", [128, M * T], BF16)
        qT = sbt("qT", [128, H * T], BF16)
        lnscr = sbt("lnscr", [128, T], F32)
        sq = sbt("sq", [128, 2 * T], BF16)
        cpool = sbt("cpool", [128, len(cvals)], F32)
        onescol = sbt("onescol", [128, 1], BF16)
        onescolf = sbt("onescolf", [128, 1], F32)
        identb = sbt("identb", [128, 128], BF16)
        bcols = sbt("bcols_sb", [128, 5 * M], F32)
        rw_sb = sbt("rw_sb", [128, M * 8], BF16)
        stats = sbt("stats", [1, 4 * T], F32)
        e64 = sbt("e64", [128, 64], F32)
        onesrow = sbt("onesrow", [1, 128], BF16)
        k64 = sbt("k64", [128, 64], F32)
        m8 = sbt("m8", [128, 2 * 8], F32)
        w_sb = sbt("w_sb", [128, 64], BF16)
        wT_bf = sbt("wT_bf", [8, T], BF16)
        WSLOT = 2 * D
        wring = sbt("wring", [128, 2 * WSLOT], BF16)
        AR = max(8 * T,
                 2 * CF * T + 6 * T + 6 * F,
                 36 * T)
        arena = sbt("arena", [128, AR], mybir.dt.uint8)
        ar_f32 = arena.ap().bitcast(F32)
        ar_bf = arena.ap().bitcast(BF16)
        nb2 = AR // 2
        nf4 = AR // 4
        A2bc = ar_bf[:, 0:T]
        B2bc = ar_bf[:, T:2 * T]
        hu_v = ar_bf[:, 0:CF * T]
        wbc_v = ar_bf[:, CF * T:CF * T + 3 * T]
        wdr_v = ar_bf[:, CF * T + 3 * T:CF * T + 3 * T + 3 * F]
        kring_v = ar_bf[:, 0:2 * T]
        vring_v = ar_bf[:, 2 * T:4 * T]
        expb_v = ar_bf[:, 4 * T:8 * T]
        evr_v = ar_bf[:, 8 * T:12 * T]
        recbc_v = ar_f32[:, 6 * T:8 * T]
        acc_v = ar_f32[:, 8 * T:9 * T]

        PA = pst("PA", [128, T], F32)
        PB = pst("PB", [128, T], F32)
        PC = pst("PC", [128, T], F32)
        PD = pst("PD", [128, T], F32)
        PS = {"PA": PA, "PB": PB, "PC": PC, "PD": PD}

        for i, v in enumerate(cvals):
            nc.const_aps.register(F32, v, cpool[:, i:i + 1])

        xT_c = lambda c: xT[:, c * T:(c + 1) * T]
        hT_c = lambda c: hT[:, c * T:(c + 1) * T]
        qT_c = lambda c: qT[:, c * T:(c + 1) * T]
        sq_s = lambda s: sq[:, s * T:(s + 1) * T]
        wring_s = lambda s, n=None: wring[:, s * WSLOT:s * WSLOT + (n or WSLOT)]
        wdr_s = lambda s: wdr_v[:, s * F:(s + 1) * F]
        hu_c = lambda c: hu_v[:, c * T:(c + 1) * T]
        wbc_s = lambda s: wbc_v[:, s * T:(s + 1) * T]
        kring_s = lambda s: kring_v[:, s * T:(s + 1) * T]
        vring_s = lambda s: vring_v[:, s * T:(s + 1) * T]
        expb_s = lambda s: expb_v[:, s * T:(s + 1) * T]
        evr_s = lambda s: evr_v[:, s * T:(s + 1) * T]
        recbc_s = lambda s: recbc_v[:, s * T:(s + 1) * T]
        rrow_s = lambda s: stats[0:1, s * T:(s + 1) * T]
        lnscr_bf = lnscr.ap().bitcast(BF16)[:, 0:T]
        LNK = ["A2bc", "B2bc"]
        FFNK = ([f"hu{c}" for c in range(CF)] + ["wbc0", "wbc1", "wbc2"]
                + [f"wdr{i}" for i in range(3)])
        ATTK = (["kring0", "kring1", "vring0", "vring1"]
                + [f"exp{i}" for i in range(4)] + [f"evr{i}" for i in range(4)]
                + ["recbc0", "recbc1", "accL", "accR"])

        bq_c = lambda c: bcols[:, 0 * M + c:0 * M + c + 1]
        bk_c = lambda c: bcols[:, 1 * M + c:1 * M + c + 1]
        bv_c = lambda c: bcols[:, 2 * M + c:2 * M + c + 1]
        bo_c = lambda c: bcols[:, 3 * M + c:3 * M + c + 1]
        rms_c = lambda c: bcols[:, 4 * M + c:4 * M + c + 1]

        kf = k_in.ap().rearrange("a b -> (a b)")
        vf = v_in.ap().rearrange("a b -> (a b)")
        agkf = ag_k.ap().rearrange("a b -> (a b)")
        agvf = ag_v.ap().rearrange("a b -> (a b)")

        def k_stage_ap(h):
            return kf[h * 128 * T:(h * 128 + 128) * T].rearrange(
                "(p k) -> p k", k=T)

        def v_stage_ap(h):
            vv = vf.rearrange("(k d) -> k d", d=D)
            return vv.rearrange("(s p) d -> p s d", p=128)[:, :, h * 128:(h + 1) * 128]

        def k_read_ap(r, h):
            base = r * D * T
            return agkf[base + h * 128 * T:base + (h * 128 + 128) * T].rearrange(
                "(p k) -> p k", k=T)

        def v_read_ap(r, h):
            base = r * D * T
            vv = agvf[base:base + D * T].rearrange("(k d) -> k d", d=D)
            return vv.rearrange("(c p) d -> p c d", p=128)[:, :, h * 128:(h + 1) * 128]

        m_cp = b.dma("sync", cpool[:, :],
                     CPOOL_D[:, :].to_broadcast([128, len(cvals)]), sem="s_pre")
        m_oc = b.dma("sync", onescol[:, :], ONESC_D[:, :], sem="s_pre")
        b.dma("sync", onescolf[:, :], ONESF_D[:, :], sem="s_pre")
        b.dma("sync", onesrow[:, :], ONESR_D[:, :], sem="s_pre")
        m_idb = b.dma("sync", identb[:, :], IDB_D[:, :], sem="s_pre")
        m_bcl = b.dma("sync", bcols[:, :], BCOLS[:, :], sem="s_pre")
        m_xt0 = b.dma("sync", xT[:, 0:(M // 2) * T],
                      XT[0:D // 2, :].rearrange("(m p) t -> p m t", p=128),
                      sem="s_x0")
        m_xt1 = b.dma("sync", xT[:, (M // 2) * T:],
                      XT[D // 2:D, :].rearrange("(m p) t -> p m t", p=128),
                      sem="s_x1")
        m_rw = b.dma("gpsimd", rw_sb[:, :], RW[:, :], sem="s_pre2")
        m_all = ("s_pre", b.planned["s_pre"])
        m_cp = m_oc = m_idb = m_bcl = m_all
        for c in range(M):
            b.produced(f"xT{c}", m_xt0 if c < M // 2 else m_xt1)
        b.produced("cpool", m_cp)

        def strow(i):
            return stats[0:1, i * T:(i + 1) * T]

        def layernorm(with_rms):
            cast_m = {}
            sum_m = suma_m = None
            for c in range(M):
                cm = b.dve(lambda v, c=c: v.tensor_copy(hT_c(c), xT_c(c)),
                           waits=b.r(f"xT{c}") + b.w(f"hT{c}"))
                b.consumed(f"xT{c}", cm)
                cast_m[c] = cm
                sl = c % 2
                sm = b.act(lambda s, c=c, sl=sl: s.activation(
                    sq_s(sl), xT_c(c), AF.Square),
                    waits=b.r(f"xT{c}") + b.w(f"sq{sl}") + b.r("cpool"))
                b.consumed(f"xT{c}", sm)
                b.produced(f"sq{sl}", sm)
                for bi, (c0, nb) in enumerate(NBLK):
                    last = (c == M - 1) and (bi == len(NBLK) - 1)
                    wa = [cm, m_oc] + (b.w("PA") if c == 0 and bi == 0 else [])
                    ma = b.mm(PA[0:1, c0:c0 + nb], onescol[:, :],
                              hT_c(c)[:, c0:c0 + nb],
                              start=(c == 0), stop=(c == M - 1), waits=wa,
                              inc=last)
                    b.consumed(f"hT{c}", ma)
                    wb = [sm, m_oc] + (b.w("PB") if c == 0 and bi == 0 else [])
                    mb = b.mm(PB[0:1, c0:c0 + nb], onescol[:, :],
                              sq_s(sl)[:, c0:c0 + nb],
                              start=(c == 0), stop=(c == M - 1), waits=wb,
                              inc=last)
                    b.consumed(f"sq{sl}", mb)
                    if last:
                        sum_m, suma_m = mb, ma
            b.produced("PA", suma_m)
            b.produced("PB", sum_m)
            r0, r1, r2, r3 = strow(0), strow(1), strow(2), strow(3)
            m_mu = b.dve(lambda v: v.tensor_scalar_mul(r0, PA[0:1, :], 1.0 / D),
                         waits=[suma_m, sum_m] + b.w("st0"))
            b.produced("st0", m_mu)
            b.consumed("PA", m_mu)
            m_e2 = b.dve(lambda v: v.tensor_scalar_mul(r1, PB[0:1, :], 1.0 / D),
                         waits=[sum_m] + b.w("st1"))
            b.produced("st1", m_e2)
            b.consumed("PB", m_e2)
            m_m2 = b.dve(lambda v: v.tensor_mul(r2, r0, r0),
                         waits=[m_mu] + b.w("st2"))
            b.produced("st2", m_m2)
            b.consumed("st0", m_m2)
            m_v = b.dve(lambda v: v.tensor_sub(r1, r1, r2),
                        waits=[m_e2, m_m2] + b.w("st1"))
            b.produced("st1", m_v)
            b.consumed("st2", m_v)
            m_sd = b.act(lambda sc: sc.activation(r2, r1, AF.Sqrt, bias=1e-5,
                                                  scale=1.0),
                         waits=[m_v] + b.r("cpool") + b.w("st2"))
            b.produced("st2", m_sd)
            b.consumed("st1", m_sd)
            m_rst = b.dve(lambda v: v.reciprocal(r2, r2),
                          waits=[m_sd] + b.w("st2"))
            b.produced("st2", m_rst)
            if with_rms:
                m_q = b.dve(lambda v: v.tensor_mul(r3, r2, r2),
                            waits=[m_rst] + b.w("st3"))
                b.produced("st3", m_q)
                b.consumed("st2", m_q)
                m_vr = b.dve(lambda v: v.tensor_mul(r3, r1, r3),
                             waits=[m_q, m_v] + b.w("st3"))
                b.produced("st3", m_vr)
                b.consumed("st1", m_vr)
                m_sr = b.act(lambda sc: sc.activation(r3, r3, AF.Sqrt, bias=1e-6,
                                                      scale=1.0),
                             waits=[m_vr] + b.r("cpool") + b.w("st3"))
                b.produced("st3", m_sr)
                m_rmr = b.dve(lambda v: v.reciprocal(r3, r3),
                              waits=[m_sr] + b.w("st3"))
                b.produced("st3", m_rmr)
                m_a2f = b.dve(lambda v: v.tensor_mul(r3, r2, r3),
                              waits=[m_rmr, m_rst] + b.w("st3"))
                b.produced("st3", m_a2f)
                b.consumed("st2", m_a2f)
                a2row, k_a2row = r3, "st3"
            else:
                a2row, k_a2row, m_a2f = r2, "st2", m_rst
            m_b2 = b.dve(lambda v: v.scalar_tensor_tensor(
                r0, r0, -1.0, a2row, ALU.mult, ALU.mult),
                waits=[m_mu, m_a2f] + b.w("st0"))
            b.produced("st0", m_b2)
            b.consumed(k_a2row, m_b2)
            b2, k_b2 = r0, "st0"
            a2bf = sq[0:1, 0:T]
            b2bf = sq[0:1, T:2 * T]
            m_cA = b.dve(lambda v: v.tensor_copy(a2bf, a2row),
                         waits=[m_a2f, m_b2] + b.w("sq0"))
            b.produced("sq0", m_cA)
            b.consumed(k_a2row, m_cA)
            m_cB = b.dve(lambda v: v.tensor_copy(b2bf, b2),
                         waits=[m_b2] + b.w("sq1"))
            b.produced("sq1", m_cB)
            b.consumed(k_b2, m_cB)
            mA = mB = None
            for bi, (c0, nb) in enumerate(NBLK):
                wv_ = [m_cA, m_all] + (b.w("PA") if bi == 0 else [])
                mA = b.mm(PA[:, c0:c0 + nb], onesrow[:, :], a2bf[:, c0:c0 + nb],
                          start=True, stop=True, waits=wv_,
                          inc=(bi == len(NBLK) - 1))
                b.consumed("sq0", mA)
                wv_ = [m_cB, m_all] + (b.w("PB") if bi == 0 else [])
                mB = b.mm(PB[:, c0:c0 + nb], onesrow[:, :], b2bf[:, c0:c0 + nb],
                          start=True, stop=True, waits=wv_,
                          inc=(bi == len(NBLK) - 1))
                b.consumed("sq1", mB)
            b.produced("PA", mA)
            b.produced("PB", mB)
            m_iA = b.act(lambda sc: sc.activation(A2bc, PA[:, :], AF.Identity,
                                                  bias=0.0, scale=1.0),
                         waits=[mA] + b.w("A2bc") + b.r("cpool"))
            b.consumed("PA", m_iA)
            b.produced("A2bc", m_iA)
            m_iB = b.act(lambda sc: sc.activation(B2bc, PB[:, :], AF.Identity,
                                                  bias=0.0, scale=1.0),
                         waits=[mB] + b.w("B2bc") + b.r("cpool"))
            b.consumed("PB", m_iB)
            b.produced("B2bc", m_iB)
            scr_bf = lnscr.ap().bitcast(BF16)[:, 0:T]
            for c in range(M):
                wv_ = ([cast_m[c], m_iA, sum_m, suma_m] + b.w("lnscr"))
                m_t1 = b.dve(lambda v, c=c: v.tensor_mul(
                    scr_bf, hT_c(c), A2bc), waits=wv_)
                b.produced("lnscr", m_t1)
                b.consumed("A2bc", m_t1)
                b.consumed(f"hT{c}", m_t1)
                if with_rms:
                    m_t2 = b.dve(lambda v: v.tensor_add(
                        scr_bf, scr_bf, B2bc), waits=[m_t1, m_iB])
                    b.produced("lnscr", m_t2)
                    b.consumed("B2bc", m_t2)
                    m_h = b.dve(lambda v, c=c: v.tensor_scalar_mul(
                        hT_c(c), scr_bf, rms_c(c)),
                        waits=[m_t2, m_bcl] + b.w(f"hT{c}"))
                    b.consumed("lnscr", m_h)
                else:
                    m_h = b.dve(lambda v, c=c: v.tensor_add(
                        hT_c(c), scr_bf, B2bc),
                        waits=[m_t1, m_iB] + b.w(f"hT{c}"))
                    b.consumed("B2bc", m_h)
                    b.consumed("lnscr", m_h)
                b.produced(f"hT{c}", m_h)

        def router():
            NS = TS
            lg = None
            for sub in range(NS):
                for m in range(M):
                    wv_ = b.r(f"hT{m}") + [m_rw]
                    if sub == 0 and m == 0:
                        wv_ += b.w("PC")
                    lg = b.mm(PC[:, 8 * sub:8 * sub + 8],
                              hT_c(m)[:, sub * 128:(sub + 1) * 128],
                              rw_sb[:, m * 8:(m + 1) * 8],
                              start=(sub == 0 and m == 0),
                              stop=(sub == NS - 1 and m == M - 1), waits=wv_,
                              inc=(sub == NS - 1 and m == M - 1))
                    b.consumed(f"hT{m}", lg)
            b.produced("PC", lg)
            W8 = 8 * NS
            lv = k64[:, 0:W8]
            ev = e64[:, 0:W8]
            kv = k64[:, 0:W8]
            l3 = lambda ap: ap.rearrange("p (s e) -> p s e", e=8)
            bc = lambda ap8: ap8[:, :, None].to_broadcast([128, NS, 8])
            m1 = m8[:, 0:NS]
            m1e_ = m8[:, 0:NS]
            m2 = m8[:, 8:8 + NS]
            den = m8[:, 0:NS]
            m_ev0 = b.act(lambda sc: sc.activation(lv, PC[:, 0:W8], AF.Identity,
                                                   bias=0.0, scale=1.0),
                          waits=[lg] + b.w("k64") + b.r("cpool"))
            b.consumed("PC", m_ev0)
            b.produced("k64", m_ev0)
            m_m1 = b.dve(lambda v: v.reduce_max(
                m1, l3(lv), axis=AXX), waits=[m_ev0] + b.w("m8"))
            m_ms = b.dve(lambda v: v.tensor_sub(l3(lv), l3(lv), bc(m1)),
                         waits=[m_m1] + b.w("k64"))
            b.produced("k64", m_ms)
            m_e = b.act(lambda sc: sc.activation(ev, lv, AF.Exp),
                        waits=[m_ms] + b.w("e64") + b.r("cpool"))
            b.consumed("k64", m_e)
            b.produced("e64", m_e)
            m_em = b.dve(lambda v: v.reduce_max(m1e_, l3(ev), axis=AXX),
                         waits=[m_e, m_ms, m_m1])
            m_k1 = b.dve(lambda v: v.tensor_tensor(l3(kv), l3(ev), bc(m1e_),
                                                   ALU.is_ge),
                         waits=[m_em] + b.w("k64"))
            m_k1m = b.dve(lambda v: v.tensor_mul(l3(kv), l3(kv), bc(m1e_)),
                          waits=[m_k1])
            m_e2 = b.dve(lambda v: v.tensor_sub(kv, ev, kv), waits=[m_k1m])
            m_m2 = b.dve(lambda v: v.reduce_max(m2, l3(kv), axis=AXX),
                         waits=[m_e2])
            m_kp = b.dve(lambda v: v.tensor_tensor(l3(kv), l3(ev), bc(m2),
                                                   ALU.is_ge), waits=[m_m2])
            m_wr = b.dve(lambda v: v.tensor_mul(kv, ev, kv), waits=[m_kp])
            b.consumed("e64", m_wr)
            m_dn = b.dve(lambda v: v.tensor_add(den, m1e_, m2),
                         waits=[m_m2, m_em])
            m_rc = b.dve(lambda v: v.reciprocal(den, den), waits=[m_dn])
            m_rh = b.dve(lambda v: v.tensor_scalar_mul(den, den, 0.5),
                         waits=[m_rc])
            m_w = b.dve(lambda v: v.tensor_mul(l3(w_sb[:, 0:W8]), l3(kv),
                                               bc(den)),
                        waits=[m_rh, m_wr] + b.w("w_sb"))
            b.produced("w_sb", m_w)
            b.produced("m8", m_w)
            b.produced("k64", m_w)
            tr_ms = []
            PDw = PD.ap().bitcast(BF16)
            for sub in range(NS):
                wv_ = [m_w, m_idb]
                if sub == 0:
                    wv_ += b.w("PD")
                first_in_bank = (sub * 128) % 1024 == 0
                last_in_bank = ((sub + 1) * 128) % 1024 == 0 or sub == NS - 1
                tr = b.emit("tensor",
                            lambda t, sub=sub, a=first_in_bank, z=last_in_bank:
                                t.matmul(PDw[0:8, sub * 128:(sub + 1) * 128],
                                         w_sb[:, 8 * sub:8 * sub + 8],
                                         identb[:, :],
                                         start=a, stop=z, is_transpose=True),
                            waits=wv_, inc=("s_pe", 1))
                b.consumed("w_sb", tr)
                tr_ms.append(tr)
            b.produced("PD", tr_ms[-1])
            m_ev = b.act(lambda sc: sc.activation(wT_bf[:, :], PDw[0:8, 0:T],
                                                  AF.Identity, bias=0.0,
                                                  scale=1.0),
                         waits=tr_ms + b.w("wT_bf") + b.r("cpool"))
            b.consumed("PD", m_ev)
            b.produced("wT_bf", m_ev)
            m_st = b.dma("sync", w_dram[0:8, :], wT_bf[:, :],
                         waits=[m_ev] + b.w("w_dram"), sem="s_wdram")
            b.produced("w_dram", m_st)
            b.consumed("wT_bf", m_st)
            return m_st

        wslab_i = [0]
        pslab_i = [0]
        wd_i = [0]

        def ffn(tag, final_store=False, hoisted=None):
            m_route = router()
            m_w8 = b.emit("gpsimd", lambda g: g.memset(wbc_s(2), 0.5),
                          waits=b.w("wbc2"), inc=("s_gpe", 1))
            b.produced("wbc2", m_w8)
            for e in range(NE):
                if e < E:
                    ws = e % 2
                    wkey = f"wbc{ws}"
                    m_bc = b.dma("scalar", wbc_s(ws),
                                 w_dram[e:e + 1, :].to_broadcast([128, T]),
                                 waits=[m_route] + b.w(wkey), sem=f"s_wbc{ws}")
                    b.produced(wkey, m_bc)
                    wbc_ap = wbc_s(ws)
                else:
                    wkey = "wbc2"
                    wbc_ap = wbc_s(2)
                for c in range(CF):
                    sl = wslab_i[0] % 2
                    wslab_i[0] += 1
                    if hoisted is not None and e == 0 and c < len(hoisted):
                        m_sl = hoisted[c]
                    else:
                        m_sl = b.dma("gpsimd", wring_s(sl),
                                     WGU[e, c, :, :], waits=b.w(f"wring{sl}"),
                                     sem=f"s_wr{sl}")
                    b.produced(f"wring{sl}", m_sl)
                    lg = lu = None
                    for m in range(M):
                        for bi, (c0, nb) in enumerate(NBLK):
                            wv_ = [m_sl] + b.r(f"hT{m}")
                            if m == 0 and bi == 0:
                                wv_ += b.w("PA")
                            lg = b.mm(PA[:, c0:c0 + nb],
                                      wring_s(sl)[:, m * 128:(m + 1) * 128],
                                      hT_c(m)[:, c0:c0 + nb],
                                      start=(m == 0), stop=(m == M - 1),
                                      waits=wv_,
                                      inc=(m == M - 1 and bi == len(NBLK) - 1))
                            b.consumed(f"hT{m}", lg)
                            b.consumed(f"wring{sl}", lg)
                    for m in range(M):
                        for bi, (c0, nb) in enumerate(NBLK):
                            wv_ = [m_sl] + b.r(f"hT{m}")
                            if m == 0 and bi == 0:
                                wv_ += b.w("PB")
                            lu = b.mm(PB[:, c0:c0 + nb],
                                      wring_s(sl)[:, D + m * 128:D + (m + 1) * 128],
                                      hT_c(m)[:, c0:c0 + nb],
                                      start=(m == 0), stop=(m == M - 1),
                                      waits=wv_,
                                      inc=(m == M - 1 and bi == len(NBLK) - 1))
                            b.consumed(f"hT{m}", lu)
                            b.consumed(f"wring{sl}", lu)
                    b.produced("PA", lg)
                    b.produced("PB", lu)
                    ssl = c % 2
                    m_si = b.act(lambda s, sl2=ssl: s.activation(
                        sq_s(sl2), PA[:, :], AF.Silu),
                        waits=[lg] + b.w(f"sq{ssl}") + b.r("cpool"))
                    b.consumed("PA", m_si)
                    b.produced(f"sq{ssl}", m_si)
                    m_t = b.dve(lambda v, sl2=ssl, c=c: v.tensor_mul(
                        hu_c(c), sq_s(sl2), PB[:, :]),
                        waits=[m_si, lu] + b.w(f"hu{c}"))
                    b.consumed("PB", m_t)
                    b.consumed(f"sq{ssl}", m_t)
                    b.produced(f"hu{c}", m_t)
                    m_hu = b.dve(lambda v, c=c, wb=wbc_ap: v.tensor_mul(
                        hu_c(c), hu_c(c), wb),
                        waits=[m_t] + b.r(wkey) + b.w(f"hu{c}"))
                    b.consumed(wkey, m_hu)
                    b.produced(f"hu{c}", m_hu)
                for j in range(M):
                    sl = wd_i[0] % 3
                    wd_i[0] += 1
                    m_sl = b.dma("gpsimd", wdr_s(sl), WD[e, j, :, :],
                                 waits=b.w(f"wdr{sl}"), sem=f"s_wd{sl}")
                    b.produced(f"wdr{sl}", m_sl)
                    pk = "PC" if j % 2 == 0 else "PD"
                    P = PS[pk]
                    dn = None
                    for c in range(CF):
                        for bi, (c0, nb) in enumerate(NBLK):
                            wv_ = [m_sl] + b.r(f"hu{c}")
                            if c == 0 and bi == 0:
                                wv_ += b.w(pk)
                            dn = b.mm(P[:, c0:c0 + nb],
                                      wdr_s(sl)[:, c * 128:(c + 1) * 128],
                                      hu_c(c)[:, c0:c0 + nb],
                                      start=(c == 0), stop=(c == CF - 1),
                                      waits=wv_,
                                      inc=(c == CF - 1 and bi == len(NBLK) - 1))
                            b.consumed(f"hu{c}", dn)
                            b.consumed(f"wdr{sl}", dn)
                    b.produced(pk, dn)
                    m_x = b.dve(lambda v, j=j, P=P: v.tensor_add(
                        xT_c(j), xT_c(j), P[:, :]),
                        waits=[dn] + b.w(f"xT{j}"))
                    b.consumed(pk, m_x)
                    b.produced(f"xT{j}", m_x)
                    if final_store and e == NE - 1:
                        mo = b.dma("sync",
                                   OUT.ap().rearrange("(m p) t -> p m t", p=128)
                                   [:, j:j + 1, :],
                                   xT_c(j), waits=[m_x], sem="s_out")
                        b.consumed(f"xT{j}", mo)
                        out_stores.append(mo)

        out_stores = []

        def pr_s(sl):
            return wring[:, sl * D:(sl + 1) * D]

        def hoist_proj(Wdram, n=3):
            ms = []
            for j in range(n):
                sl = pslab_i[0] % 4
                pslab_i[0] += 1
                ms.append(b.dma("gpsimd", pr_s(sl), Wdram[j, :, :],
                                waits=b.w(f"pr{sl}"), sem=f"s_pr{sl}"))
            pslab_i[0] -= n
            return ms

        def proj_feature_major(Wdram, bias_c, out_key_fn, out_ap_fn,
                               hoisted=None):
            for j in range(M):
                sl = pslab_i[0] % 4
                pslab_i[0] += 1
                if hoisted is not None and j < len(hoisted):
                    m_sl = hoisted[j]
                else:
                    m_sl = b.dma("gpsimd", pr_s(sl), Wdram[j, :, :],
                                 waits=b.w(f"pr{sl}"), sem=f"s_pr{sl}")
                b.produced(f"pr{sl}", m_sl)
                pk = "PA" if j % 2 == 0 else "PB"
                P = PS[pk]
                pm = None
                for m in range(M):
                    for bi, (c0, nb) in enumerate(NBLK):
                        wv_ = [m_sl] + b.r(f"hT{m}")
                        if m == 0 and bi == 0:
                            wv_ += b.w(pk)
                        pm = b.mm(P[:, c0:c0 + nb],
                                  pr_s(sl)[:, m * 128:(m + 1) * 128],
                                  hT_c(m)[:, c0:c0 + nb],
                                  start=(m == 0), stop=(m == M - 1), waits=wv_,
                                  inc=(m == M - 1 and bi == len(NBLK) - 1))
                        b.consumed(f"hT{m}", pm)
                        b.consumed(f"pr{sl}", pm)
                b.produced(pk, pm)
                okey = out_key_fn(j)
                m_ev = b.act(lambda s, j=j, P=P, oap=out_ap_fn(j), bap=bias_c(j):
                             s.activation(oap, P[:, :], AF.Identity, bias=bap,
                                          scale=1.0),
                             waits=[pm, m_bcl] + b.w(okey) + b.r("cpool"))
                b.consumed(pk, m_ev)
                b.produced(okey, m_ev)
                yield j, okey, m_ev

        def qkv_and_ag():
            k_writes, v_writes = [], []
            for j, okey, m_ev in proj_feature_major(
                    WK, bk_c, lambda j: f"evr{j % 2}", lambda j: evr_s(j % 2)):
                m_o = b.dma("sync", k_stage_ap(j), evr_s(j % 2), waits=[m_ev],
                             sem=f"s_kv{j % 2}")
                b.consumed(okey, m_o)
                k_writes.append(m_o)
            hv = hoist_proj(WV, n=3)
            m_ccK = b.emit("gpsimd",
                           lambda g: g.collective_compute(
                               "AllGather", ALU.bypass,
                               replica_groups=[list(range(R))],
                               ins=[k_in.ap().opt()],
                               outs=[ag_k.ap().opt()]),
                           waits=k_writes, inc=("s_cc", 1))
            PDb = PD.ap().bitcast(BF16)
            b.fence(["PD"], ["PDh0", "PDh1"])
            for j, okey, m_ev in proj_feature_major(
                    WV, bv_c, lambda j: f"evr{j % 2}", lambda j: evr_s(j % 2),
                    hoisted=hv):
                hk = f"PDh{j % 2}"
                off = (j % 2) * T
                trs = []
                for s in range(TS):
                    wv_ = [m_ev, m_idb]
                    if s == 0:
                        wv_ += b.w(hk)
                    tr = b.emit("tensor",
                                lambda t, s=s, j=j, off=off: t.matmul(
                                    PDb[:, off + s * 128:off + (s + 1) * 128],
                                    evr_s(j % 2)[:, s * 128:(s + 1) * 128],
                                    identb[:, :],
                                    start=(s == 0), stop=(s == TS - 1),
                                    is_transpose=True),
                                waits=wv_, inc=("s_pe", 1))
                    b.consumed(okey, tr)
                    trs.append(tr)
                b.produced(hk, trs[-1])
                vk = 2 + (j % 2)
                m_vv = b.act(lambda s, vk=vk, off=off: s.activation(
                    evr_s(vk), PDb[:, off:off + T], AF.Identity, bias=0.0,
                    scale=1.0),
                    waits=trs + b.w(f"evr{vk}") + b.r("cpool"))
                b.consumed(hk, m_vv)
                b.produced(f"evr{vk}", m_vv)
                m_o = b.dma("sync", v_stage_ap(j),
                            evr_s(vk).rearrange("p (s d) -> p s d", d=128),
                            waits=[m_vv], sem=f"s_kv{vk}")
                b.consumed(f"evr{vk}", m_o)
                v_writes.append(m_o)
            b.fence(["PDh0", "PDh1"], ["PD"])
            hq = hoist_proj(WQ, n=3)
            m_ccV = b.emit("gpsimd",
                           lambda g: g.collective_compute(
                               "AllGather", ALU.bypass,
                               replica_groups=[list(range(R))],
                               ins=[v_in.ap().opt()],
                               outs=[ag_v.ap().opt()]),
                           waits=v_writes, inc=("s_cc2", 1))
            for _ in proj_feature_major(WQ, bq_c, lambda j: f"qT{j}",
                                        lambda j: qT_c(j), hoisted=hq):
                pass
            return m_ccK, m_ccV

        def attention(m_ccK, m_ccV):
            NKC = R * TS

            def kv_dma(h, r):
                ks = r % 2
                m_k = b.dma("sync", kring_s(ks), k_read_ap(r, h),
                            waits=[m_ccK] + b.w(f"kring{ks}"), sem=f"s_kr{ks}")
                b.produced(f"kring{ks}", m_k)
                m_v = b.dma("sync",
                            vring_s(ks).rearrange("p (c d) -> p c d", d=128),
                            v_read_ap(r, h), waits=[m_ccV] + b.w(f"vring{ks}"),
                            sem=f"s_vr{ks}")
                b.produced(f"vring{ks}", m_v)
                return ks, m_k, m_v

            exp_i = [0]
            pend_tail = [None]

            def emit_div(h):
                rs = h % 2
                m_ri = b.rdy[f"recbc{rs}"]
                m_at = b.rdy[f"qT{h}"]
                m_dv = b.dve(lambda v, h=h, rs=rs: v.tensor_mul(
                    qT_c(h), qT_c(h), recbc_s(rs)),
                    waits=[m_ri, m_at])
                b.consumed(f"recbc{rs}", m_dv)
                b.produced(f"qT{h}", m_dv)

            def emit_tail():
                if pend_tail[0] is None:
                    return
                h, avs = pend_tail[0]
                pend_tail[0] = None
                m_aL = b.rdy["accL"]
                sms = None
                for bi, (c0, nb) in enumerate(NBLK):
                    wv_ = [m_aL, m_oc] + (b.w("PD") if bi == 0 else [])
                    sms = b.mm(PD[0:1, c0:c0 + nb], onescolf[:, :],
                               acc_v[:, c0:c0 + nb],
                               start=True, stop=True, waits=wv_,
                               inc=(bi == len(NBLK) - 1))
                    b.consumed("accL", sms)
                b.produced("PD", sms)
                m_at = b.act(lambda sc, h=h: sc.activation(
                    qT_c(h), PC[:, :], AF.Identity, bias=0.0, scale=1.0),
                    waits=[avs] + b.w(f"qT{h}") + b.r("cpool"))
                b.consumed("PC", m_at)
                b.produced(f"qT{h}", m_at)
                rs = h % 2
                m_rc = b.dve(lambda v, rs=rs: v.reciprocal(rrow_s(rs),
                                                           PD[0:1, :]),
                             waits=[sms] + b.w(f"st{rs}"))
                b.consumed("PD", m_rc)
                b.produced(f"st{rs}", m_rc)
                m_ro = b.dma("gpsimd", bc_dram[2:3, :], rrow_s(rs),
                             waits=[m_rc] + b.w("bcd2"), sem=f"s_rec{rs}")
                b.produced("bcd2", m_ro)
                b.consumed(f"st{rs}", m_ro)
                m_ri = b.dma("gpsimd", recbc_s(rs),
                             bc_dram[2:3, :].to_broadcast([128, T]),
                             waits=[m_ro] + b.w(f"recbc{rs}"), sem=f"s_rec{rs}")
                b.produced(f"recbc{rs}", m_ri)
                b.consumed("bcd2", m_ri)

            for h in range(H):
                pend = {}
                pend[0] = kv_dma(h, 0)
                if R > 1:
                    pend[1] = kv_dma(h, 1)
                avs = None
                av_q = []
                for r in range(R):
                    ks, m_k, m_v = pend.pop(r)
                    for c in range(TS):
                        kc = r * TS + c
                        spk = "PA" if kc % 2 == 0 else "PB"
                        SP = PS[spk]
                        sc_m = None
                        for bi, (c0, nb) in enumerate(NBLK):
                            wv_ = [m_k] + b.r(f"qT{h}") + (b.w(spk) if bi == 0
                                                           else [])
                            sc_m = b.mm(SP[:, c0:c0 + nb],
                                        kring_s(ks)[:, c * 128:(c + 1) * 128],
                                        qT_c(h)[:, c0:c0 + nb],
                                        start=True, stop=True, waits=wv_,
                                        inc=(bi == len(NBLK) - 1))
                            b.consumed(f"qT{h}", sc_m)
                            b.consumed(f"kring{ks}", sc_m)
                        b.produced(spk, sc_m)
                        es = exp_i[0] % 4
                        exp_i[0] += 1
                        m_x = b.act(lambda sc, es=es, SP=SP: sc.activation(
                            expb_s(es), SP[:, :], AF.Exp, bias=EXP_SHIFT,
                            scale=EXP_SCALE),
                            waits=[sc_m] + b.w(f"exp{es}") + b.r("cpool"))
                        b.consumed(spk, m_x)
                        b.produced(f"exp{es}", m_x)
                        if kc == 0:
                            emit_tail()
                        if kc == 0:
                            m_aL = b.dve(lambda v, es=es: v.tensor_copy(
                                acc_v, expb_s(es)),
                                waits=[m_x] + b.w("accL"))
                        else:
                            m_aL = b.dve(lambda v, es=es: v.tensor_add(
                                acc_v, acc_v, expb_s(es)),
                                waits=[m_x] + b.w("accL"))
                        b.produced("accL", m_aL)
                        b.consumed(f"exp{es}", m_aL)
                        av_q.append((es, ks, kc, m_v))
                        if len(av_q) > 2:
                            pes, pks, pkc, pm_v = av_q.pop(0)
                            pc = pkc % TS
                            m_px = b.rdy[f"exp{pes}"]
                            for bi, (c0, nb) in enumerate(NBLK):
                                wv_ = [m_px, pm_v] + (b.w("PC")
                                                      if pkc == 0 and bi == 0
                                                      else [])
                                avs = b.mm(PC[:, c0:c0 + nb],
                                           vring_s(pks)[:, pc * 128:(pc + 1) * 128],
                                           expb_s(pes)[:, c0:c0 + nb],
                                           start=(pkc == 0), stop=False,
                                           waits=wv_,
                                           inc=(bi == len(NBLK) - 1))
                                b.consumed(f"exp{pes}", avs)
                                b.consumed(f"vring{pks}", avs)
                        if h > 0 and r == 0 and c == min(4, TS - 1):
                            emit_div(h - 1)
                        if c == min(2, TS - 1) and r + 1 < R and (r + 1) not in pend:
                            pend[r + 1] = kv_dma(h, r + 1)
                while av_q:
                    pes, pks, pkc, pm_v = av_q.pop(0)
                    pc = pkc % TS
                    m_px = b.rdy[f"exp{pes}"]
                    for bi, (c0, nb) in enumerate(NBLK):
                        wv_ = [m_px, pm_v] + (b.w("PC") if pkc == 0 and bi == 0
                                              else [])
                        avs = b.mm(PC[:, c0:c0 + nb],
                                   vring_s(pks)[:, pc * 128:(pc + 1) * 128],
                                   expb_s(pes)[:, c0:c0 + nb],
                                   start=(pkc == 0), stop=(pkc == NKC - 1),
                                   waits=wv_,
                                   inc=(bi == len(NBLK) - 1))
                        b.consumed(f"exp{pes}", avs)
                        b.consumed(f"vring{pks}", avs)
                b.produced("PC", avs)
                pend_tail[0] = (h, avs)
            emit_tail()
            emit_div(H - 1)
        def outproj(hoisted=None):
            for j in range(M):
                sl = pslab_i[0] % 4
                pslab_i[0] += 1
                if hoisted is not None and j < len(hoisted):
                    m_sl = hoisted[j]
                else:
                    m_sl = b.dma("gpsimd", pr_s(sl), WO[j, :, :],
                                 waits=b.w(f"pr{sl}"), sem=f"s_pr{sl}")
                b.produced(f"pr{sl}", m_sl)
                pk = "PA" if j % 2 == 0 else "PB"
                P = PS[pk]
                pm = None
                for m in range(M):
                    for bi, (c0, nb) in enumerate(NBLK):
                        wv_ = [m_sl] + b.r(f"qT{m}")
                        if m == 0 and bi == 0:
                            wv_ += b.w(pk)
                        pm = b.mm(P[:, c0:c0 + nb],
                                  pr_s(sl)[:, m * 128:(m + 1) * 128],
                                  qT_c(m)[:, c0:c0 + nb],
                                  start=(m == 0), stop=(m == M - 1), waits=wv_,
                                  inc=(m == M - 1 and bi == len(NBLK) - 1))
                        b.consumed(f"qT{m}", pm)
                        b.consumed(f"pr{sl}", pm)
                b.produced(pk, pm)
                m_x = b.dve(lambda v, j=j, P=P: v.scalar_tensor_tensor(
                    xT_c(j), P[:, :], bo_c(j), xT_c(j), ALU.add, ALU.add),
                    waits=[pm, m_bcl] + b.w(f"xT{j}"))
                b.consumed(pk, m_x)
                b.produced(f"xT{j}", m_x)

        WRK = ["wring0", "wring1"]
        PRK = ["pr0", "pr1", "pr2", "pr3"]

        def hoist_wgu():
            ms = []
            for c in range(2):
                sl = wslab_i[0] % 2
                wslab_i[0] += 1
                ms.append(b.dma("gpsimd", wring_s(sl), WGU[0, c, :, :],
                                waits=b.w(f"wring{sl}"), sem=f"s_wr{sl}"))
            wslab_i[0] -= 2
            return ms

        h1 = hoist_wgu()
        layernorm(with_rms=True)
        b.fence(LNK, FFNK)
        ffn("f1", hoisted=h1)
        b.fence(FFNK, LNK)
        b.fence(WRK, PRK)
        layernorm(with_rms=False)
        b.fence(LNK, ATTK)
        m_ccK, m_ccV = qkv_and_ag()
        hw = hoist_proj(WO, n=3)
        attention(m_ccK, m_ccV)
        outproj(hoisted=hw)
        b.fence(ATTK, LNK)
        b.fence(PRK, WRK)
        h2 = hoist_wgu()
        layernorm(with_rms=True)
        b.fence(LNK, FFNK)
        ffn("f2", final_store=True, hoisted=h2)
        b.emit("sync", lambda e: e.nop(), waits=[out_stores[-1]])

        block = ctx.enter_context(nc.Block())
        b.realize(block)
    return nc



def _host_layout(inputs, cfg: CFG):
    D, M, F, E, NE, CF = cfg.D, cfg.M, cfg.F, cfg.E, cfg.NE, cfg.CF
    x = np.asarray(inputs["L0_attn_hidden_states"], np.float32)[0]
    wq = np.asarray(inputs["L0_attn_wq"], np.float32)
    wk = np.asarray(inputs["L0_attn_wk"], np.float32)
    wv = np.asarray(inputs["L0_attn_wv"], np.float32)
    wo = np.asarray(inputs["L0_attn_wo"], np.float32)

    def wslabs(w):
        wt = np.ascontiguousarray(w.T)
        a = wt.reshape(M, 128, M, 128)
        return np.ascontiguousarray(a.transpose(2, 1, 0, 3)).reshape(M, 128, D)

    def bcol(v):
        return np.ascontiguousarray(np.asarray(v, np.float32).reshape(M, 128).T)

    bq = bcol(inputs["L0_attn_bq"]); bk = bcol(inputs["L0_attn_bk"])
    bv = bcol(inputs["L0_attn_bv"]); bo = bcol(inputs["L0_attn_bo"])
    rms = bcol(inputs["L0_ffn_rms_weight"])
    bcols = np.ascontiguousarray(np.concatenate([bq, bk, bv, bo, rms], axis=1))

    rw = np.asarray(inputs["L0_ffn_router_w"], np.float32)
    rw_s = np.ascontiguousarray(rw.reshape(M, 128, 8).transpose(1, 0, 2)
                                .reshape(128, M * 8))

    wg = np.asarray(inputs["L0_ffn_w_gate"], np.float32)
    wu = np.asarray(inputs["L0_ffn_w_up"], np.float32)
    wd = np.asarray(inputs["L0_ffn_w_down"], np.float32)
    swg = np.asarray(inputs["L0_ffn_sw_gate"], np.float32)
    swu = np.asarray(inputs["L0_ffn_sw_up"], np.float32)
    swd = np.asarray(inputs["L0_ffn_sw_down"], np.float32)

    def gu_slabs(g, u):
        def t(w):
            return w.reshape(M, 128, CF, 128).transpose(2, 1, 0, 3)
        out = np.stack([t(g), t(u)], axis=2)
        return out.reshape(CF, 128, 2 * D)

    wgu = np.empty((NE, CF, 128, 2 * D), np.float32)
    for e in range(E):
        wgu[e] = gu_slabs(wg[e], wu[e])
    wgu[E] = gu_slabs(swg, swu)
    wgu = np.ascontiguousarray(wgu)

    def d_slabs(w):
        return (w.reshape(CF, 128, M, 128).transpose(2, 1, 0, 3)
                .reshape(M, 128, F))

    wds = np.empty((NE, M, 128, F), np.float32)
    for e in range(E):
        wds[e] = d_slabs(wd[e])
    wds[E] = d_slabs(swd)
    wds = np.ascontiguousarray(wds)

    shared = dict(wq=wslabs(wq), wk=wslabs(wk), wv=wslabs(wv), wo=wslabs(wo),
                  bcols=bcols, rw=rw_s, wgu=wgu, wd=wds)
    in_maps = []
    for r in range(cfg.R):
        m = dict(shared)
        m["xt"] = np.ascontiguousarray(x[r * cfg.T:(r + 1) * cfg.T, :].T)
        in_maps.append(m)
    return in_maps


def kernel(**inputs):
    cfg = CFG()
    nc = build(cfg)
    in_maps = _host_layout(inputs, cfg)
    res = run_bass_kernel_spmd(nc, in_maps, core_ids=list(range(cfg.R)))
    out = np.empty((1, cfg.S, cfg.D), np.float32)
    for r in range(cfg.R):
        out[0, r * cfg.T:(r + 1) * cfg.T, :] = res.results[r]["outT"].T
    return out
